# version 37
# baseline (speedup 1.0000x reference)
import numpy as np
import ml_dtypes
import concourse.bass as bass
import concourse.mybir as mybir
from concourse.bass_utils import run_bass_kernel_spmd

F32 = mybir.dt.float32
BF16 = mybir.dt.bfloat16
AF = mybir.ActivationFunctionType
ALU = mybir.AluOpType

NCORES = 8
D = 1024
G = 8
POS = 1024
B = 2
CH = 512
NCH = POS // CH
KB = 1024

ENG_SEM = {"tensor": "T", "vector": "V", "scalar": "S", "gpsimd": "G", "sync": "Y"}


class Buf:
    __slots__ = ("name", "w", "r")

    def __init__(self, name):
        self.name = name
        self.w = None
        self.r = []


class Plan:
    def __init__(self):
        self.seq = {e: [] for e in ENG_SEM}
        self.cnt = {}
        self.rings = {}

    def ring(self, name, n):
        self.rings[name] = {"sems": [f"{name}_{i}" for i in range(n)], "i": 0,
                            "last": [None] * n}

    def op(self, eng, fn, reads=(), writes=(), deps=(), queue=None, sem=None,
           inc=None):
        dep_list = []
        for b in reads:
            if b.w is not None:
                dep_list.append(b.w)
        for b in writes:
            if b.w is not None:
                dep_list.append(b.w)
            dep_list.extend(b.r)
        dep_list.extend(t for t in deps if t is not None)
        slot = None
        if queue is not None:
            rg = self.rings[queue]
            slot = rg["i"]
            rg["i"] = (slot + 1) % len(rg["sems"])
            sem = rg["sems"][slot]
            inc = 16
            if rg["last"][slot] is not None:
                dep_list.append(rg["last"][slot])
            async_tok = True
        elif sem is not None:
            async_tok = True
        else:
            sem = ENG_SEM[eng]
            inc = 1
            async_tok = False
        waits = []
        for (teng, tsem, tval) in dep_list:
            if teng == eng:
                continue
            waits.append((tsem, tval))
        self.cnt[sem] = self.cnt.get(sem, 0) + inc
        tok = (None if async_tok else eng, sem, self.cnt[sem])
        self.seq[eng].append((waits, fn, sem, inc))
        if queue is not None:
            self.rings[queue]["last"][slot] = tok
        for b in reads:
            b.r.append(tok)
        for b in writes:
            b.w = tok
            b.r = []
        return tok

    def emit(self, eng_name, eng, sems, nc=None, rec=None):
        hw = {}
        for oi, (waits, fn, sem, inc) in enumerate(self.seq[eng_name]):
            emitted = []
            for (sname, val) in waits:
                if hw.get(sname, 0) >= val:
                    continue
                eng.wait_ge(sems[sname], val)
                hw[sname] = val
                emitted.append((sname, val))
            n0 = nc.n_instructions() if nc is not None else 0
            inst = fn(eng)
            if inst is not None and sem is not None:
                inst.then_inc(sems[sem], inc)
            if rec is not None:
                rec.append((eng_name, oi, nc.n_instructions() - n0,
                            emitted, sem, inc))


_DEBUG = False
_NO_CC = False


def build_nc(dbg=False):
    nc = bass.Bass("TRN2", target_bir_lowering=False)

    xT = [nc.dram_tensor(f"x{b}T", [D, POS], F32, kind="ExternalInput").ap()
          for b in range(B)]
    xTb = [nc.dram_tensor(f"x{b}Tb", [D, POS], BF16, kind="ExternalInput").ap()
           for b in range(B)]
    wqkT_d = nc.dram_tensor("wqkT", [D, 3 * D], BF16, kind="ExternalInput").ap()
    gamma_in = nc.dram_tensor("gamma", [128, G], F32, kind="ExternalInput").ap()
    wT = {}
    for nm in ("wf", "wi", "wc", "wo", "uf", "ui", "uc", "uo"):
        wT[nm] = nc.dram_tensor(nm + "T", [D, D], BF16, kind="ExternalInput").ap()
    bias_in = {}
    for nm in ("bf", "bi", "bc", "bo", "inc"):
        bias_in[nm] = nc.dram_tensor(nm + "_b", [128, G], F32,
                                     kind="ExternalInput").ap()
    mlt_in = nc.dram_tensor("mask_lt", [128, G], F32, kind="ExternalInput").ap()
    mlc_in = nc.dram_tensor("mask_c", [128, G], F32, kind="ExternalInput").ap()
    outT = nc.dram_tensor("outT", [B, D, POS], F32, kind="ExternalOutput").ap()

    _spk = "ExternalOutput" if dbg else "Internal"
    uv_sp = nc.dram_tensor("uv_sp", [B, 2, NCH, 128, G * CH], BF16,
                           kind=_spk).ap()
    x2_sp = nc.dram_tensor("x2_sp", [B, NCH, 128, G * CH], F32, kind=_spk).ap()
    cc_in = [nc.dram_tensor(f"cc_in{b}", [128, 16], F32) for b in range(B)]
    cc_out = [nc.dram_tensor(f"cc_out{b}", [NCORES, 128, 16], F32)
              for b in range(B)]
    cc_all_in = nc.dram_tensor("cc_all_in", [128, 32], F32)
    cc_all_out = nc.dram_tensor("cc_all_out", [NCORES, 128, 32], F32)
    cc_wrm = nc.dram_tensor("cc_wrm", [128, 16], F32)
    cc_wrm_o = nc.dram_tensor("cc_wrm_o", [NCORES, 128, 16], F32)

    SB_BASE = nc.sbuf_top - nc.sbuf_bytes_remaining
    SB_BASE = (SB_BASE + 127) // 128 * 128
    assert nc.sbuf_bytes_remaining >= 205 * KB, nc.sbuf_bytes_remaining

    def at(name, shape, dtype, off):
        return nc.alloc_sbuf_tensor_at(name, shape, dtype, offset=SB_BASE + off)

    o = 0

    def bump(name, shape, dtype, align=32):
        nonlocal o
        es = mybir.dt.size(dtype)
        o = (o + align - 1) // align * align
        h = at(name, shape, dtype, o)
        n = 1
        for s in shape[1:]:
            n *= s
        o += n * es
        return h

    t_gamma = bump("gammat", [128, G], F32)
    t_b = {nm: bump(nm + "t", [128, G], F32)
           for nm in ("bf", "bi", "bc", "bo", "inc")}
    t_incbf = bump("incbf", [128, G], BF16)
    t_mlt = bump("mltt", [128, G], F32)
    t_mlc = bump("mlct", [128, G], F32)
    t_cy = bump("cyt", [128, 2 * G], F32)
    t_cp = bump("cpt", [128, 2 * G], F32)
    t_cin = bump("cint", [128, 2 * G], F32)
    t_cst = bump("cstt", [128, 2 * G], F32)
    t_tmpA = bump("tmpAt", [128, 2 * G], F32)
    t_tmpY = bump("tmpYt", [128, 2 * G], F32)
    t_vr = {nm: bump("vr" + nm, [128, G], F32) for nm in ("f", "i", "c")}
    t_b0 = {nm: bump("b0" + nm, [128, G], F32) for nm in ("f", "i", "c")}
    t_summ = bump("summt", [128, 32], F32)
    t_gath = bump("gatht", [128, 256], F32)
    t_onescol = bump("onescol", [128, 1], BF16)
    t_onesrow = bump("onesrow", [1, 128], F32)
    t_ones512 = bump("ones512", [128, CH], F32)
    t_ceps = bump("cepst", [1, 1], F32)
    t_cmax = bump("cmaxt", [1, 1], F32)
    r_m = bump("r_m", [1, CH], F32)
    r_v = bump("r_v", [1, CH], F32)
    r_r = bump("r_r", [1, CH], F32)
    r_mr = bump("r_mr", [1, CH], F32)
    t_sg = [bump(f"sg{i}", [128, CH], F32) for i in range(3)]
    t_cbA = bump("cbA", [128, 72], F32)
    t_cbY = bump("cbY", [128, 72], F32)
    t_cbS = bump("cbS", [128, 72], F32)
    assert o <= 20 * KB, f"small region overflow: {o}"

    t_wqk = at("wqk", [128, G * 3 * D], BF16, 20 * KB)
    t_s0 = at("s0", [128, G * D], BF16, 20 * KB)
    t_s1 = at("s1", [128, G * D], BF16, 36 * KB)
    t_sqB = at("sqB", [128, G * CH], BF16, 36 * KB)
    t_s2 = at("s2", [128, G * D], BF16, 52 * KB)
    t_xbf = at("xbf", [128, G * POS], BF16, 68 * KB)
    t_s3 = at("s3", [128, G * D], BF16, 68 * KB)
    t_hbf0 = at("hbf0", [128, G * POS], BF16, 84 * KB)
    t_s4 = at("s4", [128, G * D], BF16, 84 * KB)
    t_vchE2 = at("vchE2", [128, G * CH], BF16, 92 * KB)
    t_hD = at("hD", [128, G * CH], F32, 84 * KB)
    t_hbf1 = at("hbf1", [128, G * POS], BF16, 100 * KB)
    t_s5 = at("s5", [128, G * D], BF16, 100 * KB)
    t_x2r = at("x2r", [128, G * CH], F32, 100 * KB)
    t_sqA = at("sqA", [128, G * CH], BF16, 116 * KB)
    t_xm = at("xmscr", [128, G * (CH // 2)], F32, 116 * KB)
    t_uchE = at("uchE", [128, G * CH], BF16, 116 * KB)
    t_abf = at("abf", [128, G * CH], BF16, 116 * KB)
    t_hbfD = at("hbfD", [128, G * CH], BF16, 116 * KB)
    t_y = at("ych", [128, G * CH], F32, 124 * KB)
    t_c1a = at("c1a", [128, 4 * POS], F32, 124 * KB)
    t_hD = at("hD", [128, G * CH], F32, 124 * KB)
    t_P = at("Pch", [128, G * CH], F32, 140 * KB)
    t_uch = at("uch", [128, G * CH], F32, 140 * KB)
    t_uchb = at("uchb", [128, G * CH], BF16, 140 * KB)
    t_vchb = at("vchb", [128, G * CH], BF16, 148 * KB)
    t_ui2 = at("ui2", [128, G * D], BF16, 140 * KB)
    t_x2r = at("x2r", [128, G * CH], F32, 140 * KB)
    t_q = at("qch", [128, G * CH], F32, 156 * KB)
    t_vch = at("vch", [128, G * CH], F32, 156 * KB)
    t_uc2 = at("uc2", [128, G * D], BF16, 156 * KB)
    t_x2r2 = at("x2r2", [128, G * CH], F32, 156 * KB)
    t_kv = at("kvch", [128, G * CH], F32, 172 * KB)
    t_z1 = at("z1", [128, G * POS], BF16, 172 * KB)
    t_a = at("ach", [128, G * CH], F32, 188 * KB)
    t_ubf = at("ubf", [128, G * CH], BF16, 188 * KB)
    t_vbf = at("vbf", [128, G * CH], BF16, 196 * KB)
    t_z0 = at("z0", [128, G * POS], BF16, 188 * KB)
    t_c1b = at("c1b", [128, 4 * POS], F32, 84 * KB)
    t_hD2 = at("hD2", [128, G * CH], F32, 84 * KB)
    t_c1bf = at("c1bf", [128, G * POS], BF16, 100 * KB)

    pmm = [nc.alloc_psum_tensor(f"pmm{i}", [128, 512], F32) for i in range(4)]
    pstat = [nc.alloc_psum_tensor(f"pstat{i}", [1, 512], F32) for i in range(2)]
    pbc = [nc.alloc_psum_tensor(f"pbc{i}", [128, 512], F32) for i in range(2)]

    sems = {}
    for s in ("T", "V", "S", "G", "Y", "CC"):
        sems[s] = nc.alloc_semaphore(s)
    P = Plan()
    P.ring("SY", 6)
    P.ring("SQ", 4)
    P.ring("GQ", 2)
    for rg in ("SY", "SQ", "GQ"):
        for s in P.rings[rg]["sems"]:
            sems[s] = nc.alloc_semaphore(s)

    bufs = {}

    def B_(name):
        if name not in bufs:
            bufs[name] = Buf(name)
        return bufs[name]

    def dma(eng, queue, dst_ap, src_ap, reads=(), writes=(), deps=()):
        return P.op(eng, lambda e: e.dma_start(dst_ap, src_ap),
                    reads=reads, writes=writes, deps=deps, queue=queue)

    def vguard(bufs_to_stamp):
        P.op("vector", lambda e: e.memset(t_sg[2][:], 0.0),
             reads=[], writes=[B_("sg2")])
        P.op("vector", lambda e: e.memset(t_sg[2][:], 0.0),
             reads=[], writes=[B_("sg2")] + list(bufs_to_stamp))

    dbg_names = []

    def dbg_dump(name, src_ap, reads):
        if not dbg:
            return
        shp = list(src_ap.shape)
        dt = src_ap.dtype
        d = nc.dram_tensor("dbg_" + name, shp, dt, kind="ExternalOutput").ap()
        dbg_names.append("dbg_" + name)
        dma("sync", "SY", d, src_ap, reads=list(reads))

    def g3(t, c0=0, ncc=CH, width=CH, ng=G):
        ap = t.ap()
        row = ap.ap[0][0]
        return bass.AP(t, c0, [[row, 128], [width, ng], [1, ncc]])

    def bc3(pt, c0=0, ncc=CH, ng=G):
        ap = pt.ap()
        row = ap.ap[0][0]
        return bass.AP(pt, c0, [[row, 128], [0, ng], [1, ncc]])

    b_pstat = [B_("pstat0"), B_("pstat1")]
    b_pbc = [B_("pbc0"), B_("pbc1")]
    b_pmm = [B_(f"pmm{i}") for i in range(4)]
    b_rows = B_("rows")
    b_consts = B_("consts")
    b_params = B_("params")

    b_xbf = B_("xbf")
    b_wqk = B_("wqk")
    b_wqk3 = [B_("wqk3a"), B_("wqk3b"), B_("wqk3c")]
    dma("scalar", "SQ",
        t_xbf.ap().rearrange("p (g c) -> p g c", g=G),
        xTb[0].rearrange("(g p) c -> p g c", p=128),
        writes=[b_xbf])
    dma("scalar", "SQ",
        t_wqk.ap().rearrange("p (g o) -> p g o", g=G),
        wqkT_d.rearrange("(g p) o -> p g o", p=128),
        writes=[b_wqk] + b_wqk3)
    dma("sync", "SY", t_gamma[:], gamma_in, writes=[b_params])
    for nm in ("bf", "bi", "bc", "bo", "inc"):
        dma("sync", "SY", t_b[nm][:], bias_in[nm], writes=[B_("b_" + nm)])
    dma("sync", "SY", t_mlt[:], mlt_in, writes=[B_("masks")])
    dma("sync", "SY", t_mlc[:], mlc_in, writes=[B_("masks")])

    def consts(e):
        e.memset(t_onescol[:], 1.0 / 1024.0)
        e.memset(t_onesrow[:], 1.0)
        e.memset(t_ones512[:], 1.0)
        e.memset(t_ceps[:], 1e-5)
        e.memset(t_cmax[:], 9.5e-28)
        e.memset(t_cy[:], 0.0)
        e.memset(t_cp[:], 1.0)
        return e.memset(t_cst[:], 0.0)
    P.op("vector", consts, writes=[b_consts, B_("cy"), B_("cp"), B_("cst")])

    P.op("vector", lambda e: e.tensor_copy(t_incbf[:], t_b["inc"][:]),
         reads=[B_("b_inc")], writes=[B_("incbf")])


    def rows_block(eps_ap, floor, want_mr):
        er = "vector"
        P.op(er, lambda e: e.tensor_copy(r_m[:], pstat[0][0:1, :]),
             reads=[b_pstat[0]], writes=[b_rows])

        def f_var(e):
            e.tensor_tensor(r_v[:], r_m[:], r_m[:], ALU.mult)
            return e.tensor_tensor(r_v[:], pstat[1][0:1, :], r_v[:], ALU.subtract)
        P.op(er, f_var, reads=[b_pstat[1]], writes=[b_rows])
        if floor:
            P.op("vector",
                 lambda e: e.tensor_scalar(r_v[:], r_v[:], t_cmax[0:1, :], None,
                                           ALU.max),
                 reads=[b_consts], writes=[b_rows])

        def f_ln(e):
            if eps_ap is None:
                e.activation(r_r[:], r_v[:], AF.Ln)
            else:
                e.activation(r_r[:], r_v[:], AF.Ln, bias=eps_ap)
            return e.activation(r_r[:], r_r[:], AF.Exp, scale=-0.5)
        P.op("scalar", f_ln, reads=[b_rows, b_consts], writes=[b_rows])
        if want_mr:
            P.op(er, lambda e: e.tensor_tensor(r_mr[:], r_m[:], r_r[:],
                                               ALU.mult),
                 reads=[b_rows], writes=[b_rows])

    def bcast(row_tile, which):
        def f(e):
            return e.matmul(pbc[which][:, :], t_onesrow[:], row_tile[:],
                            start=True, stop=True)
        return f

    def stat_sum(src_tile, src_buf):
        def fsum(e):
            ins = None
            for g in range(G):
                ins = e.matmul(pstat[0][0:1, :], t_onescol[:],
                               src_tile[:, g * CH:(g + 1) * CH],
                               start=(g == 0), stop=(g == G - 1))
            return ins
        P.op("tensor", fsum, reads=[src_buf, b_consts], writes=[b_pstat[0]])

    def stat_sq(sq_tile, sq_buf):
        def fsq(e):
            ins = None
            for g in range(G):
                ins = e.matmul(pstat[1][0:1, :], t_onescol[:],
                               sq_tile[:, g * CH:(g + 1) * CH],
                               start=(g == 0), stop=(g == G - 1))
            return ins
        P.op("tensor", fsq, reads=[sq_buf, b_consts], writes=[b_pstat[1]])

    b_abfh = [B_("abfh0"), B_("abfh1")]

    def stat_half(bank, tile, b_half, h, start):
        lo = h * 4 * CH

        def f(e, lo=lo, bank=bank, start=start):
            ins = None
            for g in range(4):
                ins = e.matmul(pstat[bank][0:1, :], t_onescol[:],
                               tile[:, lo + g * CH: lo + (g + 1) * CH],
                               start=(start and g == 0),
                               stop=(not start and g == 3))
            return ins
        P.op("tensor", f, reads=[b_half, b_consts], writes=[b_pstat[bank]])

    def ln_stats(src_tile, b_src, abf_tile, alias=(), eng="scalar"):
        H = 4 * CH

        def cast_f(e, h):
            if eng == "scalar":
                return e.activation(abf_tile[:, h * H:(h + 1) * H],
                                    src_tile[:, h * H:(h + 1) * H], AF.Copy)
            return e.tensor_copy(abf_tile[:, h * H:(h + 1) * H],
                                 src_tile[:, h * H:(h + 1) * H])

        def sq_f(e, h):
            if eng == "scalar":
                return e.activation(abf_tile[:, h * H:(h + 1) * H],
                                    src_tile[:, h * H:(h + 1) * H], AF.Square)
            return e.tensor_tensor(abf_tile[:, h * H:(h + 1) * H],
                                   src_tile[:, h * H:(h + 1) * H],
                                   src_tile[:, h * H:(h + 1) * H], ALU.mult)
        for h in range(2):
            P.op(eng, lambda e, h=h: cast_f(e, h),
                 reads=[b_src], writes=[b_abfh[h]] + list(alias))
        for h in range(2):
            stat_half(0, abf_tile, b_abfh[h], h, start=(h == 0))
        for h in range(2):
            P.op(eng, lambda e, h=h: sq_f(e, h),
                 reads=[b_src], writes=[b_abfh[h]])
        for h in range(2):
            stat_half(1, abf_tile, b_abfh[h], h, start=(h == 0))

    def stat_pair(src_tile, src_buf, src_off, src_w, sq_tile, sq_buf):
        def fsum(e):
            ins = None
            for g in range(G):
                ins = e.matmul(pstat[0][0:1, :], t_onescol[:],
                               src_tile[:, g * src_w + src_off:
                                        g * src_w + src_off + CH],
                               start=(g == 0), stop=(g == G - 1))
            return ins
        P.op("tensor", fsum, reads=[src_buf, b_consts], writes=[b_pstat[0]])

        def fsq(e):
            ins = None
            for g in range(G):
                ins = e.matmul(pstat[1][0:1, :], t_onescol[:],
                               sq_tile[:, g * CH:(g + 1) * CH],
                               start=(g == 0), stop=(g == G - 1))
            return ins
        P.op("tensor", fsq, reads=[sq_buf, b_consts], writes=[b_pstat[1]])

    def rank1_mm(nm, off):
        def f(e, nm=nm, off=off):
            ins = None
            t = wslot[nm]
            for m in range(G):
                for g in range(G):
                    ins = e.matmul(pmm[3][:, off + m:off + m + 1],
                                   t[:, g * D + m * 128: g * D + (m + 1) * 128],
                                   t_incbf[:, g:g + 1],
                                   start=(g == 0), stop=(g == G - 1))
            return ins
        P.op("tensor", f, reads=[b_slot[slot_of[nm]], B_("incbf")],
             writes=[b_pmm[3]])

    def rank1_fin():
        for off, gate, bname in ((0, "f", "bf"), (G, "i", "bi"),
                                 (2 * G, "c", "bc")):
            P.op("vector",
                 lambda e, off=off, gate=gate, bname=bname: e.tensor_tensor(
                     t_b0[gate][:], pmm[3][:, off:off + G], t_b[bname][:],
                     ALU.add),
                 reads=[b_pmm[3], B_("b_" + bname)], writes=[B_("b0" + gate)])
        vguard([B_("b0f"), B_("b0i"), B_("b0c")])


    b_sqA = B_("sqA")
    b_hbf = [B_("hbf0"), B_("hbf1")]
    b_q = [B_(f"qch{g}") for g in range(G)]
    b_kv = [B_(f"kvch{g}") for g in range(G)]
    b_a = [B_(f"ach{g}") for g in range(G)]
    b_y, b_P = B_("ych"), B_("Pch")
    b_ubf, b_vbf = B_("ubf"), B_("vbf")
    b_cy, b_cp = B_("cy"), B_("cp")
    hbf_t = [t_hbf0, t_hbf1]
    uv_tok = {}

    def ln1_block(b):
        for c in range(NCH):
            P.op("scalar",
                 lambda e, c=c: e.activation(
                     g3(t_sqA), g3(t_xbf, c0=c * CH, width=POS), AF.Square),
                 reads=[b_xbf], writes=[b_sqA])
            stat_pair(t_xbf, b_xbf, c * CH, POS, t_sqA, b_sqA)
            rows_block(None, True, False)
            P.op("tensor", bcast(r_m, 0), reads=[b_rows, b_consts],
                 writes=[b_pbc[0]])
            P.op("tensor", bcast(r_r, 1), reads=[b_rows, b_consts],
                 writes=[b_pbc[1]])
            for h2 in range(2):
                c0 = c * CH + h2 * (CH // 2)
                P.op("vector",
                     lambda e, c0=c0: e.tensor_tensor(
                         g3(t_xm, ncc=CH // 2, width=CH // 2),
                         g3(t_xbf, c0=c0, ncc=CH // 2, width=POS),
                         bc3(pbc[0], c0=(c0 % CH), ncc=CH // 2),
                         ALU.subtract),
                     reads=[b_xbf, b_pbc[0]], writes=[b_sqA])
                for g in range(G):
                    P.op("vector",
                         lambda e, b=b, c0=c0, g=g: e.scalar_tensor_tensor(
                             hbf_t[b][:, g * POS + c0:
                                      g * POS + c0 + CH // 2],
                             t_xm[:, g * (CH // 2):(g + 1) * (CH // 2)],
                             t_gamma[:, g:g + 1],
                             pbc[1][:, (c0 % CH):(c0 % CH) + CH // 2],
                             ALU.mult, ALU.mult),
                         reads=[b_sqA, b_pbc[1], b_params],
                         writes=[b_hbf[b]])
        dbg_dump(f"hbf{b}", hbf_t[b][:], [b_hbf[b]])

    def qkva_chunk(b, c):
        order = []
        for g in range(G):
            order.append((8 + g, "kv"))
            order.append((16 + g, "a"))
        for g in range(G):
            order.append((g, "q"))
        for k, (m, kind) in enumerate(order):
            bank = k % 4

            def fmm2(e, b=b, c=c, m=m, bank=bank):
                ins = None
                for g in range(G):
                    ins = e.matmul(
                        pmm[bank][:, 0:CH],
                        t_wqk[:, g * 3 * D + m * 128:
                              g * 3 * D + (m + 1) * 128],
                        hbf_t[b][:, g * POS + c * CH:
                                 g * POS + (c + 1) * CH],
                        start=(g == 0), stop=(g == G - 1))
                return ins
            P.op("tensor", fmm2, reads=[b_wqk, b_hbf[b]] + b_wqk3,
                 writes=[b_pmm[bank]])
            if kind == "q":
                P.op("scalar",
                     lambda e, m=m, bank=bank: e.copy(
                         t_q[:, m * CH:(m + 1) * CH], pmm[bank][:, 0:CH]),
                     reads=[b_pmm[bank]], writes=[b_q[m]])
            elif kind == "kv":
                P.op("scalar",
                     lambda e, m=m, bank=bank: e.copy(
                         t_kv[:, (m - 8) * CH:(m - 7) * CH],
                         pmm[bank][:, 0:CH]),
                     reads=[b_pmm[bank]], writes=[b_kv[m - 8]])
            else:
                P.op("scalar",
                     lambda e, m=m, bank=bank: e.activation(
                         t_a[:, (m - 16) * CH:(m - 15) * CH],
                         pmm[bank][:, 0:CH], AF.Sigmoid),
                     reads=[b_pmm[bank]],
                     writes=[b_a[m - 16],
                             b_ubf if (m - 16) < 4 else b_vbf])
        if b == 0 and dbg:
            dbg_dump(f"q0{c}", t_q[:], list(b_q))
            dbg_dump(f"kv0{c}", t_kv[:], list(b_kv))
            dbg_dump(f"a0{c}", t_a[:], list(b_a))
        for g in range(G):
            sl = slice(g * CH, (g + 1) * CH)
            cyc = t_cy[:, b * G + g: b * G + g + 1]
            P.op("vector",
                 lambda e, sl=sl, cyc=cyc: e.tensor_tensor_scan(
                     t_y[:, sl], t_a[:, sl], t_kv[:, sl], cyc,
                     ALU.mult, ALU.add),
                 reads=[b_a[g], b_kv[g], b_cy], writes=[b_y])
        for g in range(G):
            sl = slice(g * CH, (g + 1) * CH)
            cpc = t_cp[:, b * G + g: b * G + g + 1]
            P.op("vector",
                 lambda e, sl=sl, cpc=cpc: e.tensor_tensor_scan(
                     t_P[:, sl], t_a[:, sl], t_ones512[:], cpc,
                     ALU.mult, ALU.mult),
                 reads=[b_a[g], b_consts, b_cp], writes=[b_P])
        for g in range(G):
            sl = slice(g * CH, (g + 1) * CH)
            cyc = t_cy[:, b * G + g: b * G + g + 1]
            P.op("vector",
                 lambda e, sl=sl, cyc=cyc: e.tensor_copy(
                     cyc, t_y[:, sl.stop - 1: sl.stop]),
                 reads=[b_y], writes=[b_cy])
        for g in range(G):
            sl = slice(g * CH, (g + 1) * CH)
            cpc = t_cp[:, b * G + g: b * G + g + 1]
            P.op("vector",
                 lambda e, sl=sl, cpc=cpc: e.tensor_copy(
                     cpc, t_P[:, sl.stop - 1: sl.stop]),
                 reads=[b_P], writes=[b_cp])
        peng = "gpsimd" if (b, c) == (1, 1) else "vector"
        for g in range(G):
            sl = slice(g * CH, (g + 1) * CH)
            P.op(peng,
                 lambda e, sl=sl: e.tensor_tensor(
                     t_ubf[:, sl], t_y[:, sl], t_q[:, sl], ALU.mult),
                 reads=[b_y, b_q[g]],
                 writes=[b_ubf] + ([b_a[g2] for g2 in range(4)]
                                   if g == 0 else []))
        for g in range(G):
            sl = slice(g * CH, (g + 1) * CH)
            P.op(peng,
                 lambda e, sl=sl: e.tensor_tensor(
                     t_vbf[:, sl], t_P[:, sl], t_q[:, sl], ALU.mult),
                 reads=[b_P, b_q[g]],
                 writes=[b_vbf] + ([b_a[g2] for g2 in range(4, G)]
                                   if g == 0 else []))
        if b == 0 and dbg:
            dbg_dump(f"y0{c}", t_y[:], [b_y])
            dbg_dump(f"P0{c}", t_P[:], [b_P])
            dbg_dump(f"cyA{c}", t_cy[:], [b_cy])
        uv_tok[(b, 0, c)] = dma("sync", "SY", uv_sp[b, 0, c], t_ubf[:],
                                reads=[b_ubf])
        uv_tok[(b, 1, c)] = dma("sync", "SY", uv_sp[b, 1, c], t_vbf[:],
                                reads=[b_vbf])

    b_summ = [B_("summ0"), B_("summ1")]
    b_gath = [B_("gath0"), B_("gath1")]
    b_cin = [B_("cin0"), B_("cin1")]
    tok_cc = [None, None]

    def x_trigger(b):
        P.op("vector",
             lambda e, b=b: e.tensor_copy(
                 t_summ[:, b * 16: b * 16 + 8], t_cy[:, b * G:(b + 1) * G]),
             reads=[b_cy], writes=[b_summ[b]])
        P.op("vector",
             lambda e, b=b: e.tensor_copy(
                 t_summ[:, b * 16 + 8: b * 16 + 16],
                 t_cp[:, b * G:(b + 1) * G]),
             reads=[b_cp], writes=[b_summ[b]])
        vguard([b_summ[b]])
        tok_sm = dma("gpsimd", "GQ", cc_in[b].ap(),
                     t_summ[:, b * 16:b * 16 + 16], reads=[b_summ[b]])
        tok = P.op("gpsimd",
                   lambda e, b=b: e.collective_compute(
                       "AllGather", ALU.bypass,
                       replica_groups=[list(range(NCORES))],
                       ins=[cc_in[b].ap()], outs=[cc_out[b].ap()]),
                   deps=[tok_sm], sem="CC", inc=1)
        tok_cc[b] = tok
        dma("gpsimd", "GQ",
            t_gath[:, b * 128:(b + 1) * 128].rearrange("p (r c) -> p r c", r=8),
            cc_out[b].ap().rearrange("r p c -> p r c"),
            writes=[b_gath[b]], deps=[tok])

    def st3(t, off):
        ap = t.ap()
        row = ap.ap[0][0]
        return bass.AP(t, off, [[row, 128], [9, 8], [1, 1]])

    def g8(b, off):
        ap = t_gath.ap()
        row = ap.ap[0][0]
        return bass.AP(t_gath, b * 128 + off, [[row, 128], [1, 8], [1, 1]])

    def x_combine(b):
        P.op("vector", lambda e: e.memset(t_cbA[:], 0.0), writes=[B_("cbA")])
        P.op("vector", lambda e: e.memset(t_cbY[:], 0.0), writes=[B_("cbY")])
        for r in range(NCORES):
            P.op("vector",
                 lambda e, r=r, b=b: e.tensor_scalar(
                     st3(t_cbA, 1 + r), g8(b, r * 16 + 8),
                     t_mlt[:, r:r + 1], t_mlc[:, r:r + 1], ALU.mult, ALU.add),
                 reads=[b_gath[b], B_("masks")], writes=[B_("cbA")])
        for r in range(NCORES):
            P.op("vector",
                 lambda e, r=r, b=b: e.tensor_scalar(
                     st3(t_cbY, 1 + r), g8(b, r * 16),
                     t_mlt[:, r:r + 1], None, ALU.mult),
                 reads=[b_gath[b], B_("masks")], writes=[B_("cbY")])
        P.op("vector", lambda e: e.memset(t_sg[2][:, 0:512], 0.0),
             writes=[B_("sg2")])
        P.op("vector",
             lambda e: e.tensor_tensor_scan(t_cbS[:], t_cbA[:], t_cbY[:],
                                            0.0, ALU.mult, ALU.add),
             reads=[B_("cbA"), B_("cbY")], writes=[B_("cbS")])
        P.op("vector", lambda e: e.memset(t_sg[2][:, 0:512], 0.0),
             writes=[B_("sg2")])

        def stfin():
            ap = t_cbS.ap()
            row = ap.ap[0][0]
            return bass.AP(t_cbS, 8, [[row, 128], [9, 8]])
        P.op("vector",
             lambda e, b=b: e.tensor_copy(t_cin[:, b * G:(b + 1) * G], stfin()),
             reads=[B_("cbS")], writes=[b_cin[b]])
        vguard([b_cin[b]])
        if b == 1:
            dbg_dump("cy", t_cy[:], [b_cy])
            dbg_dump("cp", t_cp[:], [b_cp])
            dbg_dump("gath", t_gath[:], [b_gath[0], b_gath[1]])
            dbg_dump("cin", t_cin[:], [b_cin[0], b_cin[1]])

    wslot = {"wf": t_s0, "wi": t_s1, "wc": t_s2, "uf": t_s3, "ui": t_s4,
             "uc": t_s5, "wo": t_s0, "uo": t_s3}
    slot_of = {"wf": "s0", "wi": "s1", "wc": "s2", "uf": "s3", "ui": "s4",
               "uc": "s5", "wo": "s0", "uo": "s3"}
    b_slot = {s: B_("slot_" + s) for s in ("s0", "s1", "s2", "s3", "s4", "s5")}
    slot_alias = {"s0": [b_wqk3[0]],
                  "s1": [b_wqk3[1], B_("sqB")],
                  "s2": [b_wqk3[2]],
                  "s3": [b_xbf], "s4": [b_hbf[0]], "s5": [b_hbf[1]]}

    def load_w(nm):
        t = wslot[nm]
        s = slot_of[nm]
        return dma("scalar", "SQ",
                   t.ap().rearrange("p (g d) -> p g d", g=G),
                   wT[nm].rearrange("(g p) d -> p g d", p=128),
                   writes=[b_slot[s]] + slot_alias[s])

    ln1_block(0)
    dma("scalar", "SQ",
        t_xbf.ap().rearrange("p (g c) -> p g c", g=G),
        xTb[1].rearrange("(g p) c -> p g c", p=128),
        writes=[b_xbf])
    qkva_chunk(0, 0)
    ln1_block(1)
    load_w("uf")
    qkva_chunk(0, 1)
    load_w("ui")
    x_trigger(0)
    dma("sync", "SY", t_uchE[:], uv_sp[0, 0, 0],
        writes=[B_("uchE"), b_sqA], deps=[uv_tok[(0, 0, 0)]])
    qkva_chunk(1, 0)
    x_combine(0)
    qkva_chunk(1, 1)
    rank1_mm("uf", 0)
    rank1_mm("ui", G)
    dma("gpsimd", "GQ", t_vchE2[:], uv_sp[0, 1, 0],
        writes=[B_("vchE2"), b_slot["s4"]], deps=[uv_tok[(0, 1, 0)]])
    x_trigger(1)
    for _nm in ("uc", "wf", "wi", "wc"):
        load_w(_nm)

    b_uch, b_vch = B_("uch"), B_("vch")
    b_abf = B_("abf")
    b_z = [[B_("z00"), B_("z01")], [B_("z10"), B_("z11")]]
    z_t = [t_z0, t_z1]
    b_c1a, b_c1b, b_c1bf = B_("c1a"), B_("c1b"), B_("c1bf")
    b_ui2, b_uc2 = B_("ui2"), B_("uc2")
    x2_tok = {}

    def c1_ap(m, c):
        t = t_c1a if m < 4 else t_c1b
        mm = m if m < 4 else m - 4
        return t[:, mm * POS + c * CH: mm * POS + (c + 1) * CH]

    def c1_buf(m):
        return b_c1a if m < 4 else b_c1b

    def uv_restream(b, c, early=False):
        if early:
            tu, tv = t_uchE, t_vchE
            bu, bv = B_("uchE"), B_("vchE")
            al = [b_slot["s1"]]
        else:
            tu, tv = t_uchb, t_vchb
            bu, bv = B_("uchb"), B_("vchb")
            al = [b_uch] + ([b_P] if (b, c) == (0, 1) else [])
        dma("sync", "SY", tu[:], uv_sp[b, 0, c],
            writes=[bu] + al, deps=[uv_tok[(b, 0, c)]])
        dma("sync", "SY", tv[:], uv_sp[b, 1, c],
            writes=[bv] + al, deps=[uv_tok[(b, 1, c)]])
        return tu, tv, bu, bv

    def B_chunk_p1(b, c, rs):
        tu, tv, bu, bv = rs
        for g in range(G):
            sl = slice(g * CH, (g + 1) * CH)
            P.op("vector",
                 lambda e, sl=sl, b=b, tu=tu, tv=tv, g=g:
                 e.scalar_tensor_tensor(
                     t_vch[:, sl], tv[:, sl],
                     t_cin[:, b * G + g: b * G + g + 1],
                     tu[:, sl], ALU.mult, ALU.add),
                 reads=[bu, bv, b_cin[b]],
                 writes=[b_vch] + (list(b_q) if g == 0 else []))
        ln_stats(t_vch, b_vch, t_abf, alias=[b_abf, b_sqA, B_("uchE")])
        rows_block(t_ceps[0:1, :], False, True)
        P.op("tensor", bcast(r_r, 0), reads=[b_rows, b_consts],
             writes=[b_pbc[0]])
        P.op("tensor", bcast(r_mr, 1), reads=[b_rows, b_consts],
             writes=[b_pbc[1]])
        P.op("vector",
             lambda e: e.tensor_tensor(g3(t_vch), g3(t_vch), bc3(pbc[0]),
                                       ALU.mult),
             reads=[b_pbc[0]], writes=[b_vch])
        P.op("vector",
             lambda e: e.tensor_tensor(g3(t_vch), g3(t_vch), bc3(pbc[1]),
                                       ALU.subtract),
             reads=[b_pbc[1]], writes=[b_vch])
        xal = [b_P] if (b, c) == (0, 0) else []
        dma("scalar", "SQ",
            t_uch.ap().rearrange("p (g c) -> p g c", g=G),
            xT[b].rearrange("(g p) c -> p g c", p=128)[:, :, c * CH:(c + 1) * CH],
            writes=[b_uch, B_("uchb"), B_("vchb")] + xal)
        P.op("vector",
             lambda e: e.tensor_tensor(t_vch[:], t_vch[:], t_uch[:], ALU.add),
             reads=[b_uch], writes=[b_vch])
        x2_tok[(b, c)] = dma("scalar", "SQ", x2_sp[b, c], t_vch[:],
                             reads=[b_vch])

    def B_chunk_p2(b, c):
        ln_stats(t_vch, b_vch, t_abf, alias=[b_abf])
        rows_block(t_ceps[0:1, :], False, False)
        P.op("tensor", bcast(r_m, 0), reads=[b_rows, b_consts],
             writes=[b_pbc[0]])
        P.op("tensor", bcast(r_r, 1), reads=[b_rows, b_consts],
             writes=[b_pbc[1]])
        P.op("vector",
             lambda e: e.tensor_tensor(g3(t_uch), g3(t_vch), bc3(pbc[0]),
                                       ALU.subtract),
             reads=[b_vch, b_pbc[0]], writes=[b_uch])
        extra_w = (list(b_a) + [b_ubf, b_vbf]) if b == 0 else list(b_kv)
        P.op("vector",
             lambda e, b=b, c=c: e.tensor_tensor(
                 g3(z_t[b], c0=c * CH, width=POS), g3(t_uch), bc3(pbc[1]),
                 ALU.mult),
             reads=[b_uch, b_pbc[1]], writes=[b_z[b][c]] + extra_w)

    def gate_mms(wt, m, acts_tile, c, bank, extra=None):
        def f(e, wt=wt, m=m, c=c, bank=bank, extra=extra):
            ins = None
            n = G if extra is None else 2 * G
            k = 0
            for g in range(G):
                ins = e.matmul(pmm[bank][:, 0:CH],
                               wt[:, g * D + m * 128: g * D + (m + 1) * 128],
                               acts_tile[:, g * POS + c * CH:
                                         g * POS + (c + 1) * CH],
                               start=(k == 0), stop=(k == n - 1))
                k += 1
            if extra is not None:
                wt2, acts2 = extra
                for g in range(G):
                    ins = e.matmul(pmm[bank][:, 0:CH],
                                   wt2[:, g * D + m * 128: g * D + (m + 1) * 128],
                                   acts2[:, g * POS + c * CH:
                                         g * POS + (c + 1) * CH],
                                   start=False, stop=(k == n - 1))
                    k += 1
            return ins
        return f

    def step0_f(c, ms):
        for m in ms:
            bank = (c * G + m) % 4
            P.op("tensor", gate_mms(t_s0, m, t_z0, c, bank),
                 reads=[b_slot["s0"], b_z[0][c], B_("b0f")],
                 writes=[b_pmm[bank]])
            P.op("scalar",
                 lambda e, m=m, bank=bank: e.activation(
                     t_sg[0][:], pmm[bank][:, 0:CH], AF.Sigmoid,
                     bias=t_b0["f"][:, m:m + 1]),
                 reads=[b_pmm[bank], B_("b0f")], writes=[B_("sg0")])
            al = [b_y] if m < 4 else [b_slot["s4"], B_("vchE2")]
            P.op("vector",
                 lambda e, m=m, c=c: e.tensor_scalar_mul(
                     c1_ap(m, c), t_sg[0][:], t_b["inc"][:, m:m + 1]),
                 reads=[B_("sg0"), B_("b_inc")], writes=[c1_buf(m)] + al)

    def step0_ig(c, ms):
        for m in ms:
            bank_i = (c * G + m) % 2
            bank_g = 2 + (c * G + m) % 2
            P.op("tensor", gate_mms(t_s1, m, t_z0, c, bank_i),
                 reads=[b_slot["s1"], b_z[0][c], B_("b0i")],
                 writes=[b_pmm[bank_i]])
            P.op("tensor", gate_mms(t_s2, m, t_z0, c, bank_g),
                 reads=[b_slot["s2"], b_z[0][c], B_("b0c")],
                 writes=[b_pmm[bank_g]])
            P.op("scalar",
                 lambda e, m=m, bank_i=bank_i: e.activation(
                     t_sg[1][:], pmm[bank_i][:, 0:CH], AF.Sigmoid,
                     bias=t_b0["i"][:, m:m + 1]),
                 reads=[b_pmm[bank_i], B_("b0i")], writes=[B_("sg1")])
            P.op("scalar",
                 lambda e, m=m, bank_g=bank_g: e.activation(
                     t_sg[2][:], pmm[bank_g][:, 0:CH], AF.Tanh,
                     bias=t_b0["c"][:, m:m + 1]),
                 reads=[b_pmm[bank_g], B_("b0c")], writes=[B_("sg2")])
            P.op("vector",
                 lambda e: e.tensor_tensor(t_sg[1][:], t_sg[1][:], t_sg[2][:],
                                           ALU.mult),
                 reads=[B_("sg2")], writes=[B_("sg1")])
            P.op("vector",
                 lambda e, m=m, c=c: e.tensor_tensor(
                     c1_ap(m, c), c1_ap(m, c), t_sg[1][:], ALU.add),
                 reads=[B_("sg1")], writes=[c1_buf(m)])

    b_c1bfc = [B_("c1bfc0"), B_("c1bfc1")]

    def c1bf_cast(c):
        al = [b_slot["s5"]] if c == 0 else []
        row_d = G * POS
        for half, (t_src, bsrc) in enumerate(((t_c1a, b_c1a),
                                              (t_c1b, b_c1b))):
            row_s = 4 * POS
            P.op("vector",
                 lambda e, half=half, t_src=t_src, c=c: e.tensor_copy(
                     bass.AP(t_c1bf, half * 4 * POS + c * CH,
                             [[row_d, 128], [POS, 4], [1, CH]]),
                     bass.AP(t_src, c * CH, [[row_s, 128], [POS, 4], [1, CH]])),
                 reads=[bsrc], writes=[b_c1bfc[c], b_c1bf] + al)

    def step1_f(c, ms):
        for m in ms:
            bank = (c * G + m) % 4
            P.op("tensor", gate_mms(t_s0, m, t_z1, c, bank,
                                    extra=(t_s3, t_c1bf)),
                 reads=[b_slot["s0"], b_slot["s3"], b_z[1][c], b_c1bfc[c]],
                 writes=[b_pmm[bank]])
            P.op("scalar",
                 lambda e, m=m, bank=bank: e.activation(
                     t_sg[0][:], pmm[bank][:, 0:CH], AF.Sigmoid,
                     bias=t_b["bf"][:, m:m + 1]),
                 reads=[b_pmm[bank], B_("b_bf")], writes=[B_("sg0")])
            P.op("vector",
                 lambda e, m=m, c=c: e.tensor_tensor(
                     c1_ap(m, c), c1_ap(m, c), t_sg[0][:], ALU.mult),
                 reads=[B_("sg0")], writes=[c1_buf(m)])

    def step1_ig(c, ms):
        for m in ms:
            bank_i = (c * G + m) % 2
            bank_g = 2 + (c * G + m) % 2
            P.op("tensor", gate_mms(t_s1, m, t_z1, c, bank_i,
                                    extra=(t_ui2, t_c1bf)),
                 reads=[b_slot["s1"], b_ui2, b_z[1][c], b_c1bfc[c]],
                 writes=[b_pmm[bank_i]])
            P.op("tensor", gate_mms(t_s2, m, t_z1, c, bank_g,
                                    extra=(t_uc2, t_c1bf)),
                 reads=[b_slot["s2"], b_uc2, b_z[1][c], b_c1bfc[c]],
                 writes=[b_pmm[bank_g]])
            P.op("scalar",
                 lambda e, m=m, bank_i=bank_i: e.activation(
                     t_sg[1][:], pmm[bank_i][:, 0:CH], AF.Sigmoid,
                     bias=t_b["bi"][:, m:m + 1]),
                 reads=[b_pmm[bank_i], B_("b_bi")], writes=[B_("sg1")])
            P.op("scalar",
                 lambda e, m=m, bank_g=bank_g: e.activation(
                     t_sg[2][:], pmm[bank_g][:, 0:CH], AF.Tanh,
                     bias=t_b["bc"][:, m:m + 1]),
                 reads=[b_pmm[bank_g], B_("b_bc")], writes=[B_("sg2")])
            P.op("vector",
                 lambda e: e.tensor_tensor(t_sg[1][:], t_sg[1][:], t_sg[2][:],
                                           ALU.mult),
                 reads=[B_("sg2")], writes=[B_("sg1")])
            P.op("vector",
                 lambda e, m=m, c=c: e.tensor_tensor(
                     c1_ap(m, c), c1_ap(m, c), t_sg[1][:], ALU.add),
                 reads=[B_("sg1")], writes=[c1_buf(m)])


    B_chunk_p1(0, 0, (t_uchE, t_vchE2, B_("uchE"), B_("vchE2")))
    rank1_mm("uc", 2 * G)
    rank1_fin()
    dbg_dump("b0f", t_b0["f"][:], [B_("b0f")])
    B_chunk_p2(0, 0)
    x_combine(1)
    rs01 = uv_restream(0, 1)
    step0_f(0, range(0, 5))
    B_chunk_p1(0, 1, rs01)
    step0_f(0, range(5, 8))
    step0_ig(0, range(0, 3))
    B_chunk_p2(0, 1)
    rs10 = uv_restream(1, 0)
    step0_ig(0, range(3, 5))
    B_chunk_p1(1, 0, rs10)
    step0_ig(0, range(5, 8))
    step0_f(1, range(0, 2))
    B_chunk_p2(1, 0)
    rs11 = uv_restream(1, 1)
    step0_f(1, range(2, 8))
    step0_ig(1, range(0, 1))
    B_chunk_p1(1, 1, rs11)
    step0_ig(1, range(1, 5))
    B_chunk_p2(1, 1)
    step0_ig(1, range(5, 8))
    c1bf_cast(0)
    step1_f(0, range(0, 4))
    dbg_dump("z0", t_z0[:], b_z[0])
    dbg_dump("z1", t_z1[:], b_z[1])

    dbg_dump("c1", t_c1bf[:], [b_c1bf])
    dma("scalar", "SQ",
        t_ui2.ap().rearrange("p (g d) -> p g d", g=G),
        wT["ui"].rearrange("(g p) d -> p g d", p=128),
        writes=[b_ui2, b_uch])
    dma("scalar", "SQ",
        t_uc2.ap().rearrange("p (g d) -> p g d", g=G),
        wT["uc"].rearrange("(g p) d -> p g d", p=128),
        writes=[b_uc2, b_vch] + list(b_q))

    step1_f(0, range(4, G))
    c1bf_cast(1)
    step1_f(1, range(G))
    load_w("wo")
    load_w("uo")
    step1_ig(0, range(G))
    step1_ig(1, range(G))

    P.op("vector", lambda e: e.tensor_copy(t_c1bf[:, 0:4 * POS], t_c1a[:]),
         reads=[b_c1a], writes=[b_c1bf])
    P.op("vector", lambda e: e.tensor_copy(t_c1bf[:, 4 * POS:], t_c1b[:]),
         reads=[b_c1b], writes=[b_c1bf])
    dbg_dump("c2", t_c1bf[:], [b_c1bf])

    b_hD = [B_("hD0"), B_("hD1")]
    b_x2r = [B_("x2r0"), B_("x2r1")]
    b_hbfD = B_("hbfD")
    b_sqD = B_("sqD")
    hD_t = [t_hD, t_hD2]
    x2r_t = [t_x2r, t_x2r2]
    hD_alias = [[b_c1a], [b_c1b]]
    x2r_alias = [[b_ui2], [b_uc2]]
    st_toks = []
    chunks = [(b, c) for b in range(B) for c in range(NCH)]

    def D_compute(idx, ms):
        b, c = chunks[idx]
        par = idx % 2
        thD, tx2 = hD_t[par], x2r_t[par]
        bhD, bx2 = b_hD[par], b_x2r[par]
        if ms[0] == 0:
            dma("sync", "SY", tx2[:], x2_sp[b, c],
                writes=[bx2] + x2r_alias[par], deps=[x2_tok[(b, c)]])
        for m in ms:
            bank = (c * G + m) % 4
            P.op("tensor", gate_mms(t_s0, m, z_t[b], c, bank,
                                    extra=(t_s3, t_c1bf)),
                 reads=[b_slot["s0"], b_slot["s3"], b_z[b][c], b_c1bf],
                 writes=[b_pmm[bank]])
            P.op("scalar",
                 lambda e, m=m, bank=bank, thD=thD: e.activation(
                     thD[:, m * CH:(m + 1) * CH], pmm[bank][:, 0:CH],
                     AF.Tanh, bias=t_b["bo"][:, m:m + 1]),
                 reads=[b_pmm[bank], B_("b_bo")],
                 writes=[bhD] + hD_alias[par])
        if b == 0 and c == 0:
            dbg_dump("hD00", thD[:], [bhD])

    def D_ln_stats(idx):
        b, c = chunks[idx]
        par = idx % 2
        thD = hD_t[par]
        bhD = b_hD[par]
        ln_stats(thD, bhD, t_hbfD, alias=[b_hbfD, b_abf], eng="vector")
        rows_block(t_ceps[0:1, :], False, True)

    def D_ln_tail(idx):
        b, c = chunks[idx]
        par = idx % 2
        thD, tx2 = hD_t[par], x2r_t[par]
        bhD, bx2 = b_hD[par], b_x2r[par]
        P.op("tensor", bcast(r_r, 0), reads=[b_rows, b_consts],
             writes=[b_pbc[0]])
        P.op("tensor", bcast(r_mr, 1), reads=[b_rows, b_consts],
             writes=[b_pbc[1]])
        P.op("vector",
             lambda e, thD=thD: e.tensor_tensor(g3(thD), g3(thD), bc3(pbc[0]),
                                                ALU.mult),
             reads=[b_pbc[0]], writes=[bhD])
        P.op("vector",
             lambda e, thD=thD: e.tensor_tensor(g3(thD), g3(thD), bc3(pbc[1]),
                                                ALU.subtract),
             reads=[b_pbc[1]], writes=[bhD])
        P.op("vector",
             lambda e, thD=thD, tx2=tx2: e.tensor_tensor(
                 tx2[:], thD[:], tx2[:], ALU.add),
             reads=[bhD], writes=[bx2])
        st_toks.append(
            dma("scalar", "SQ",
                outT[b].rearrange("(g p) c -> p g c", p=128)[:, :, c * CH:(c + 1) * CH],
                tx2.ap().rearrange("p (g c) -> p g c", g=G),
                reads=[bx2]))

    D_compute(0, range(G))
    for idx in range(1, len(chunks)):
        D_compute(idx, range(0, 6))
        D_ln_stats(idx - 1)
        D_compute(idx, range(6, 8))
        D_ln_tail(idx - 1)
    D_ln_stats(len(chunks) - 1)
    D_ln_tail(len(chunks) - 1)

    P.seq["scalar"].append((
        [(s, v) for (_e, s, v) in st_toks], lambda e: None, None, 0))

    nc._rec = []
    nc._plan = P
    with nc.Block() as block:

        @block.tensor
        def _(eng):
            P.emit("tensor", eng, sems, nc, nc._rec)

        @block.vector
        def _(eng):
            P.emit("vector", eng, sems, nc, nc._rec)

        @block.scalar
        def _(eng):
            P.emit("scalar", eng, sems, nc, nc._rec)

        @block.gpsimd
        def _(eng):
            P.emit("gpsimd", eng, sems, nc, nc._rec)

        @block.sync
        def _(eng):
            P.emit("sync", eng, sems, nc, nc._rec)

    return nc


_NC_CACHE = {}


def kernel(L0_attn_x, L0_attn_w_qkva, L0_attn_gamma,
           L0_ffn_wf, L0_ffn_uf, L0_ffn_bf, L0_ffn_wi, L0_ffn_ui, L0_ffn_bi,
           L0_ffn_wc, L0_ffn_uc, L0_ffn_bc, L0_ffn_wo, L0_ffn_uo, L0_ffn_bo,
           L0_ffn_in_c):
    x = np.asarray(L0_attn_x, dtype=np.float32)
    SEQ = x.shape[1]
    PC = SEQ // NCORES

    def feat(v):
        return np.ascontiguousarray(np.asarray(v, np.float32).reshape(G, 128).T)

    def tbf(v):
        b = np.ascontiguousarray(v, np.float32)
        return b.view(np.uint16).reshape(*b.shape, 2)[..., 1].copy().view(
            ml_dtypes.bfloat16)

    shared = {
        "wqkT": tbf(np.asarray(L0_attn_w_qkva, np.float32).T),
        "gamma": feat(L0_attn_gamma),
        "wfT": tbf(np.asarray(L0_ffn_wf, np.float32).T),
        "wiT": tbf(np.asarray(L0_ffn_wi, np.float32).T),
        "wcT": tbf(np.asarray(L0_ffn_wc, np.float32).T),
        "woT": tbf(np.asarray(L0_ffn_wo, np.float32).T),
        "ufT": tbf(np.asarray(L0_ffn_uf, np.float32).T),
        "uiT": tbf(np.asarray(L0_ffn_ui, np.float32).T),
        "ucT": tbf(np.asarray(L0_ffn_uc, np.float32).T),
        "uoT": tbf(np.asarray(L0_ffn_uo, np.float32).T),
        "bf_b": feat(L0_ffn_bf), "bi_b": feat(L0_ffn_bi),
        "bc_b": feat(L0_ffn_bc), "bo_b": feat(L0_ffn_bo),
        "inc_b": feat(L0_ffn_in_c),
    }
    in_maps = []
    for k in range(NCORES):
        m = dict(shared)
        m["x0T"] = np.ascontiguousarray(x[0, k * PC:(k + 1) * PC, :].T)
        m["x1T"] = np.ascontiguousarray(x[1, k * PC:(k + 1) * PC, :].T)
        m["x0Tb"] = tbf(m["x0T"])
        m["x1Tb"] = tbf(m["x1T"])
        mlt = np.zeros((128, G), np.float32)
        mlt[:, :k] = 1.0
        m["mask_lt"] = mlt
        m["mask_c"] = np.ascontiguousarray(1.0 - mlt)
        in_maps.append(m)

    if "nc" not in _NC_CACHE:
        _NC_CACHE["nc"] = build_nc()
    res = run_bass_kernel_spmd(_NC_CACHE["nc"], in_maps,
                               core_ids=list(range(NCORES)))
    _NC_CACHE["res"] = res

    out = np.empty((B, SEQ, D), np.float32)
    for k in range(NCORES):
        ot = res.results[k]["outT"]
        out[0, k * PC:(k + 1) * PC, :] = ot[0].T
        out[1, k * PC:(k + 1) * PC, :] = ot[1].T
    return out


if __name__ == "__main__":
    build_nc()
    print("built ok")


# version 38
# speedup vs baseline: 1.0079x; 1.0079x over previous
import numpy as np
import ml_dtypes
import concourse.bass as bass
import concourse.mybir as mybir
from concourse.bass_utils import run_bass_kernel_spmd

F32 = mybir.dt.float32
BF16 = mybir.dt.bfloat16
AF = mybir.ActivationFunctionType
ALU = mybir.AluOpType

NCORES = 8
D = 1024
G = 8
POS = 1024
B = 2
CH = 512
NCH = POS // CH
KB = 1024

ENG_SEM = {"tensor": "T", "vector": "V", "scalar": "S", "gpsimd": "G", "sync": "Y"}


class Buf:
    __slots__ = ("name", "w", "r")

    def __init__(self, name):
        self.name = name
        self.w = None
        self.r = []


class Plan:
    def __init__(self):
        self.seq = {e: [] for e in ENG_SEM}
        self.cnt = {}
        self.rings = {}

    def ring(self, name, n):
        self.rings[name] = {"sems": [f"{name}_{i}" for i in range(n)], "i": 0,
                            "last": [None] * n}

    def op(self, eng, fn, reads=(), writes=(), deps=(), queue=None, sem=None,
           inc=None):
        dep_list = []
        for b in reads:
            if b.w is not None:
                dep_list.append(b.w)
        for b in writes:
            if b.w is not None:
                dep_list.append(b.w)
            dep_list.extend(b.r)
        dep_list.extend(t for t in deps if t is not None)
        slot = None
        if queue is not None:
            rg = self.rings[queue]
            slot = rg["i"]
            rg["i"] = (slot + 1) % len(rg["sems"])
            sem = rg["sems"][slot]
            inc = 16
            if rg["last"][slot] is not None:
                dep_list.append(rg["last"][slot])
            async_tok = True
        elif sem is not None:
            async_tok = True
        else:
            sem = ENG_SEM[eng]
            inc = 1
            async_tok = False
        waits = []
        for (teng, tsem, tval) in dep_list:
            if teng == eng:
                continue
            waits.append((tsem, tval))
        self.cnt[sem] = self.cnt.get(sem, 0) + inc
        tok = (None if async_tok else eng, sem, self.cnt[sem])
        self.seq[eng].append((waits, fn, sem, inc))
        if queue is not None:
            self.rings[queue]["last"][slot] = tok
        for b in reads:
            b.r.append(tok)
        for b in writes:
            b.w = tok
            b.r = []
        return tok

    def emit(self, eng_name, eng, sems, nc=None, rec=None):
        hw = {}
        for oi, (waits, fn, sem, inc) in enumerate(self.seq[eng_name]):
            emitted = []
            for (sname, val) in waits:
                if hw.get(sname, 0) >= val:
                    continue
                eng.wait_ge(sems[sname], val)
                hw[sname] = val
                emitted.append((sname, val))
            n0 = nc.n_instructions() if nc is not None else 0
            inst = fn(eng)
            if inst is not None and sem is not None:
                inst.then_inc(sems[sem], inc)
            if rec is not None:
                rec.append((eng_name, oi, nc.n_instructions() - n0,
                            emitted, sem, inc))


_DEBUG = False
_NO_CC = False


def build_nc(dbg=False):
    nc = bass.Bass("TRN2", target_bir_lowering=False)

    xT = [nc.dram_tensor(f"x{b}T", [D, POS], F32, kind="ExternalInput").ap()
          for b in range(B)]
    xTb = [nc.dram_tensor(f"x{b}Tb", [D, POS], BF16, kind="ExternalInput").ap()
           for b in range(B)]
    wqkT_d = nc.dram_tensor("wqkT", [D, 3 * D], BF16, kind="ExternalInput").ap()
    gamma_in = nc.dram_tensor("gamma", [128, G], F32, kind="ExternalInput").ap()
    wT = {}
    for nm in ("wf", "wi", "wc", "wo", "uf", "ui", "uc", "uo"):
        wT[nm] = nc.dram_tensor(nm + "T", [D, D], BF16, kind="ExternalInput").ap()
    bias_in = {}
    for nm in ("bf", "bi", "bc", "bo", "inc"):
        bias_in[nm] = nc.dram_tensor(nm + "_b", [128, G], F32,
                                     kind="ExternalInput").ap()
    mlt_in = nc.dram_tensor("mask_lt", [128, G], F32, kind="ExternalInput").ap()
    mlc_in = nc.dram_tensor("mask_c", [128, G], F32, kind="ExternalInput").ap()
    outT = nc.dram_tensor("outT", [B, D, POS], F32, kind="ExternalOutput").ap()

    _spk = "ExternalOutput" if dbg else "Internal"
    uv_sp = nc.dram_tensor("uv_sp", [B, 2, NCH, 128, G * CH], BF16,
                           kind=_spk).ap()
    x2_sp = nc.dram_tensor("x2_sp", [B, NCH, 128, G * CH], F32, kind=_spk).ap()
    cc_in = [nc.dram_tensor(f"cc_in{b}", [128, 16], F32) for b in range(B)]
    cc_out = [nc.dram_tensor(f"cc_out{b}", [NCORES, 128, 16], F32)
              for b in range(B)]
    cc_all_in = nc.dram_tensor("cc_all_in", [128, 32], F32)
    cc_all_out = nc.dram_tensor("cc_all_out", [NCORES, 128, 32], F32)
    cc_wrm = nc.dram_tensor("cc_wrm", [128, 16], F32)
    cc_wrm_o = nc.dram_tensor("cc_wrm_o", [NCORES, 128, 16], F32)

    SB_BASE = nc.sbuf_top - nc.sbuf_bytes_remaining
    SB_BASE = (SB_BASE + 127) // 128 * 128
    assert nc.sbuf_bytes_remaining >= 205 * KB, nc.sbuf_bytes_remaining

    def at(name, shape, dtype, off):
        return nc.alloc_sbuf_tensor_at(name, shape, dtype, offset=SB_BASE + off)

    o = 0

    def bump(name, shape, dtype, align=32):
        nonlocal o
        es = mybir.dt.size(dtype)
        o = (o + align - 1) // align * align
        h = at(name, shape, dtype, o)
        n = 1
        for s in shape[1:]:
            n *= s
        o += n * es
        return h

    t_gamma = bump("gammat", [128, G], F32)
    t_b = {nm: bump(nm + "t", [128, G], F32)
           for nm in ("bf", "bi", "bc", "bo", "inc")}
    t_incbf = bump("incbf", [128, G], BF16)
    t_mlt = bump("mltt", [128, G], F32)
    t_mlc = bump("mlct", [128, G], F32)
    t_cy = bump("cyt", [128, 2 * G], F32)
    t_cp = bump("cpt", [128, 2 * G], F32)
    t_cin = bump("cint", [128, 2 * G], F32)
    t_cst = bump("cstt", [128, 2 * G], F32)
    t_tmpA = bump("tmpAt", [128, 2 * G], F32)
    t_tmpY = bump("tmpYt", [128, 2 * G], F32)
    t_vr = {nm: bump("vr" + nm, [128, G], F32) for nm in ("f", "i", "c")}
    t_b0 = {nm: bump("b0" + nm, [128, G], F32) for nm in ("f", "i", "c")}
    t_summ = bump("summt", [128, 32], F32)
    t_gath = bump("gatht", [128, 256], F32)
    t_onescol = bump("onescol", [128, 1], BF16)
    t_onesrow = bump("onesrow", [1, 128], F32)
    t_ones512 = bump("ones512", [128, CH], F32)
    t_ceps = bump("cepst", [1, 1], F32)
    t_cmax = bump("cmaxt", [1, 1], F32)
    r_m = bump("r_m", [1, CH], F32)
    r_v = bump("r_v", [1, CH], F32)
    r_r = bump("r_r", [1, CH], F32)
    r_mr = bump("r_mr", [1, CH], F32)
    t_sg = [bump(f"sg{i}", [128, CH], F32) for i in range(3)]
    t_cbA = bump("cbA", [128, 72], F32)
    t_cbY = bump("cbY", [128, 72], F32)
    t_cbS = bump("cbS", [128, 72], F32)
    assert o <= 20 * KB, f"small region overflow: {o}"

    t_wqk = at("wqk", [128, G * 3 * D], BF16, 20 * KB)
    t_s0 = at("s0", [128, G * D], BF16, 20 * KB)
    t_s1 = at("s1", [128, G * D], BF16, 36 * KB)
    t_sqB = at("sqB", [128, G * CH], BF16, 36 * KB)
    t_s2 = at("s2", [128, G * D], BF16, 52 * KB)
    t_xbf = at("xbf", [128, G * POS], BF16, 68 * KB)
    t_s3 = at("s3", [128, G * D], BF16, 68 * KB)
    t_hbf0 = at("hbf0", [128, G * POS], BF16, 84 * KB)
    t_s4 = at("s4", [128, G * D], BF16, 84 * KB)
    t_vchE2 = at("vchE2", [128, G * CH], BF16, 92 * KB)
    t_hD = at("hD", [128, G * CH], F32, 84 * KB)
    t_hbf1 = at("hbf1", [128, G * POS], BF16, 100 * KB)
    t_s5 = at("s5", [128, G * D], BF16, 100 * KB)
    t_x2r = at("x2r", [128, G * CH], F32, 100 * KB)
    t_sqA = at("sqA", [128, G * CH], BF16, 116 * KB)
    t_xm = at("xmscr", [128, G * (CH // 2)], F32, 116 * KB)
    t_uchE = at("uchE", [128, G * CH], BF16, 116 * KB)
    t_abf = at("abf", [128, G * CH], BF16, 116 * KB)
    t_hbfD = at("hbfD", [128, G * CH], BF16, 116 * KB)
    t_y = at("ych", [128, G * CH], F32, 124 * KB)
    t_c1a = at("c1a", [128, 4 * POS], F32, 124 * KB)
    t_hD = at("hD", [128, G * CH], F32, 124 * KB)
    t_P = at("Pch", [128, G * CH], F32, 140 * KB)
    t_uch = at("uch", [128, G * CH], F32, 140 * KB)
    t_uchb = at("uchb", [128, G * CH], BF16, 140 * KB)
    t_vchb = at("vchb", [128, G * CH], BF16, 148 * KB)
    t_ui2 = at("ui2", [128, G * D], BF16, 140 * KB)
    t_x2r = at("x2r", [128, G * CH], F32, 140 * KB)
    t_q = at("qch", [128, G * CH], F32, 156 * KB)
    t_vch = at("vch", [128, G * CH], F32, 156 * KB)
    t_uc2 = at("uc2", [128, G * D], BF16, 156 * KB)
    t_x2r2 = at("x2r2", [128, G * CH], F32, 156 * KB)
    t_kv = at("kvch", [128, G * CH], F32, 172 * KB)
    t_z1 = at("z1", [128, G * POS], BF16, 172 * KB)
    t_a = at("ach", [128, G * CH], F32, 188 * KB)
    t_ubf = at("ubf", [128, G * CH], BF16, 188 * KB)
    t_vbf = at("vbf", [128, G * CH], BF16, 196 * KB)
    t_z0 = at("z0", [128, G * POS], BF16, 188 * KB)
    t_c1b = at("c1b", [128, 4 * POS], F32, 84 * KB)
    t_hD2 = at("hD2", [128, G * CH], F32, 84 * KB)
    t_c1bf = at("c1bf", [128, G * POS], BF16, 100 * KB)

    pmm = [nc.alloc_psum_tensor(f"pmm{i}", [128, 512], F32) for i in range(4)]
    pstat = [nc.alloc_psum_tensor(f"pstat{i}", [1, 512], F32) for i in range(2)]
    pbc = [nc.alloc_psum_tensor(f"pbc{i}", [128, 512], F32) for i in range(2)]

    sems = {}
    for s in ("T", "V", "S", "G", "Y", "CC"):
        sems[s] = nc.alloc_semaphore(s)
    P = Plan()
    P.ring("SY", 6)
    P.ring("SQ", 4)
    P.ring("GQ", 2)
    for rg in ("SY", "SQ", "GQ"):
        for s in P.rings[rg]["sems"]:
            sems[s] = nc.alloc_semaphore(s)

    bufs = {}

    def B_(name):
        if name not in bufs:
            bufs[name] = Buf(name)
        return bufs[name]

    def dma(eng, queue, dst_ap, src_ap, reads=(), writes=(), deps=()):
        return P.op(eng, lambda e: e.dma_start(dst_ap, src_ap),
                    reads=reads, writes=writes, deps=deps, queue=queue)

    def vguard(bufs_to_stamp):
        P.op("vector", lambda e: e.memset(t_sg[2][:], 0.0),
             reads=[], writes=[B_("sg2")])
        P.op("vector", lambda e: e.memset(t_sg[2][:], 0.0),
             reads=[], writes=[B_("sg2")] + list(bufs_to_stamp))

    dbg_names = []

    def dbg_dump(name, src_ap, reads):
        if not dbg:
            return
        shp = list(src_ap.shape)
        dt = src_ap.dtype
        d = nc.dram_tensor("dbg_" + name, shp, dt, kind="ExternalOutput").ap()
        dbg_names.append("dbg_" + name)
        dma("sync", "SY", d, src_ap, reads=list(reads))

    def g3(t, c0=0, ncc=CH, width=CH, ng=G):
        ap = t.ap()
        row = ap.ap[0][0]
        return bass.AP(t, c0, [[row, 128], [width, ng], [1, ncc]])

    def bc3(pt, c0=0, ncc=CH, ng=G):
        ap = pt.ap()
        row = ap.ap[0][0]
        return bass.AP(pt, c0, [[row, 128], [0, ng], [1, ncc]])

    b_pstat = [B_("pstat0"), B_("pstat1")]
    b_pbc = [B_("pbc0"), B_("pbc1")]
    b_pmm = [B_(f"pmm{i}") for i in range(4)]
    b_rows = B_("rows")
    b_consts = B_("consts")
    b_params = B_("params")

    b_xbf = B_("xbf")
    b_wqk = B_("wqk")
    b_wqk3 = [B_("wqk3a"), B_("wqk3b"), B_("wqk3c")]
    dma("scalar", "SQ",
        t_xbf.ap().rearrange("p (g c) -> p g c", g=G),
        xTb[0].rearrange("(g p) c -> p g c", p=128),
        writes=[b_xbf])
    dma("scalar", "SQ",
        t_wqk.ap().rearrange("p (g o) -> p g o", g=G),
        wqkT_d.rearrange("(g p) o -> p g o", p=128),
        writes=[b_wqk] + b_wqk3)
    dma("sync", "SY", t_gamma[:], gamma_in, writes=[b_params])
    for nm in ("bf", "bi", "bc", "bo", "inc"):
        dma("sync", "SY", t_b[nm][:], bias_in[nm], writes=[B_("b_" + nm)])
    dma("sync", "SY", t_mlt[:], mlt_in, writes=[B_("masks")])
    dma("sync", "SY", t_mlc[:], mlc_in, writes=[B_("masks")])

    def consts(e):
        e.memset(t_onescol[:], 1.0 / 1024.0)
        e.memset(t_onesrow[:], 1.0)
        e.memset(t_ones512[:], 1.0)
        e.memset(t_ceps[:], 1e-5)
        e.memset(t_cmax[:], 9.5e-28)
        e.memset(t_cy[:], 0.0)
        e.memset(t_cp[:], 1.0)
        return e.memset(t_cst[:], 0.0)
    P.op("vector", consts, writes=[b_consts, B_("cy"), B_("cp"), B_("cst")])

    P.op("vector", lambda e: e.tensor_copy(t_incbf[:], t_b["inc"][:]),
         reads=[B_("b_inc")], writes=[B_("incbf")])


    def rows_block(eps_ap, floor, want_mr):
        er = "vector"
        P.op(er, lambda e: e.tensor_copy(r_m[:], pstat[0][0:1, :]),
             reads=[b_pstat[0]], writes=[b_rows])

        def f_var(e):
            e.tensor_tensor(r_v[:], r_m[:], r_m[:], ALU.mult)
            return e.tensor_tensor(r_v[:], pstat[1][0:1, :], r_v[:], ALU.subtract)
        P.op(er, f_var, reads=[b_pstat[1]], writes=[b_rows])
        if floor:
            P.op("vector",
                 lambda e: e.tensor_scalar(r_v[:], r_v[:], t_cmax[0:1, :], None,
                                           ALU.max),
                 reads=[b_consts], writes=[b_rows])

        def f_ln(e):
            if eps_ap is None:
                e.activation(r_r[:], r_v[:], AF.Ln)
            else:
                e.activation(r_r[:], r_v[:], AF.Ln, bias=eps_ap)
            return e.activation(r_r[:], r_r[:], AF.Exp, scale=-0.5)
        P.op("scalar", f_ln, reads=[b_rows, b_consts], writes=[b_rows])
        if want_mr:
            P.op(er, lambda e: e.tensor_tensor(r_mr[:], r_m[:], r_r[:],
                                               ALU.mult),
                 reads=[b_rows], writes=[b_rows])

    def bcast(row_tile, which):
        def f(e):
            return e.matmul(pbc[which][:, :], t_onesrow[:], row_tile[:],
                            start=True, stop=True)
        return f

    def stat_sum(src_tile, src_buf):
        def fsum(e):
            ins = None
            for g in range(G):
                ins = e.matmul(pstat[0][0:1, :], t_onescol[:],
                               src_tile[:, g * CH:(g + 1) * CH],
                               start=(g == 0), stop=(g == G - 1))
            return ins
        P.op("tensor", fsum, reads=[src_buf, b_consts], writes=[b_pstat[0]])

    def stat_sq(sq_tile, sq_buf):
        def fsq(e):
            ins = None
            for g in range(G):
                ins = e.matmul(pstat[1][0:1, :], t_onescol[:],
                               sq_tile[:, g * CH:(g + 1) * CH],
                               start=(g == 0), stop=(g == G - 1))
            return ins
        P.op("tensor", fsq, reads=[sq_buf, b_consts], writes=[b_pstat[1]])

    b_abfh = [B_("abfh0"), B_("abfh1")]

    def stat_half(bank, tile, b_half, h, start):
        lo = h * 4 * CH

        def f(e, lo=lo, bank=bank, start=start):
            ins = None
            for g in range(4):
                ins = e.matmul(pstat[bank][0:1, :], t_onescol[:],
                               tile[:, lo + g * CH: lo + (g + 1) * CH],
                               start=(start and g == 0),
                               stop=(not start and g == 3))
            return ins
        P.op("tensor", f, reads=[b_half, b_consts], writes=[b_pstat[bank]])

    def ln_stats(src_tile, b_src, abf_tile, alias=(), eng="scalar"):
        H = 4 * CH

        def cast_f(e, h):
            if eng == "scalar":
                return e.activation(abf_tile[:, h * H:(h + 1) * H],
                                    src_tile[:, h * H:(h + 1) * H], AF.Copy)
            return e.tensor_copy(abf_tile[:, h * H:(h + 1) * H],
                                 src_tile[:, h * H:(h + 1) * H])

        def sq_f(e, h):
            if eng == "scalar":
                return e.activation(abf_tile[:, h * H:(h + 1) * H],
                                    src_tile[:, h * H:(h + 1) * H], AF.Square)
            return e.tensor_tensor(abf_tile[:, h * H:(h + 1) * H],
                                   src_tile[:, h * H:(h + 1) * H],
                                   src_tile[:, h * H:(h + 1) * H], ALU.mult)
        for h in range(2):
            P.op(eng, lambda e, h=h: cast_f(e, h),
                 reads=[b_src], writes=[b_abfh[h]] + list(alias))
        for h in range(2):
            stat_half(0, abf_tile, b_abfh[h], h, start=(h == 0))
        for h in range(2):
            P.op(eng, lambda e, h=h: sq_f(e, h),
                 reads=[b_src], writes=[b_abfh[h]])
        for h in range(2):
            stat_half(1, abf_tile, b_abfh[h], h, start=(h == 0))

    def stat_pair(src_tile, src_buf, src_off, src_w, sq_tile, sq_buf):
        def fsum(e):
            ins = None
            for g in range(G):
                ins = e.matmul(pstat[0][0:1, :], t_onescol[:],
                               src_tile[:, g * src_w + src_off:
                                        g * src_w + src_off + CH],
                               start=(g == 0), stop=(g == G - 1))
            return ins
        P.op("tensor", fsum, reads=[src_buf, b_consts], writes=[b_pstat[0]])

        def fsq(e):
            ins = None
            for g in range(G):
                ins = e.matmul(pstat[1][0:1, :], t_onescol[:],
                               sq_tile[:, g * CH:(g + 1) * CH],
                               start=(g == 0), stop=(g == G - 1))
            return ins
        P.op("tensor", fsq, reads=[sq_buf, b_consts], writes=[b_pstat[1]])

    def rank1_mm(nm, off):
        def f(e, nm=nm, off=off):
            ins = None
            t = wslot[nm]
            for m in range(G):
                for g in range(G):
                    ins = e.matmul(pmm[3][:, off + m:off + m + 1],
                                   t[:, g * D + m * 128: g * D + (m + 1) * 128],
                                   t_incbf[:, g:g + 1],
                                   start=(g == 0), stop=(g == G - 1))
            return ins
        P.op("tensor", f, reads=[b_slot[slot_of[nm]], B_("incbf")],
             writes=[b_pmm[3]])

    def rank1_fin():
        for off, gate, bname in ((0, "f", "bf"), (G, "i", "bi"),
                                 (2 * G, "c", "bc")):
            P.op("vector",
                 lambda e, off=off, gate=gate, bname=bname: e.tensor_tensor(
                     t_b0[gate][:], pmm[3][:, off:off + G], t_b[bname][:],
                     ALU.add),
                 reads=[b_pmm[3], B_("b_" + bname)], writes=[B_("b0" + gate)])
        vguard([B_("b0f"), B_("b0i"), B_("b0c")])


    b_sqA = B_("sqA")
    b_hbf = [B_("hbf0"), B_("hbf1")]
    b_q = [B_(f"qch{g}") for g in range(G)]
    b_kv = [B_(f"kvch{g}") for g in range(G)]
    b_a = [B_(f"ach{g}") for g in range(G)]
    b_y, b_P = B_("ych"), B_("Pch")
    b_ubf, b_vbf = B_("ubf"), B_("vbf")
    b_cy, b_cp = B_("cy"), B_("cp")
    hbf_t = [t_hbf0, t_hbf1]
    uv_tok = {}

    def ln1_block(b):
        for c in range(NCH):
            P.op("scalar",
                 lambda e, c=c: e.activation(
                     g3(t_sqA), g3(t_xbf, c0=c * CH, width=POS), AF.Square),
                 reads=[b_xbf], writes=[b_sqA])
            stat_pair(t_xbf, b_xbf, c * CH, POS, t_sqA, b_sqA)
            rows_block(None, True, False)
            P.op("tensor", bcast(r_m, 0), reads=[b_rows, b_consts],
                 writes=[b_pbc[0]])
            P.op("tensor", bcast(r_r, 1), reads=[b_rows, b_consts],
                 writes=[b_pbc[1]])
            for h2 in range(2):
                c0 = c * CH + h2 * (CH // 2)
                P.op("vector",
                     lambda e, c0=c0: e.tensor_tensor(
                         g3(t_xm, ncc=CH // 2, width=CH // 2),
                         g3(t_xbf, c0=c0, ncc=CH // 2, width=POS),
                         bc3(pbc[0], c0=(c0 % CH), ncc=CH // 2),
                         ALU.subtract),
                     reads=[b_xbf, b_pbc[0]], writes=[b_sqA])
                for g in range(G):
                    P.op("vector",
                         lambda e, b=b, c0=c0, g=g: e.scalar_tensor_tensor(
                             hbf_t[b][:, g * POS + c0:
                                      g * POS + c0 + CH // 2],
                             t_xm[:, g * (CH // 2):(g + 1) * (CH // 2)],
                             t_gamma[:, g:g + 1],
                             pbc[1][:, (c0 % CH):(c0 % CH) + CH // 2],
                             ALU.mult, ALU.mult),
                         reads=[b_sqA, b_pbc[1], b_params],
                         writes=[b_hbf[b]])
        dbg_dump(f"hbf{b}", hbf_t[b][:], [b_hbf[b]])

    def qkva_chunk(b, c):
        order = []
        for g in range(G):
            order.append((8 + g, "kv"))
            order.append((16 + g, "a"))
        for g in range(G):
            order.append((g, "q"))
        for k, (m, kind) in enumerate(order):
            bank = k % 4

            def fmm2(e, b=b, c=c, m=m, bank=bank):
                ins = None
                for g in range(G):
                    ins = e.matmul(
                        pmm[bank][:, 0:CH],
                        t_wqk[:, g * 3 * D + m * 128:
                              g * 3 * D + (m + 1) * 128],
                        hbf_t[b][:, g * POS + c * CH:
                                 g * POS + (c + 1) * CH],
                        start=(g == 0), stop=(g == G - 1))
                return ins
            P.op("tensor", fmm2, reads=[b_wqk, b_hbf[b]] + b_wqk3,
                 writes=[b_pmm[bank]])
            if kind == "q":
                P.op("scalar",
                     lambda e, m=m, bank=bank: e.copy(
                         t_q[:, m * CH:(m + 1) * CH], pmm[bank][:, 0:CH]),
                     reads=[b_pmm[bank]], writes=[b_q[m]])
            elif kind == "kv":
                P.op("scalar",
                     lambda e, m=m, bank=bank: e.copy(
                         t_kv[:, (m - 8) * CH:(m - 7) * CH],
                         pmm[bank][:, 0:CH]),
                     reads=[b_pmm[bank]], writes=[b_kv[m - 8]])
            else:
                P.op("scalar",
                     lambda e, m=m, bank=bank: e.activation(
                         t_a[:, (m - 16) * CH:(m - 15) * CH],
                         pmm[bank][:, 0:CH], AF.Sigmoid),
                     reads=[b_pmm[bank]],
                     writes=[b_a[m - 16],
                             b_ubf if (m - 16) < 4 else b_vbf])
        if b == 0 and dbg:
            dbg_dump(f"q0{c}", t_q[:], list(b_q))
            dbg_dump(f"kv0{c}", t_kv[:], list(b_kv))
            dbg_dump(f"a0{c}", t_a[:], list(b_a))
        for g in range(G):
            sl = slice(g * CH, (g + 1) * CH)
            cyc = t_cy[:, b * G + g: b * G + g + 1]
            P.op("vector",
                 lambda e, sl=sl, cyc=cyc: e.tensor_tensor_scan(
                     t_y[:, sl], t_a[:, sl], t_kv[:, sl], cyc,
                     ALU.mult, ALU.add),
                 reads=[b_a[g], b_kv[g], b_cy], writes=[b_y])
        for g in range(G):
            sl = slice(g * CH, (g + 1) * CH)
            cpc = t_cp[:, b * G + g: b * G + g + 1]
            P.op("vector",
                 lambda e, sl=sl, cpc=cpc: e.tensor_tensor_scan(
                     t_P[:, sl], t_a[:, sl], t_ones512[:], cpc,
                     ALU.mult, ALU.mult),
                 reads=[b_a[g], b_consts, b_cp], writes=[b_P])
        for g in range(G):
            sl = slice(g * CH, (g + 1) * CH)
            cyc = t_cy[:, b * G + g: b * G + g + 1]
            P.op("vector",
                 lambda e, sl=sl, cyc=cyc: e.tensor_copy(
                     cyc, t_y[:, sl.stop - 1: sl.stop]),
                 reads=[b_y], writes=[b_cy])
        for g in range(G):
            sl = slice(g * CH, (g + 1) * CH)
            cpc = t_cp[:, b * G + g: b * G + g + 1]
            P.op("vector",
                 lambda e, sl=sl, cpc=cpc: e.tensor_copy(
                     cpc, t_P[:, sl.stop - 1: sl.stop]),
                 reads=[b_P], writes=[b_cp])
        peng = "gpsimd" if (b, c) == (1, 1) else "vector"
        for g in range(G):
            sl = slice(g * CH, (g + 1) * CH)
            P.op(peng,
                 lambda e, sl=sl: e.tensor_tensor(
                     t_ubf[:, sl], t_y[:, sl], t_q[:, sl], ALU.mult),
                 reads=[b_y, b_q[g]],
                 writes=[b_ubf] + ([b_a[g2] for g2 in range(4)]
                                   if g == 0 else []))
        for g in range(G):
            sl = slice(g * CH, (g + 1) * CH)
            P.op(peng,
                 lambda e, sl=sl: e.tensor_tensor(
                     t_vbf[:, sl], t_P[:, sl], t_q[:, sl], ALU.mult),
                 reads=[b_P, b_q[g]],
                 writes=[b_vbf] + ([b_a[g2] for g2 in range(4, G)]
                                   if g == 0 else []))
        if b == 0 and dbg:
            dbg_dump(f"y0{c}", t_y[:], [b_y])
            dbg_dump(f"P0{c}", t_P[:], [b_P])
            dbg_dump(f"cyA{c}", t_cy[:], [b_cy])
        uv_tok[(b, 0, c)] = dma("sync", "SY", uv_sp[b, 0, c], t_ubf[:],
                                reads=[b_ubf])
        uv_tok[(b, 1, c)] = dma("sync", "SY", uv_sp[b, 1, c], t_vbf[:],
                                reads=[b_vbf])

    b_summ = [B_("summ0"), B_("summ1")]
    b_gath = [B_("gath0"), B_("gath1")]
    b_cin = [B_("cin0"), B_("cin1")]
    tok_cc = [None, None]

    def x_trigger(b):
        P.op("vector",
             lambda e, b=b: e.tensor_copy(
                 t_summ[:, b * 16: b * 16 + 8], t_cy[:, b * G:(b + 1) * G]),
             reads=[b_cy], writes=[b_summ[b]])
        P.op("vector",
             lambda e, b=b: e.tensor_copy(
                 t_summ[:, b * 16 + 8: b * 16 + 16],
                 t_cp[:, b * G:(b + 1) * G]),
             reads=[b_cp], writes=[b_summ[b]])
        vguard([b_summ[b]])
        tok_sm = dma("gpsimd", "GQ", cc_in[b].ap(),
                     t_summ[:, b * 16:b * 16 + 16], reads=[b_summ[b]])
        tok = P.op("gpsimd",
                   lambda e, b=b: e.collective_compute(
                       "AllGather", ALU.bypass,
                       replica_groups=[list(range(NCORES))],
                       ins=[cc_in[b].ap()], outs=[cc_out[b].ap()]),
                   deps=[tok_sm], sem="CC", inc=1)
        tok_cc[b] = tok
        dma("gpsimd", "GQ",
            t_gath[:, b * 128:(b + 1) * 128].rearrange("p (r c) -> p r c", r=8),
            cc_out[b].ap().rearrange("r p c -> p r c"),
            writes=[b_gath[b]], deps=[tok])

    def st3(t, off):
        ap = t.ap()
        row = ap.ap[0][0]
        return bass.AP(t, off, [[row, 128], [9, 8], [1, 1]])

    def g8(b, off):
        ap = t_gath.ap()
        row = ap.ap[0][0]
        return bass.AP(t_gath, b * 128 + off, [[row, 128], [1, 8], [1, 1]])

    def x_combine(b):
        P.op("vector", lambda e: e.memset(t_cbA[:], 0.0), writes=[B_("cbA")])
        P.op("vector", lambda e: e.memset(t_cbY[:], 0.0), writes=[B_("cbY")])
        for r in range(NCORES):
            P.op("vector",
                 lambda e, r=r, b=b: e.tensor_scalar(
                     st3(t_cbA, 1 + r), g8(b, r * 16 + 8),
                     t_mlt[:, r:r + 1], t_mlc[:, r:r + 1], ALU.mult, ALU.add),
                 reads=[b_gath[b], B_("masks")], writes=[B_("cbA")])
        for r in range(NCORES):
            P.op("vector",
                 lambda e, r=r, b=b: e.tensor_scalar(
                     st3(t_cbY, 1 + r), g8(b, r * 16),
                     t_mlt[:, r:r + 1], None, ALU.mult),
                 reads=[b_gath[b], B_("masks")], writes=[B_("cbY")])
        P.op("vector", lambda e: e.memset(t_sg[2][:, 0:512], 0.0),
             writes=[B_("sg2")])
        P.op("vector",
             lambda e: e.tensor_tensor_scan(t_cbS[:], t_cbA[:], t_cbY[:],
                                            0.0, ALU.mult, ALU.add),
             reads=[B_("cbA"), B_("cbY")], writes=[B_("cbS")])
        P.op("vector", lambda e: e.memset(t_sg[2][:, 0:512], 0.0),
             writes=[B_("sg2")])

        def stfin():
            ap = t_cbS.ap()
            row = ap.ap[0][0]
            return bass.AP(t_cbS, 8, [[row, 128], [9, 8]])
        P.op("vector",
             lambda e, b=b: e.tensor_copy(t_cin[:, b * G:(b + 1) * G], stfin()),
             reads=[B_("cbS")], writes=[b_cin[b]])
        vguard([b_cin[b]])
        if b == 1:
            dbg_dump("cy", t_cy[:], [b_cy])
            dbg_dump("cp", t_cp[:], [b_cp])
            dbg_dump("gath", t_gath[:], [b_gath[0], b_gath[1]])
            dbg_dump("cin", t_cin[:], [b_cin[0], b_cin[1]])

    wslot = {"wf": t_s0, "wi": t_s1, "wc": t_s2, "uf": t_s3, "ui": t_s4,
             "uc": t_s5, "wo": t_s0, "uo": t_s3}
    slot_of = {"wf": "s0", "wi": "s1", "wc": "s2", "uf": "s3", "ui": "s4",
               "uc": "s5", "wo": "s0", "uo": "s3"}
    b_slot = {s: B_("slot_" + s) for s in ("s0", "s1", "s2", "s3", "s4", "s5")}
    slot_alias = {"s0": [b_wqk3[0]],
                  "s1": [b_wqk3[1], B_("sqB")],
                  "s2": [b_wqk3[2]],
                  "s3": [b_xbf], "s4": [b_hbf[0]], "s5": [b_hbf[1]]}

    def load_w(nm):
        t = wslot[nm]
        s = slot_of[nm]
        return dma("scalar", "SQ",
                   t.ap().rearrange("p (g d) -> p g d", g=G),
                   wT[nm].rearrange("(g p) d -> p g d", p=128),
                   writes=[b_slot[s]] + slot_alias[s])

    ln1_block(0)
    dma("scalar", "SQ",
        t_xbf.ap().rearrange("p (g c) -> p g c", g=G),
        xTb[1].rearrange("(g p) c -> p g c", p=128),
        writes=[b_xbf])
    qkva_chunk(0, 0)
    ln1_block(1)
    load_w("uf")
    qkva_chunk(0, 1)
    load_w("ui")
    x_trigger(0)
    dma("sync", "SY", t_uchE[:], uv_sp[0, 0, 0],
        writes=[B_("uchE"), b_sqA], deps=[uv_tok[(0, 0, 0)]])
    qkva_chunk(1, 0)
    x_combine(0)
    qkva_chunk(1, 1)
    rank1_mm("uf", 0)
    rank1_mm("ui", G)
    dma("gpsimd", "GQ", t_vchE2[:], uv_sp[0, 1, 0],
        writes=[B_("vchE2"), b_slot["s4"]], deps=[uv_tok[(0, 1, 0)]])
    x_trigger(1)
    for _nm in ("uc", "wf", "wi", "wc"):
        load_w(_nm)

    b_uch, b_vch = B_("uch"), B_("vch")
    b_abf = B_("abf")
    b_z = [[B_("z00"), B_("z01")], [B_("z10"), B_("z11")]]
    z_t = [t_z0, t_z1]
    b_c1a, b_c1b, b_c1bf = B_("c1a"), B_("c1b"), B_("c1bf")
    b_ui2, b_uc2 = B_("ui2"), B_("uc2")
    x2_tok = {}

    def c1_ap(m, c):
        t = t_c1a if m < 4 else t_c1b
        mm = m if m < 4 else m - 4
        return t[:, mm * POS + c * CH: mm * POS + (c + 1) * CH]

    def c1_buf(m):
        return b_c1a if m < 4 else b_c1b

    def uv_restream(b, c, early=False):
        if early:
            tu, tv = t_uchE, t_vchE
            bu, bv = B_("uchE"), B_("vchE")
            al = [b_slot["s1"]]
        else:
            tu, tv = t_uchb, t_vchb
            bu, bv = B_("uchb"), B_("vchb")
            al = [b_uch] + ([b_P] if (b, c) == (0, 1) else [])
        dma("gpsimd", "GQ", tu[:], uv_sp[b, 0, c],
            writes=[bu] + al, deps=[uv_tok[(b, 0, c)]])
        dma("gpsimd", "GQ", tv[:], uv_sp[b, 1, c],
            writes=[bv] + al, deps=[uv_tok[(b, 1, c)]])
        return tu, tv, bu, bv

    def B_chunk_p1(b, c, rs):
        tu, tv, bu, bv = rs
        for g in range(G):
            sl = slice(g * CH, (g + 1) * CH)
            P.op("vector",
                 lambda e, sl=sl, b=b, tu=tu, tv=tv, g=g:
                 e.scalar_tensor_tensor(
                     t_vch[:, sl], tv[:, sl],
                     t_cin[:, b * G + g: b * G + g + 1],
                     tu[:, sl], ALU.mult, ALU.add),
                 reads=[bu, bv, b_cin[b]],
                 writes=[b_vch] + (list(b_q) if g == 0 else []))
        ln_stats(t_vch, b_vch, t_abf, alias=[b_abf, b_sqA, B_("uchE")])
        rows_block(t_ceps[0:1, :], False, True)
        P.op("tensor", bcast(r_r, 0), reads=[b_rows, b_consts],
             writes=[b_pbc[0]])
        P.op("tensor", bcast(r_mr, 1), reads=[b_rows, b_consts],
             writes=[b_pbc[1]])
        P.op("vector",
             lambda e: e.tensor_tensor(g3(t_vch), g3(t_vch), bc3(pbc[0]),
                                       ALU.mult),
             reads=[b_pbc[0]], writes=[b_vch])
        P.op("vector",
             lambda e: e.tensor_tensor(g3(t_vch), g3(t_vch), bc3(pbc[1]),
                                       ALU.subtract),
             reads=[b_pbc[1]], writes=[b_vch])
        xal = [b_P] if (b, c) == (0, 0) else []
        dma("sync", "SY",
            t_uch.ap().rearrange("p (g c) -> p g c", g=G),
            xT[b].rearrange("(g p) c -> p g c", p=128)[:, :, c * CH:(c + 1) * CH],
            writes=[b_uch, B_("uchb"), B_("vchb")] + xal)
        P.op("vector",
             lambda e: e.tensor_tensor(t_vch[:], t_vch[:], t_uch[:], ALU.add),
             reads=[b_uch], writes=[b_vch])
        x2_tok[(b, c)] = dma("scalar", "SQ", x2_sp[b, c], t_vch[:],
                             reads=[b_vch])

    def B_chunk_p2(b, c):
        ln_stats(t_vch, b_vch, t_abf, alias=[b_abf])
        rows_block(t_ceps[0:1, :], False, False)
        P.op("tensor", bcast(r_m, 0), reads=[b_rows, b_consts],
             writes=[b_pbc[0]])
        P.op("tensor", bcast(r_r, 1), reads=[b_rows, b_consts],
             writes=[b_pbc[1]])
        P.op("vector",
             lambda e: e.tensor_tensor(g3(t_uch), g3(t_vch), bc3(pbc[0]),
                                       ALU.subtract),
             reads=[b_vch, b_pbc[0]], writes=[b_uch])
        extra_w = (list(b_a) + [b_ubf, b_vbf]) if b == 0 else list(b_kv)
        P.op("vector",
             lambda e, b=b, c=c: e.tensor_tensor(
                 g3(z_t[b], c0=c * CH, width=POS), g3(t_uch), bc3(pbc[1]),
                 ALU.mult),
             reads=[b_uch, b_pbc[1]], writes=[b_z[b][c]] + extra_w)

    def gate_mms(wt, m, acts_tile, c, bank, extra=None):
        def f(e, wt=wt, m=m, c=c, bank=bank, extra=extra):
            ins = None
            n = G if extra is None else 2 * G
            k = 0
            for g in range(G):
                ins = e.matmul(pmm[bank][:, 0:CH],
                               wt[:, g * D + m * 128: g * D + (m + 1) * 128],
                               acts_tile[:, g * POS + c * CH:
                                         g * POS + (c + 1) * CH],
                               start=(k == 0), stop=(k == n - 1))
                k += 1
            if extra is not None:
                wt2, acts2 = extra
                for g in range(G):
                    ins = e.matmul(pmm[bank][:, 0:CH],
                                   wt2[:, g * D + m * 128: g * D + (m + 1) * 128],
                                   acts2[:, g * POS + c * CH:
                                         g * POS + (c + 1) * CH],
                                   start=False, stop=(k == n - 1))
                    k += 1
            return ins
        return f

    def step0_f(c, ms):
        for m in ms:
            bank = (c * G + m) % 4
            P.op("tensor", gate_mms(t_s0, m, t_z0, c, bank),
                 reads=[b_slot["s0"], b_z[0][c], B_("b0f")],
                 writes=[b_pmm[bank]])
            P.op("scalar",
                 lambda e, m=m, bank=bank: e.activation(
                     t_sg[0][:], pmm[bank][:, 0:CH], AF.Sigmoid,
                     bias=t_b0["f"][:, m:m + 1]),
                 reads=[b_pmm[bank], B_("b0f")], writes=[B_("sg0")])
            al = [b_y] if m < 4 else [b_slot["s4"], B_("vchE2")]
            P.op("vector",
                 lambda e, m=m, c=c: e.tensor_scalar_mul(
                     c1_ap(m, c), t_sg[0][:], t_b["inc"][:, m:m + 1]),
                 reads=[B_("sg0"), B_("b_inc")], writes=[c1_buf(m)] + al)

    def step0_ig(c, ms):
        for m in ms:
            bank_i = (c * G + m) % 2
            bank_g = 2 + (c * G + m) % 2
            P.op("tensor", gate_mms(t_s1, m, t_z0, c, bank_i),
                 reads=[b_slot["s1"], b_z[0][c], B_("b0i")],
                 writes=[b_pmm[bank_i]])
            P.op("tensor", gate_mms(t_s2, m, t_z0, c, bank_g),
                 reads=[b_slot["s2"], b_z[0][c], B_("b0c")],
                 writes=[b_pmm[bank_g]])
            P.op("scalar",
                 lambda e, m=m, bank_i=bank_i: e.activation(
                     t_sg[1][:], pmm[bank_i][:, 0:CH], AF.Sigmoid,
                     bias=t_b0["i"][:, m:m + 1]),
                 reads=[b_pmm[bank_i], B_("b0i")], writes=[B_("sg1")])
            P.op("scalar",
                 lambda e, m=m, bank_g=bank_g: e.activation(
                     t_sg[2][:], pmm[bank_g][:, 0:CH], AF.Tanh,
                     bias=t_b0["c"][:, m:m + 1]),
                 reads=[b_pmm[bank_g], B_("b0c")], writes=[B_("sg2")])
            P.op("vector",
                 lambda e: e.tensor_tensor(t_sg[1][:], t_sg[1][:], t_sg[2][:],
                                           ALU.mult),
                 reads=[B_("sg2")], writes=[B_("sg1")])
            P.op("vector",
                 lambda e, m=m, c=c: e.tensor_tensor(
                     c1_ap(m, c), c1_ap(m, c), t_sg[1][:], ALU.add),
                 reads=[B_("sg1")], writes=[c1_buf(m)])

    b_c1bfc = [B_("c1bfc0"), B_("c1bfc1")]

    def c1bf_cast(c):
        al = [b_slot["s5"]] if c == 0 else []
        row_d = G * POS
        for half, (t_src, bsrc) in enumerate(((t_c1a, b_c1a),
                                              (t_c1b, b_c1b))):
            row_s = 4 * POS
            P.op("vector",
                 lambda e, half=half, t_src=t_src, c=c: e.tensor_copy(
                     bass.AP(t_c1bf, half * 4 * POS + c * CH,
                             [[row_d, 128], [POS, 4], [1, CH]]),
                     bass.AP(t_src, c * CH, [[row_s, 128], [POS, 4], [1, CH]])),
                 reads=[bsrc], writes=[b_c1bfc[c], b_c1bf] + al)

    def step1_f(c, ms):
        for m in ms:
            bank = (c * G + m) % 4
            P.op("tensor", gate_mms(t_s0, m, t_z1, c, bank,
                                    extra=(t_s3, t_c1bf)),
                 reads=[b_slot["s0"], b_slot["s3"], b_z[1][c], b_c1bfc[c]],
                 writes=[b_pmm[bank]])
            P.op("scalar",
                 lambda e, m=m, bank=bank: e.activation(
                     t_sg[0][:], pmm[bank][:, 0:CH], AF.Sigmoid,
                     bias=t_b["bf"][:, m:m + 1]),
                 reads=[b_pmm[bank], B_("b_bf")], writes=[B_("sg0")])
            P.op("vector",
                 lambda e, m=m, c=c: e.tensor_tensor(
                     c1_ap(m, c), c1_ap(m, c), t_sg[0][:], ALU.mult),
                 reads=[B_("sg0")], writes=[c1_buf(m)])

    def step1_ig(c, ms):
        for m in ms:
            bank_i = (c * G + m) % 2
            bank_g = 2 + (c * G + m) % 2
            P.op("tensor", gate_mms(t_s1, m, t_z1, c, bank_i,
                                    extra=(t_ui2, t_c1bf)),
                 reads=[b_slot["s1"], b_ui2, b_z[1][c], b_c1bfc[c]],
                 writes=[b_pmm[bank_i]])
            P.op("tensor", gate_mms(t_s2, m, t_z1, c, bank_g,
                                    extra=(t_uc2, t_c1bf)),
                 reads=[b_slot["s2"], b_uc2, b_z[1][c], b_c1bfc[c]],
                 writes=[b_pmm[bank_g]])
            P.op("scalar",
                 lambda e, m=m, bank_i=bank_i: e.activation(
                     t_sg[1][:], pmm[bank_i][:, 0:CH], AF.Sigmoid,
                     bias=t_b["bi"][:, m:m + 1]),
                 reads=[b_pmm[bank_i], B_("b_bi")], writes=[B_("sg1")])
            P.op("scalar",
                 lambda e, m=m, bank_g=bank_g: e.activation(
                     t_sg[2][:], pmm[bank_g][:, 0:CH], AF.Tanh,
                     bias=t_b["bc"][:, m:m + 1]),
                 reads=[b_pmm[bank_g], B_("b_bc")], writes=[B_("sg2")])
            P.op("vector",
                 lambda e: e.tensor_tensor(t_sg[1][:], t_sg[1][:], t_sg[2][:],
                                           ALU.mult),
                 reads=[B_("sg2")], writes=[B_("sg1")])
            P.op("vector",
                 lambda e, m=m, c=c: e.tensor_tensor(
                     c1_ap(m, c), c1_ap(m, c), t_sg[1][:], ALU.add),
                 reads=[B_("sg1")], writes=[c1_buf(m)])


    B_chunk_p1(0, 0, (t_uchE, t_vchE2, B_("uchE"), B_("vchE2")))
    rank1_mm("uc", 2 * G)
    rank1_fin()
    dbg_dump("b0f", t_b0["f"][:], [B_("b0f")])
    B_chunk_p2(0, 0)
    x_combine(1)
    rs01 = uv_restream(0, 1)
    step0_f(0, range(0, 5))
    B_chunk_p1(0, 1, rs01)
    step0_f(0, range(5, 8))
    step0_ig(0, range(0, 3))
    B_chunk_p2(0, 1)
    rs10 = uv_restream(1, 0)
    step0_ig(0, range(3, 5))
    B_chunk_p1(1, 0, rs10)
    step0_ig(0, range(5, 8))
    step0_f(1, range(0, 2))
    B_chunk_p2(1, 0)
    rs11 = uv_restream(1, 1)
    step0_f(1, range(2, 8))
    step0_ig(1, range(0, 1))
    B_chunk_p1(1, 1, rs11)
    step0_ig(1, range(1, 5))
    B_chunk_p2(1, 1)
    step0_ig(1, range(5, 8))
    c1bf_cast(0)
    step1_f(0, range(0, 4))
    dbg_dump("z0", t_z0[:], b_z[0])
    dbg_dump("z1", t_z1[:], b_z[1])

    dbg_dump("c1", t_c1bf[:], [b_c1bf])
    dma("scalar", "SQ",
        t_ui2.ap().rearrange("p (g d) -> p g d", g=G),
        wT["ui"].rearrange("(g p) d -> p g d", p=128),
        writes=[b_ui2, b_uch])
    dma("scalar", "SQ",
        t_uc2.ap().rearrange("p (g d) -> p g d", g=G),
        wT["uc"].rearrange("(g p) d -> p g d", p=128),
        writes=[b_uc2, b_vch] + list(b_q))

    step1_f(0, range(4, G))
    c1bf_cast(1)
    step1_f(1, range(G))
    load_w("wo")
    load_w("uo")
    step1_ig(0, range(G))
    step1_ig(1, range(G))

    P.op("vector", lambda e: e.tensor_copy(t_c1bf[:, 0:4 * POS], t_c1a[:]),
         reads=[b_c1a], writes=[b_c1bf])
    P.op("vector", lambda e: e.tensor_copy(t_c1bf[:, 4 * POS:], t_c1b[:]),
         reads=[b_c1b], writes=[b_c1bf])
    dbg_dump("c2", t_c1bf[:], [b_c1bf])

    b_hD = [B_("hD0"), B_("hD1")]
    b_x2r = [B_("x2r0"), B_("x2r1")]
    b_hbfD = B_("hbfD")
    b_sqD = B_("sqD")
    hD_t = [t_hD, t_hD2]
    x2r_t = [t_x2r, t_x2r2]
    hD_alias = [[b_c1a], [b_c1b]]
    x2r_alias = [[b_ui2], [b_uc2]]
    st_toks = []
    chunks = [(b, c) for b in range(B) for c in range(NCH)]

    def D_compute(idx, ms):
        b, c = chunks[idx]
        par = idx % 2
        thD, tx2 = hD_t[par], x2r_t[par]
        bhD, bx2 = b_hD[par], b_x2r[par]
        if ms[0] == 0:
            dma("sync", "SY", tx2[:], x2_sp[b, c],
                writes=[bx2] + x2r_alias[par], deps=[x2_tok[(b, c)]])
        for m in ms:
            bank = (c * G + m) % 4
            P.op("tensor", gate_mms(t_s0, m, z_t[b], c, bank,
                                    extra=(t_s3, t_c1bf)),
                 reads=[b_slot["s0"], b_slot["s3"], b_z[b][c], b_c1bf],
                 writes=[b_pmm[bank]])
            P.op("scalar",
                 lambda e, m=m, bank=bank, thD=thD: e.activation(
                     thD[:, m * CH:(m + 1) * CH], pmm[bank][:, 0:CH],
                     AF.Tanh, bias=t_b["bo"][:, m:m + 1]),
                 reads=[b_pmm[bank], B_("b_bo")],
                 writes=[bhD] + hD_alias[par])
        if b == 0 and c == 0:
            dbg_dump("hD00", thD[:], [bhD])

    def D_ln_stats(idx):
        b, c = chunks[idx]
        par = idx % 2
        thD = hD_t[par]
        bhD = b_hD[par]
        ln_stats(thD, bhD, t_hbfD, alias=[b_hbfD, b_abf], eng="vector")
        rows_block(t_ceps[0:1, :], False, True)

    def D_ln_tail(idx):
        b, c = chunks[idx]
        par = idx % 2
        thD, tx2 = hD_t[par], x2r_t[par]
        bhD, bx2 = b_hD[par], b_x2r[par]
        P.op("tensor", bcast(r_r, 0), reads=[b_rows, b_consts],
             writes=[b_pbc[0]])
        P.op("tensor", bcast(r_mr, 1), reads=[b_rows, b_consts],
             writes=[b_pbc[1]])
        P.op("vector",
             lambda e, thD=thD: e.tensor_tensor(g3(thD), g3(thD), bc3(pbc[0]),
                                                ALU.mult),
             reads=[b_pbc[0]], writes=[bhD])
        P.op("vector",
             lambda e, thD=thD: e.tensor_tensor(g3(thD), g3(thD), bc3(pbc[1]),
                                                ALU.subtract),
             reads=[b_pbc[1]], writes=[bhD])
        P.op("vector",
             lambda e, thD=thD, tx2=tx2: e.tensor_tensor(
                 tx2[:], thD[:], tx2[:], ALU.add),
             reads=[bhD], writes=[bx2])
        st_toks.append(
            dma("scalar", "SQ",
                outT[b].rearrange("(g p) c -> p g c", p=128)[:, :, c * CH:(c + 1) * CH],
                tx2.ap().rearrange("p (g c) -> p g c", g=G),
                reads=[bx2]))

    D_compute(0, range(G))
    for idx in range(1, len(chunks)):
        D_compute(idx, range(0, 6))
        D_ln_stats(idx - 1)
        D_compute(idx, range(6, 8))
        D_ln_tail(idx - 1)
    D_ln_stats(len(chunks) - 1)
    D_ln_tail(len(chunks) - 1)

    P.seq["scalar"].append((
        [(s, v) for (_e, s, v) in st_toks], lambda e: None, None, 0))

    nc._rec = []
    nc._plan = P
    with nc.Block() as block:

        @block.tensor
        def _(eng):
            P.emit("tensor", eng, sems, nc, nc._rec)

        @block.vector
        def _(eng):
            P.emit("vector", eng, sems, nc, nc._rec)

        @block.scalar
        def _(eng):
            P.emit("scalar", eng, sems, nc, nc._rec)

        @block.gpsimd
        def _(eng):
            P.emit("gpsimd", eng, sems, nc, nc._rec)

        @block.sync
        def _(eng):
            P.emit("sync", eng, sems, nc, nc._rec)

    return nc


_NC_CACHE = {}


def kernel(L0_attn_x, L0_attn_w_qkva, L0_attn_gamma,
           L0_ffn_wf, L0_ffn_uf, L0_ffn_bf, L0_ffn_wi, L0_ffn_ui, L0_ffn_bi,
           L0_ffn_wc, L0_ffn_uc, L0_ffn_bc, L0_ffn_wo, L0_ffn_uo, L0_ffn_bo,
           L0_ffn_in_c):
    x = np.asarray(L0_attn_x, dtype=np.float32)
    SEQ = x.shape[1]
    PC = SEQ // NCORES

    def feat(v):
        return np.ascontiguousarray(np.asarray(v, np.float32).reshape(G, 128).T)

    def tbf(v):
        b = np.ascontiguousarray(v, np.float32)
        return b.view(np.uint16).reshape(*b.shape, 2)[..., 1].copy().view(
            ml_dtypes.bfloat16)

    shared = {
        "wqkT": tbf(np.asarray(L0_attn_w_qkva, np.float32).T),
        "gamma": feat(L0_attn_gamma),
        "wfT": tbf(np.asarray(L0_ffn_wf, np.float32).T),
        "wiT": tbf(np.asarray(L0_ffn_wi, np.float32).T),
        "wcT": tbf(np.asarray(L0_ffn_wc, np.float32).T),
        "woT": tbf(np.asarray(L0_ffn_wo, np.float32).T),
        "ufT": tbf(np.asarray(L0_ffn_uf, np.float32).T),
        "uiT": tbf(np.asarray(L0_ffn_ui, np.float32).T),
        "ucT": tbf(np.asarray(L0_ffn_uc, np.float32).T),
        "uoT": tbf(np.asarray(L0_ffn_uo, np.float32).T),
        "bf_b": feat(L0_ffn_bf), "bi_b": feat(L0_ffn_bi),
        "bc_b": feat(L0_ffn_bc), "bo_b": feat(L0_ffn_bo),
        "inc_b": feat(L0_ffn_in_c),
    }
    in_maps = []
    for k in range(NCORES):
        m = dict(shared)
        m["x0T"] = np.ascontiguousarray(x[0, k * PC:(k + 1) * PC, :].T)
        m["x1T"] = np.ascontiguousarray(x[1, k * PC:(k + 1) * PC, :].T)
        m["x0Tb"] = tbf(m["x0T"])
        m["x1Tb"] = tbf(m["x1T"])
        mlt = np.zeros((128, G), np.float32)
        mlt[:, :k] = 1.0
        m["mask_lt"] = mlt
        m["mask_c"] = np.ascontiguousarray(1.0 - mlt)
        in_maps.append(m)

    if "nc" not in _NC_CACHE:
        _NC_CACHE["nc"] = build_nc()
    res = run_bass_kernel_spmd(_NC_CACHE["nc"], in_maps,
                               core_ids=list(range(NCORES)))
    _NC_CACHE["res"] = res

    out = np.empty((B, SEQ, D), np.float32)
    for k in range(NCORES):
        ot = res.results[k]["outT"]
        out[0, k * PC:(k + 1) * PC, :] = ot[0].T
        out[1, k * PC:(k + 1) * PC, :] = ot[1].T
    return out


if __name__ == "__main__":
    build_nc()
    print("built ok")


# version 39
# speedup vs baseline: 1.0150x; 1.0070x over previous
import numpy as np
import ml_dtypes
import concourse.bass as bass
import concourse.mybir as mybir
from concourse.bass_utils import run_bass_kernel_spmd

F32 = mybir.dt.float32
BF16 = mybir.dt.bfloat16
AF = mybir.ActivationFunctionType
ALU = mybir.AluOpType

NCORES = 8
D = 1024
G = 8
POS = 1024
B = 2
CH = 512
NCH = POS // CH
KB = 1024

ENG_SEM = {"tensor": "T", "vector": "V", "scalar": "S", "gpsimd": "G", "sync": "Y"}


class Buf:
    __slots__ = ("name", "w", "r")

    def __init__(self, name):
        self.name = name
        self.w = None
        self.r = []


class Plan:
    def __init__(self):
        self.seq = {e: [] for e in ENG_SEM}
        self.cnt = {}
        self.rings = {}

    def ring(self, name, n):
        self.rings[name] = {"sems": [f"{name}_{i}" for i in range(n)], "i": 0,
                            "last": [None] * n}

    def op(self, eng, fn, reads=(), writes=(), deps=(), queue=None, sem=None,
           inc=None):
        dep_list = []
        for b in reads:
            if b.w is not None:
                dep_list.append(b.w)
        for b in writes:
            if b.w is not None:
                dep_list.append(b.w)
            dep_list.extend(b.r)
        dep_list.extend(t for t in deps if t is not None)
        slot = None
        if queue is not None:
            rg = self.rings[queue]
            slot = rg["i"]
            rg["i"] = (slot + 1) % len(rg["sems"])
            sem = rg["sems"][slot]
            inc = 16
            if rg["last"][slot] is not None:
                dep_list.append(rg["last"][slot])
            async_tok = True
        elif sem is not None:
            async_tok = True
        else:
            sem = ENG_SEM[eng]
            inc = 1
            async_tok = False
        waits = []
        for (teng, tsem, tval) in dep_list:
            if teng == eng:
                continue
            waits.append((tsem, tval))
        self.cnt[sem] = self.cnt.get(sem, 0) + inc
        tok = (None if async_tok else eng, sem, self.cnt[sem])
        self.seq[eng].append((waits, fn, sem, inc))
        if queue is not None:
            self.rings[queue]["last"][slot] = tok
        for b in reads:
            b.r.append(tok)
        for b in writes:
            b.w = tok
            b.r = []
        return tok

    def emit(self, eng_name, eng, sems, nc=None, rec=None):
        hw = {}
        for oi, (waits, fn, sem, inc) in enumerate(self.seq[eng_name]):
            emitted = []
            for (sname, val) in waits:
                if hw.get(sname, 0) >= val:
                    continue
                eng.wait_ge(sems[sname], val)
                hw[sname] = val
                emitted.append((sname, val))
            n0 = nc.n_instructions() if nc is not None else 0
            inst = fn(eng)
            if inst is not None and sem is not None:
                inst.then_inc(sems[sem], inc)
            if rec is not None:
                rec.append((eng_name, oi, nc.n_instructions() - n0,
                            emitted, sem, inc))


_DEBUG = False
_NO_CC = False


def build_nc(dbg=False):
    nc = bass.Bass("TRN2", target_bir_lowering=False)

    xT = [nc.dram_tensor(f"x{b}T", [D, POS], F32, kind="ExternalInput").ap()
          for b in range(B)]
    xTb = [nc.dram_tensor(f"x{b}Tb", [D, POS], BF16, kind="ExternalInput").ap()
           for b in range(B)]
    wqkT_d = nc.dram_tensor("wqkT", [D, 3 * D], BF16, kind="ExternalInput").ap()
    gamma_in = nc.dram_tensor("gamma", [128, G], F32, kind="ExternalInput").ap()
    wT = {}
    for nm in ("wf", "wi", "wc", "wo", "uf", "ui", "uc", "uo"):
        wT[nm] = nc.dram_tensor(nm + "T", [D, D], BF16, kind="ExternalInput").ap()
    bias_in = {}
    for nm in ("bf", "bi", "bc", "bo", "inc"):
        bias_in[nm] = nc.dram_tensor(nm + "_b", [128, G], F32,
                                     kind="ExternalInput").ap()
    mlt_in = nc.dram_tensor("mask_lt", [128, G], F32, kind="ExternalInput").ap()
    mlc_in = nc.dram_tensor("mask_c", [128, G], F32, kind="ExternalInput").ap()
    outT = nc.dram_tensor("outT", [B, D, POS], F32, kind="ExternalOutput").ap()

    _spk = "ExternalOutput" if dbg else "Internal"
    uv_sp = nc.dram_tensor("uv_sp", [B, 2, NCH, 128, G * CH], BF16,
                           kind=_spk).ap()
    x2_sp = nc.dram_tensor("x2_sp", [B, NCH, 128, G * CH], F32, kind=_spk).ap()
    cc_in = [nc.dram_tensor(f"cc_in{b}", [128, 16], F32) for b in range(B)]
    cc_out = [nc.dram_tensor(f"cc_out{b}", [NCORES, 128, 16], F32)
              for b in range(B)]
    cc_all_in = nc.dram_tensor("cc_all_in", [128, 32], F32)
    cc_all_out = nc.dram_tensor("cc_all_out", [NCORES, 128, 32], F32)
    cc_wrm = nc.dram_tensor("cc_wrm", [128, 16], F32)
    cc_wrm_o = nc.dram_tensor("cc_wrm_o", [NCORES, 128, 16], F32)

    SB_BASE = nc.sbuf_top - nc.sbuf_bytes_remaining
    SB_BASE = (SB_BASE + 127) // 128 * 128
    assert nc.sbuf_bytes_remaining >= 205 * KB, nc.sbuf_bytes_remaining

    def at(name, shape, dtype, off):
        return nc.alloc_sbuf_tensor_at(name, shape, dtype, offset=SB_BASE + off)

    o = 0

    def bump(name, shape, dtype, align=32):
        nonlocal o
        es = mybir.dt.size(dtype)
        o = (o + align - 1) // align * align
        h = at(name, shape, dtype, o)
        n = 1
        for s in shape[1:]:
            n *= s
        o += n * es
        return h

    t_gamma = bump("gammat", [128, G], F32)
    t_b = {nm: bump(nm + "t", [128, G], F32)
           for nm in ("bf", "bi", "bc", "bo", "inc")}
    t_incbf = bump("incbf", [128, G], BF16)
    t_mlt = bump("mltt", [128, G], F32)
    t_mlc = bump("mlct", [128, G], F32)
    t_cy = bump("cyt", [128, 2 * G], F32)
    t_cp = bump("cpt", [128, 2 * G], F32)
    t_cin = bump("cint", [128, 2 * G], F32)
    t_cst = bump("cstt", [128, 2 * G], F32)
    t_tmpA = bump("tmpAt", [128, 2 * G], F32)
    t_tmpY = bump("tmpYt", [128, 2 * G], F32)
    t_vr = {nm: bump("vr" + nm, [128, G], F32) for nm in ("f", "i", "c")}
    t_b0 = {nm: bump("b0" + nm, [128, G], F32) for nm in ("f", "i", "c")}
    t_summ = bump("summt", [128, 32], F32)
    t_gath = bump("gatht", [128, 256], F32)
    t_onescol = bump("onescol", [128, 1], BF16)
    t_onesrow = bump("onesrow", [1, 128], F32)
    t_ones512 = bump("ones512", [128, CH], F32)
    t_ceps = bump("cepst", [1, 1], F32)
    t_cmax = bump("cmaxt", [1, 1], F32)
    r_m = bump("r_m", [1, CH], F32)
    r_v = bump("r_v", [1, CH], F32)
    r_r = bump("r_r", [1, CH], F32)
    r_mr = bump("r_mr", [1, CH], F32)
    t_sg = [bump(f"sg{i}", [128, CH], F32) for i in range(3)]
    t_cbA = bump("cbA", [128, 72], F32)
    t_cbY = bump("cbY", [128, 72], F32)
    t_cbS = bump("cbS", [128, 72], F32)
    assert o <= 20 * KB, f"small region overflow: {o}"

    t_wqk = at("wqk", [128, G * 3 * D], BF16, 20 * KB)
    t_s0 = at("s0", [128, G * D], BF16, 20 * KB)
    t_s1 = at("s1", [128, G * D], BF16, 36 * KB)
    t_sqB = at("sqB", [128, G * CH], BF16, 36 * KB)
    t_s2 = at("s2", [128, G * D], BF16, 52 * KB)
    t_xbf = at("xbf", [128, G * POS], BF16, 68 * KB)
    t_s3 = at("s3", [128, G * D], BF16, 68 * KB)
    t_hbf0 = at("hbf0", [128, G * POS], BF16, 84 * KB)
    t_s4 = at("s4", [128, G * D], BF16, 84 * KB)
    t_vchE2 = at("vchE2", [128, G * CH], BF16, 92 * KB)
    t_hD = at("hD", [128, G * CH], F32, 84 * KB)
    t_hbf1 = at("hbf1", [128, G * POS], BF16, 100 * KB)
    t_s5 = at("s5", [128, G * D], BF16, 100 * KB)
    t_x2r = at("x2r", [128, G * CH], F32, 100 * KB)
    t_sqA = at("sqA", [128, G * CH], BF16, 116 * KB)
    t_xm = at("xmscr", [128, G * (CH // 2)], F32, 116 * KB)
    t_uchE = at("uchE", [128, G * CH], BF16, 116 * KB)
    t_abf = at("abf", [128, G * CH], BF16, 116 * KB)
    t_hbfD = at("hbfD", [128, G * CH], BF16, 116 * KB)
    t_y = at("ych", [128, G * CH], F32, 124 * KB)
    t_c1a = at("c1a", [128, 4 * POS], F32, 124 * KB)
    t_hD = at("hD", [128, G * CH], F32, 124 * KB)
    t_P = at("Pch", [128, G * CH], F32, 140 * KB)
    t_uch = at("uch", [128, G * CH], F32, 140 * KB)
    t_uchb = at("uchb", [128, G * CH], BF16, 140 * KB)
    t_vchb = at("vchb", [128, G * CH], BF16, 148 * KB)
    t_ui2 = at("ui2", [128, G * D], BF16, 140 * KB)
    t_x2r = at("x2r", [128, G * CH], F32, 140 * KB)
    t_q = at("qch", [128, G * CH], F32, 156 * KB)
    t_vch = at("vch", [128, G * CH], F32, 156 * KB)
    t_uc2 = at("uc2", [128, G * D], BF16, 156 * KB)
    t_x2r2 = at("x2r2", [128, G * CH], F32, 156 * KB)
    t_kv = at("kvch", [128, G * CH], F32, 172 * KB)
    t_z1 = at("z1", [128, G * POS], BF16, 172 * KB)
    t_a = at("ach", [128, G * CH], F32, 188 * KB)
    t_ubf = at("ubf", [128, G * CH], BF16, 188 * KB)
    t_vbf = at("vbf", [128, G * CH], BF16, 196 * KB)
    t_z0 = at("z0", [128, G * POS], BF16, 188 * KB)
    t_c1b = at("c1b", [128, 4 * POS], F32, 84 * KB)
    t_hD2 = at("hD2", [128, G * CH], F32, 84 * KB)
    t_c1bf = at("c1bf", [128, G * POS], BF16, 100 * KB)

    pmm = [nc.alloc_psum_tensor(f"pmm{i}", [128, 512], F32) for i in range(4)]
    pstat = [nc.alloc_psum_tensor(f"pstat{i}", [1, 512], F32) for i in range(2)]
    pbc = [nc.alloc_psum_tensor(f"pbc{i}", [128, 512], F32) for i in range(2)]

    sems = {}
    for s in ("T", "V", "S", "G", "Y", "CC"):
        sems[s] = nc.alloc_semaphore(s)
    P = Plan()
    P.ring("SY", 6)
    P.ring("SQ", 4)
    P.ring("GQ", 2)
    for rg in ("SY", "SQ", "GQ"):
        for s in P.rings[rg]["sems"]:
            sems[s] = nc.alloc_semaphore(s)

    bufs = {}

    def B_(name):
        if name not in bufs:
            bufs[name] = Buf(name)
        return bufs[name]

    def dma(eng, queue, dst_ap, src_ap, reads=(), writes=(), deps=()):
        return P.op(eng, lambda e: e.dma_start(dst_ap, src_ap),
                    reads=reads, writes=writes, deps=deps, queue=queue)

    def vguard(bufs_to_stamp):
        P.op("vector", lambda e: e.memset(t_sg[2][:], 0.0),
             reads=[], writes=[B_("sg2")])
        P.op("vector", lambda e: e.memset(t_sg[2][:], 0.0),
             reads=[], writes=[B_("sg2")] + list(bufs_to_stamp))

    dbg_names = []

    def dbg_dump(name, src_ap, reads):
        if not dbg:
            return
        shp = list(src_ap.shape)
        dt = src_ap.dtype
        d = nc.dram_tensor("dbg_" + name, shp, dt, kind="ExternalOutput").ap()
        dbg_names.append("dbg_" + name)
        dma("sync", "SY", d, src_ap, reads=list(reads))

    def g3(t, c0=0, ncc=CH, width=CH, ng=G):
        ap = t.ap()
        row = ap.ap[0][0]
        return bass.AP(t, c0, [[row, 128], [width, ng], [1, ncc]])

    def bc3(pt, c0=0, ncc=CH, ng=G):
        ap = pt.ap()
        row = ap.ap[0][0]
        return bass.AP(pt, c0, [[row, 128], [0, ng], [1, ncc]])

    b_pstat = [B_("pstat0"), B_("pstat1")]
    b_pbc = [B_("pbc0"), B_("pbc1")]
    b_pmm = [B_(f"pmm{i}") for i in range(4)]
    b_rows = B_("rows")
    b_consts = B_("consts")
    b_params = B_("params")

    b_xbf = B_("xbf")
    b_wqk = B_("wqk")
    b_wqk3 = [B_("wqk3a"), B_("wqk3b"), B_("wqk3c")]
    dma("scalar", "SQ",
        t_xbf.ap().rearrange("p (g c) -> p g c", g=G),
        xTb[0].rearrange("(g p) c -> p g c", p=128),
        writes=[b_xbf])
    dma("scalar", "SQ",
        t_wqk.ap().rearrange("p (g o) -> p g o", g=G),
        wqkT_d.rearrange("(g p) o -> p g o", p=128),
        writes=[b_wqk] + b_wqk3)
    dma("sync", "SY", t_gamma[:], gamma_in, writes=[b_params])
    for nm in ("bf", "bi", "bc", "bo", "inc"):
        dma("sync", "SY", t_b[nm][:], bias_in[nm], writes=[B_("b_" + nm)])
    dma("sync", "SY", t_mlt[:], mlt_in, writes=[B_("masks")])
    dma("sync", "SY", t_mlc[:], mlc_in, writes=[B_("masks")])

    def consts(e):
        e.memset(t_onescol[:], 1.0 / 1024.0)
        e.memset(t_onesrow[:], 1.0)
        e.memset(t_ones512[:], 1.0)
        e.memset(t_ceps[:], 1e-5)
        e.memset(t_cmax[:], 9.5e-28)
        e.memset(t_cy[:], 0.0)
        e.memset(t_cp[:], 1.0)
        return e.memset(t_cst[:], 0.0)
    P.op("vector", consts, writes=[b_consts, B_("cy"), B_("cp"), B_("cst")])

    P.op("vector", lambda e: e.tensor_copy(t_incbf[:], t_b["inc"][:]),
         reads=[B_("b_inc")], writes=[B_("incbf")])


    def rows_block(eps_ap, floor, want_mr):
        er = "vector"
        P.op(er, lambda e: e.tensor_copy(r_m[:], pstat[0][0:1, :]),
             reads=[b_pstat[0]], writes=[b_rows])

        def f_var(e):
            e.tensor_tensor(r_v[:], r_m[:], r_m[:], ALU.mult)
            return e.tensor_tensor(r_v[:], pstat[1][0:1, :], r_v[:], ALU.subtract)
        P.op(er, f_var, reads=[b_pstat[1]], writes=[b_rows])
        if floor:
            P.op("vector",
                 lambda e: e.tensor_scalar(r_v[:], r_v[:], t_cmax[0:1, :], None,
                                           ALU.max),
                 reads=[b_consts], writes=[b_rows])

        def f_ln(e):
            if eps_ap is None:
                e.activation(r_r[:], r_v[:], AF.Ln)
            else:
                e.activation(r_r[:], r_v[:], AF.Ln, bias=eps_ap)
            return e.activation(r_r[:], r_r[:], AF.Exp, scale=-0.5)
        P.op("scalar", f_ln, reads=[b_rows, b_consts], writes=[b_rows])
        if want_mr:
            P.op(er, lambda e: e.tensor_tensor(r_mr[:], r_m[:], r_r[:],
                                               ALU.mult),
                 reads=[b_rows], writes=[b_rows])

    def bcast(row_tile, which):
        def f(e):
            return e.matmul(pbc[which][:, :], t_onesrow[:], row_tile[:],
                            start=True, stop=True)
        return f

    def stat_sum(src_tile, src_buf):
        def fsum(e):
            ins = None
            for g in range(G):
                ins = e.matmul(pstat[0][0:1, :], t_onescol[:],
                               src_tile[:, g * CH:(g + 1) * CH],
                               start=(g == 0), stop=(g == G - 1))
            return ins
        P.op("tensor", fsum, reads=[src_buf, b_consts], writes=[b_pstat[0]])

    def stat_sq(sq_tile, sq_buf):
        def fsq(e):
            ins = None
            for g in range(G):
                ins = e.matmul(pstat[1][0:1, :], t_onescol[:],
                               sq_tile[:, g * CH:(g + 1) * CH],
                               start=(g == 0), stop=(g == G - 1))
            return ins
        P.op("tensor", fsq, reads=[sq_buf, b_consts], writes=[b_pstat[1]])

    b_abfh = [B_("abfh0"), B_("abfh1")]

    def stat_half(bank, tile, b_half, h, start):
        lo = h * 4 * CH

        def f(e, lo=lo, bank=bank, start=start):
            ins = None
            for g in range(4):
                ins = e.matmul(pstat[bank][0:1, :], t_onescol[:],
                               tile[:, lo + g * CH: lo + (g + 1) * CH],
                               start=(start and g == 0),
                               stop=(not start and g == 3))
            return ins
        P.op("tensor", f, reads=[b_half, b_consts], writes=[b_pstat[bank]])

    def ln_stats(src_tile, b_src, abf_tile, alias=(), eng="scalar",
                 b_src_h=None):
        H = 4 * CH

        def cast_f(e, h):
            if eng == "scalar":
                return e.activation(abf_tile[:, h * H:(h + 1) * H],
                                    src_tile[:, h * H:(h + 1) * H], AF.Copy)
            return e.tensor_copy(abf_tile[:, h * H:(h + 1) * H],
                                 src_tile[:, h * H:(h + 1) * H])

        def sq_f(e, h):
            if eng == "scalar":
                return e.activation(abf_tile[:, h * H:(h + 1) * H],
                                    src_tile[:, h * H:(h + 1) * H], AF.Square)
            return e.tensor_tensor(abf_tile[:, h * H:(h + 1) * H],
                                   src_tile[:, h * H:(h + 1) * H],
                                   src_tile[:, h * H:(h + 1) * H], ALU.mult)
        for h in range(2):
            P.op(eng, lambda e, h=h: cast_f(e, h),
                 reads=[b_src_h[h] if b_src_h else b_src],
                 writes=[b_abfh[h]] + list(alias))
        for h in range(2):
            stat_half(0, abf_tile, b_abfh[h], h, start=(h == 0))
        for h in range(2):
            P.op(eng, lambda e, h=h: sq_f(e, h),
                 reads=[b_src_h[h] if b_src_h else b_src],
                 writes=[b_abfh[h]])
        for h in range(2):
            stat_half(1, abf_tile, b_abfh[h], h, start=(h == 0))

    def stat_pair(src_tile, src_buf, src_off, src_w, sq_tile, sq_buf):
        def fsum(e):
            ins = None
            for g in range(G):
                ins = e.matmul(pstat[0][0:1, :], t_onescol[:],
                               src_tile[:, g * src_w + src_off:
                                        g * src_w + src_off + CH],
                               start=(g == 0), stop=(g == G - 1))
            return ins
        P.op("tensor", fsum, reads=[src_buf, b_consts], writes=[b_pstat[0]])

        def fsq(e):
            ins = None
            for g in range(G):
                ins = e.matmul(pstat[1][0:1, :], t_onescol[:],
                               sq_tile[:, g * CH:(g + 1) * CH],
                               start=(g == 0), stop=(g == G - 1))
            return ins
        P.op("tensor", fsq, reads=[sq_buf, b_consts], writes=[b_pstat[1]])

    def rank1_mm(nm, off):
        def f(e, nm=nm, off=off):
            ins = None
            t = wslot[nm]
            for m in range(G):
                for g in range(G):
                    ins = e.matmul(pmm[3][:, off + m:off + m + 1],
                                   t[:, g * D + m * 128: g * D + (m + 1) * 128],
                                   t_incbf[:, g:g + 1],
                                   start=(g == 0), stop=(g == G - 1))
            return ins
        P.op("tensor", f, reads=[b_slot[slot_of[nm]], B_("incbf")],
             writes=[b_pmm[3]])

    def rank1_fin():
        for off, gate, bname in ((0, "f", "bf"), (G, "i", "bi"),
                                 (2 * G, "c", "bc")):
            P.op("vector",
                 lambda e, off=off, gate=gate, bname=bname: e.tensor_tensor(
                     t_b0[gate][:], pmm[3][:, off:off + G], t_b[bname][:],
                     ALU.add),
                 reads=[b_pmm[3], B_("b_" + bname)], writes=[B_("b0" + gate)])
        vguard([B_("b0f"), B_("b0i"), B_("b0c")])


    b_sqA = B_("sqA")
    b_hbf = [B_("hbf0"), B_("hbf1")]
    b_q = [B_(f"qch{g}") for g in range(G)]
    b_kv = [B_(f"kvch{g}") for g in range(G)]
    b_a = [B_(f"ach{g}") for g in range(G)]
    b_y, b_P = B_("ych"), B_("Pch")
    b_ubf, b_vbf = B_("ubf"), B_("vbf")
    b_cy, b_cp = B_("cy"), B_("cp")
    hbf_t = [t_hbf0, t_hbf1]
    uv_tok = {}

    def ln1_block(b):
        for c in range(NCH):
            P.op("scalar",
                 lambda e, c=c: e.activation(
                     g3(t_sqA), g3(t_xbf, c0=c * CH, width=POS), AF.Square),
                 reads=[b_xbf], writes=[b_sqA])
            stat_pair(t_xbf, b_xbf, c * CH, POS, t_sqA, b_sqA)
            rows_block(None, True, False)
            P.op("tensor", bcast(r_m, 0), reads=[b_rows, b_consts],
                 writes=[b_pbc[0]])
            P.op("tensor", bcast(r_r, 1), reads=[b_rows, b_consts],
                 writes=[b_pbc[1]])
            for h2 in range(2):
                c0 = c * CH + h2 * (CH // 2)
                P.op("vector",
                     lambda e, c0=c0: e.tensor_tensor(
                         g3(t_xm, ncc=CH // 2, width=CH // 2),
                         g3(t_xbf, c0=c0, ncc=CH // 2, width=POS),
                         bc3(pbc[0], c0=(c0 % CH), ncc=CH // 2),
                         ALU.subtract),
                     reads=[b_xbf, b_pbc[0]], writes=[b_sqA])
                for g in range(G):
                    P.op("vector",
                         lambda e, b=b, c0=c0, g=g: e.scalar_tensor_tensor(
                             hbf_t[b][:, g * POS + c0:
                                      g * POS + c0 + CH // 2],
                             t_xm[:, g * (CH // 2):(g + 1) * (CH // 2)],
                             t_gamma[:, g:g + 1],
                             pbc[1][:, (c0 % CH):(c0 % CH) + CH // 2],
                             ALU.mult, ALU.mult),
                         reads=[b_sqA, b_pbc[1], b_params],
                         writes=[b_hbf[b]])
        dbg_dump(f"hbf{b}", hbf_t[b][:], [b_hbf[b]])

    def qkva_chunk(b, c):
        order = []
        for g in range(G):
            order.append((8 + g, "kv"))
            order.append((16 + g, "a"))
        for g in range(G):
            order.append((g, "q"))
        for k, (m, kind) in enumerate(order):
            bank = k % 4

            def fmm2(e, b=b, c=c, m=m, bank=bank):
                ins = None
                for g in range(G):
                    ins = e.matmul(
                        pmm[bank][:, 0:CH],
                        t_wqk[:, g * 3 * D + m * 128:
                              g * 3 * D + (m + 1) * 128],
                        hbf_t[b][:, g * POS + c * CH:
                                 g * POS + (c + 1) * CH],
                        start=(g == 0), stop=(g == G - 1))
                return ins
            P.op("tensor", fmm2, reads=[b_wqk, b_hbf[b]] + b_wqk3,
                 writes=[b_pmm[bank]])
            if kind == "q":
                P.op("scalar",
                     lambda e, m=m, bank=bank: e.copy(
                         t_q[:, m * CH:(m + 1) * CH], pmm[bank][:, 0:CH]),
                     reads=[b_pmm[bank]], writes=[b_q[m]])
            elif kind == "kv":
                P.op("scalar",
                     lambda e, m=m, bank=bank: e.copy(
                         t_kv[:, (m - 8) * CH:(m - 7) * CH],
                         pmm[bank][:, 0:CH]),
                     reads=[b_pmm[bank]], writes=[b_kv[m - 8]])
            else:
                P.op("scalar",
                     lambda e, m=m, bank=bank: e.activation(
                         t_a[:, (m - 16) * CH:(m - 15) * CH],
                         pmm[bank][:, 0:CH], AF.Sigmoid),
                     reads=[b_pmm[bank]],
                     writes=[b_a[m - 16],
                             b_ubf if (m - 16) < 4 else b_vbf])
        if b == 0 and dbg:
            dbg_dump(f"q0{c}", t_q[:], list(b_q))
            dbg_dump(f"kv0{c}", t_kv[:], list(b_kv))
            dbg_dump(f"a0{c}", t_a[:], list(b_a))
        for g in range(G):
            sl = slice(g * CH, (g + 1) * CH)
            cyc = t_cy[:, b * G + g: b * G + g + 1]
            P.op("vector",
                 lambda e, sl=sl, cyc=cyc: e.tensor_tensor_scan(
                     t_y[:, sl], t_a[:, sl], t_kv[:, sl], cyc,
                     ALU.mult, ALU.add),
                 reads=[b_a[g], b_kv[g], b_cy], writes=[b_y])
        for g in range(G):
            sl = slice(g * CH, (g + 1) * CH)
            cpc = t_cp[:, b * G + g: b * G + g + 1]
            P.op("vector",
                 lambda e, sl=sl, cpc=cpc: e.tensor_tensor_scan(
                     t_P[:, sl], t_a[:, sl], t_ones512[:], cpc,
                     ALU.mult, ALU.mult),
                 reads=[b_a[g], b_consts, b_cp], writes=[b_P])
        for g in range(G):
            sl = slice(g * CH, (g + 1) * CH)
            cyc = t_cy[:, b * G + g: b * G + g + 1]
            P.op("vector",
                 lambda e, sl=sl, cyc=cyc: e.tensor_copy(
                     cyc, t_y[:, sl.stop - 1: sl.stop]),
                 reads=[b_y], writes=[b_cy])
        for g in range(G):
            sl = slice(g * CH, (g + 1) * CH)
            cpc = t_cp[:, b * G + g: b * G + g + 1]
            P.op("vector",
                 lambda e, sl=sl, cpc=cpc: e.tensor_copy(
                     cpc, t_P[:, sl.stop - 1: sl.stop]),
                 reads=[b_P], writes=[b_cp])
        peng = "gpsimd" if (b, c) == (1, 1) else "vector"
        for g in range(G):
            sl = slice(g * CH, (g + 1) * CH)
            P.op(peng,
                 lambda e, sl=sl: e.tensor_tensor(
                     t_ubf[:, sl], t_y[:, sl], t_q[:, sl], ALU.mult),
                 reads=[b_y, b_q[g]],
                 writes=[b_ubf] + ([b_a[g2] for g2 in range(4)]
                                   if g == 0 else []))
        for g in range(G):
            sl = slice(g * CH, (g + 1) * CH)
            P.op(peng,
                 lambda e, sl=sl: e.tensor_tensor(
                     t_vbf[:, sl], t_P[:, sl], t_q[:, sl], ALU.mult),
                 reads=[b_P, b_q[g]],
                 writes=[b_vbf] + ([b_a[g2] for g2 in range(4, G)]
                                   if g == 0 else []))
        if b == 0 and dbg:
            dbg_dump(f"y0{c}", t_y[:], [b_y])
            dbg_dump(f"P0{c}", t_P[:], [b_P])
            dbg_dump(f"cyA{c}", t_cy[:], [b_cy])
        uv_tok[(b, 0, c)] = dma("sync", "SY", uv_sp[b, 0, c], t_ubf[:],
                                reads=[b_ubf])
        uv_tok[(b, 1, c)] = dma("sync", "SY", uv_sp[b, 1, c], t_vbf[:],
                                reads=[b_vbf])

    b_summ = [B_("summ0"), B_("summ1")]
    b_gath = [B_("gath0"), B_("gath1")]
    b_cin = [B_("cin0"), B_("cin1")]
    tok_cc = [None, None]

    def x_trigger(b):
        P.op("vector",
             lambda e, b=b: e.tensor_copy(
                 t_summ[:, b * 16: b * 16 + 8], t_cy[:, b * G:(b + 1) * G]),
             reads=[b_cy], writes=[b_summ[b]])
        P.op("vector",
             lambda e, b=b: e.tensor_copy(
                 t_summ[:, b * 16 + 8: b * 16 + 16],
                 t_cp[:, b * G:(b + 1) * G]),
             reads=[b_cp], writes=[b_summ[b]])
        vguard([b_summ[b]])
        tok_sm = dma("gpsimd", "GQ", cc_in[b].ap(),
                     t_summ[:, b * 16:b * 16 + 16], reads=[b_summ[b]])
        tok = P.op("gpsimd",
                   lambda e, b=b: e.collective_compute(
                       "AllGather", ALU.bypass,
                       replica_groups=[list(range(NCORES))],
                       ins=[cc_in[b].ap()], outs=[cc_out[b].ap()]),
                   deps=[tok_sm], sem="CC", inc=1)
        tok_cc[b] = tok
        dma("gpsimd", "GQ",
            t_gath[:, b * 128:(b + 1) * 128].rearrange("p (r c) -> p r c", r=8),
            cc_out[b].ap().rearrange("r p c -> p r c"),
            writes=[b_gath[b]], deps=[tok])

    def st3(t, off):
        ap = t.ap()
        row = ap.ap[0][0]
        return bass.AP(t, off, [[row, 128], [9, 8], [1, 1]])

    def g8(b, off):
        ap = t_gath.ap()
        row = ap.ap[0][0]
        return bass.AP(t_gath, b * 128 + off, [[row, 128], [1, 8], [1, 1]])

    def x_combine(b):
        P.op("vector", lambda e: e.memset(t_cbA[:], 0.0), writes=[B_("cbA")])
        P.op("vector", lambda e: e.memset(t_cbY[:], 0.0), writes=[B_("cbY")])
        for r in range(NCORES):
            P.op("vector",
                 lambda e, r=r, b=b: e.tensor_scalar(
                     st3(t_cbA, 1 + r), g8(b, r * 16 + 8),
                     t_mlt[:, r:r + 1], t_mlc[:, r:r + 1], ALU.mult, ALU.add),
                 reads=[b_gath[b], B_("masks")], writes=[B_("cbA")])
        for r in range(NCORES):
            P.op("vector",
                 lambda e, r=r, b=b: e.tensor_scalar(
                     st3(t_cbY, 1 + r), g8(b, r * 16),
                     t_mlt[:, r:r + 1], None, ALU.mult),
                 reads=[b_gath[b], B_("masks")], writes=[B_("cbY")])
        P.op("vector", lambda e: e.memset(t_sg[2][:, 0:512], 0.0),
             writes=[B_("sg2")])
        P.op("vector",
             lambda e: e.tensor_tensor_scan(t_cbS[:], t_cbA[:], t_cbY[:],
                                            0.0, ALU.mult, ALU.add),
             reads=[B_("cbA"), B_("cbY")], writes=[B_("cbS")])
        P.op("vector", lambda e: e.memset(t_sg[2][:, 0:512], 0.0),
             writes=[B_("sg2")])

        def stfin():
            ap = t_cbS.ap()
            row = ap.ap[0][0]
            return bass.AP(t_cbS, 8, [[row, 128], [9, 8]])
        P.op("vector",
             lambda e, b=b: e.tensor_copy(t_cin[:, b * G:(b + 1) * G], stfin()),
             reads=[B_("cbS")], writes=[b_cin[b]])
        vguard([b_cin[b]])
        if b == 1:
            dbg_dump("cy", t_cy[:], [b_cy])
            dbg_dump("cp", t_cp[:], [b_cp])
            dbg_dump("gath", t_gath[:], [b_gath[0], b_gath[1]])
            dbg_dump("cin", t_cin[:], [b_cin[0], b_cin[1]])

    wslot = {"wf": t_s0, "wi": t_s1, "wc": t_s2, "uf": t_s3, "ui": t_s4,
             "uc": t_s5, "wo": t_s0, "uo": t_s3}
    slot_of = {"wf": "s0", "wi": "s1", "wc": "s2", "uf": "s3", "ui": "s4",
               "uc": "s5", "wo": "s0", "uo": "s3"}
    b_slot = {s: B_("slot_" + s) for s in ("s0", "s1", "s2", "s3", "s4", "s5")}
    slot_alias = {"s0": [b_wqk3[0]],
                  "s1": [b_wqk3[1], B_("sqB")],
                  "s2": [b_wqk3[2]],
                  "s3": [b_xbf], "s4": [b_hbf[0]], "s5": [b_hbf[1]]}

    def load_w(nm):
        t = wslot[nm]
        s = slot_of[nm]
        return dma("scalar", "SQ",
                   t.ap().rearrange("p (g d) -> p g d", g=G),
                   wT[nm].rearrange("(g p) d -> p g d", p=128),
                   writes=[b_slot[s]] + slot_alias[s])

    ln1_block(0)
    dma("scalar", "SQ",
        t_xbf.ap().rearrange("p (g c) -> p g c", g=G),
        xTb[1].rearrange("(g p) c -> p g c", p=128),
        writes=[b_xbf])
    qkva_chunk(0, 0)
    ln1_block(1)
    load_w("uf")
    qkva_chunk(0, 1)
    load_w("ui")
    x_trigger(0)
    dma("sync", "SY", t_uchE[:], uv_sp[0, 0, 0],
        writes=[B_("uchE"), b_sqA], deps=[uv_tok[(0, 0, 0)]])
    qkva_chunk(1, 0)
    x_combine(0)
    qkva_chunk(1, 1)
    rank1_mm("uf", 0)
    rank1_mm("ui", G)
    dma("gpsimd", "GQ", t_vchE2[:], uv_sp[0, 1, 0],
        writes=[B_("vchE2"), b_slot["s4"]], deps=[uv_tok[(0, 1, 0)]])
    x_trigger(1)
    for _nm in ("uc", "wf", "wi", "wc"):
        load_w(_nm)

    b_uch, b_vch = B_("uch"), B_("vch")
    b_vchh = [B_("vchh0"), B_("vchh1")]
    b_abf = B_("abf")
    b_z = [[B_("z00"), B_("z01")], [B_("z10"), B_("z11")]]
    z_t = [t_z0, t_z1]
    b_c1a, b_c1b, b_c1bf = B_("c1a"), B_("c1b"), B_("c1bf")
    b_ui2, b_uc2 = B_("ui2"), B_("uc2")
    x2_tok = {}

    def c1_ap(m, c):
        t = t_c1a if m < 4 else t_c1b
        mm = m if m < 4 else m - 4
        return t[:, mm * POS + c * CH: mm * POS + (c + 1) * CH]

    def c1_buf(m):
        return b_c1a if m < 4 else b_c1b

    def uv_restream(b, c, early=False):
        if early:
            tu, tv = t_uchE, t_vchE
            bu, bv = B_("uchE"), B_("vchE")
            al = [b_slot["s1"]]
        else:
            tu, tv = t_uchb, t_vchb
            bu, bv = B_("uchb"), B_("vchb")
            al = [b_uch] + ([b_P] if (b, c) == (0, 1) else [])
        dma("sync", "SY", tu[:], uv_sp[b, 0, c],
            writes=[bu] + al, deps=[uv_tok[(b, 0, c)]])
        dma("sync", "SY", tv[:], uv_sp[b, 1, c],
            writes=[bv] + al, deps=[uv_tok[(b, 1, c)]])
        return tu, tv, bu, bv

    def B_chunk_p1(b, c, rs):
        tu, tv, bu, bv = rs
        for g in range(G):
            sl = slice(g * CH, (g + 1) * CH)
            P.op("vector",
                 lambda e, sl=sl, b=b, tu=tu, tv=tv, g=g:
                 e.scalar_tensor_tensor(
                     t_vch[:, sl], tv[:, sl],
                     t_cin[:, b * G + g: b * G + g + 1],
                     tu[:, sl], ALU.mult, ALU.add),
                 reads=[bu, bv, b_cin[b]],
                 writes=[b_vchh[0] if g < 4 else b_vchh[1]] +
                 (list(b_q) if g == 0 else ([b_vch] if g == G - 1 else [])))
        ln_stats(t_vch, b_vch, t_abf, alias=[b_abf, b_sqA, B_("uchE")],
                 b_src_h=b_vchh)
        rows_block(t_ceps[0:1, :], False, True)
        P.op("tensor", bcast(r_r, 0), reads=[b_rows, b_consts],
             writes=[b_pbc[0]])
        P.op("tensor", bcast(r_mr, 1), reads=[b_rows, b_consts],
             writes=[b_pbc[1]])
        P.op("vector",
             lambda e: e.tensor_tensor(g3(t_vch), g3(t_vch), bc3(pbc[0]),
                                       ALU.mult),
             reads=[b_pbc[0]], writes=[b_vch, b_vchh[0], b_vchh[1]])
        P.op("vector",
             lambda e: e.tensor_tensor(g3(t_vch), g3(t_vch), bc3(pbc[1]),
                                       ALU.subtract),
             reads=[b_pbc[1]], writes=[b_vch])
        xal = [b_P] if (b, c) == (0, 0) else []
        dma("sync", "SY",
            t_uch.ap().rearrange("p (g c) -> p g c", g=G),
            xT[b].rearrange("(g p) c -> p g c", p=128)[:, :, c * CH:(c + 1) * CH],
            writes=[b_uch, B_("uchb"), B_("vchb")] + xal)
        P.op("vector",
             lambda e: e.tensor_tensor(t_vch[:], t_vch[:], t_uch[:], ALU.add),
             reads=[b_uch], writes=[b_vch])
        x2_tok[(b, c)] = dma("scalar", "SQ", x2_sp[b, c], t_vch[:],
                             reads=[b_vch])

    def B_chunk_p2(b, c):
        ln_stats(t_vch, b_vch, t_abf, alias=[b_abf])
        rows_block(t_ceps[0:1, :], False, False)
        P.op("tensor", bcast(r_m, 0), reads=[b_rows, b_consts],
             writes=[b_pbc[0]])
        P.op("tensor", bcast(r_r, 1), reads=[b_rows, b_consts],
             writes=[b_pbc[1]])
        P.op("vector",
             lambda e: e.tensor_tensor(g3(t_uch), g3(t_vch), bc3(pbc[0]),
                                       ALU.subtract),
             reads=[b_vch, b_pbc[0]], writes=[b_uch])
        extra_w = (list(b_a) + [b_ubf, b_vbf]) if b == 0 else list(b_kv)
        P.op("vector",
             lambda e, b=b, c=c: e.tensor_tensor(
                 g3(z_t[b], c0=c * CH, width=POS), g3(t_uch), bc3(pbc[1]),
                 ALU.mult),
             reads=[b_uch, b_pbc[1]], writes=[b_z[b][c]] + extra_w)

    def gate_mms(wt, m, acts_tile, c, bank, extra=None):
        def f(e, wt=wt, m=m, c=c, bank=bank, extra=extra):
            ins = None
            n = G if extra is None else 2 * G
            k = 0
            for g in range(G):
                ins = e.matmul(pmm[bank][:, 0:CH],
                               wt[:, g * D + m * 128: g * D + (m + 1) * 128],
                               acts_tile[:, g * POS + c * CH:
                                         g * POS + (c + 1) * CH],
                               start=(k == 0), stop=(k == n - 1))
                k += 1
            if extra is not None:
                wt2, acts2 = extra
                for g in range(G):
                    ins = e.matmul(pmm[bank][:, 0:CH],
                                   wt2[:, g * D + m * 128: g * D + (m + 1) * 128],
                                   acts2[:, g * POS + c * CH:
                                         g * POS + (c + 1) * CH],
                                   start=False, stop=(k == n - 1))
                    k += 1
            return ins
        return f

    def step0_f(c, ms):
        for m in ms:
            bank = (c * G + m) % 4
            P.op("tensor", gate_mms(t_s0, m, t_z0, c, bank),
                 reads=[b_slot["s0"], b_z[0][c], B_("b0f")],
                 writes=[b_pmm[bank]])
            P.op("scalar",
                 lambda e, m=m, bank=bank: e.activation(
                     t_sg[0][:], pmm[bank][:, 0:CH], AF.Sigmoid,
                     bias=t_b0["f"][:, m:m + 1]),
                 reads=[b_pmm[bank], B_("b0f")], writes=[B_("sg0")])
            al = [b_y] if m < 4 else [b_slot["s4"], B_("vchE2")]
            P.op("vector",
                 lambda e, m=m, c=c: e.tensor_scalar_mul(
                     c1_ap(m, c), t_sg[0][:], t_b["inc"][:, m:m + 1]),
                 reads=[B_("sg0"), B_("b_inc")], writes=[c1_buf(m)] + al)

    def step0_ig(c, ms):
        for m in ms:
            bank_i = (c * G + m) % 2
            bank_g = 2 + (c * G + m) % 2
            P.op("tensor", gate_mms(t_s1, m, t_z0, c, bank_i),
                 reads=[b_slot["s1"], b_z[0][c], B_("b0i")],
                 writes=[b_pmm[bank_i]])
            P.op("tensor", gate_mms(t_s2, m, t_z0, c, bank_g),
                 reads=[b_slot["s2"], b_z[0][c], B_("b0c")],
                 writes=[b_pmm[bank_g]])
            P.op("scalar",
                 lambda e, m=m, bank_i=bank_i: e.activation(
                     t_sg[1][:], pmm[bank_i][:, 0:CH], AF.Sigmoid,
                     bias=t_b0["i"][:, m:m + 1]),
                 reads=[b_pmm[bank_i], B_("b0i")], writes=[B_("sg1")])
            P.op("scalar",
                 lambda e, m=m, bank_g=bank_g: e.activation(
                     t_sg[2][:], pmm[bank_g][:, 0:CH], AF.Tanh,
                     bias=t_b0["c"][:, m:m + 1]),
                 reads=[b_pmm[bank_g], B_("b0c")], writes=[B_("sg2")])
            P.op("vector",
                 lambda e: e.tensor_tensor(t_sg[1][:], t_sg[1][:], t_sg[2][:],
                                           ALU.mult),
                 reads=[B_("sg2")], writes=[B_("sg1")])
            P.op("vector",
                 lambda e, m=m, c=c: e.tensor_tensor(
                     c1_ap(m, c), c1_ap(m, c), t_sg[1][:], ALU.add),
                 reads=[B_("sg1")], writes=[c1_buf(m)])

    b_c1bfc = [B_("c1bfc0"), B_("c1bfc1")]

    def c1bf_cast(c):
        al = [b_slot["s5"]] if c == 0 else []
        row_d = G * POS
        for half, (t_src, bsrc) in enumerate(((t_c1a, b_c1a),
                                              (t_c1b, b_c1b))):
            row_s = 4 * POS
            P.op("vector",
                 lambda e, half=half, t_src=t_src, c=c: e.tensor_copy(
                     bass.AP(t_c1bf, half * 4 * POS + c * CH,
                             [[row_d, 128], [POS, 4], [1, CH]]),
                     bass.AP(t_src, c * CH, [[row_s, 128], [POS, 4], [1, CH]])),
                 reads=[bsrc], writes=[b_c1bfc[c], b_c1bf] + al)

    def step1_f(c, ms):
        for m in ms:
            bank = (c * G + m) % 4
            P.op("tensor", gate_mms(t_s0, m, t_z1, c, bank,
                                    extra=(t_s3, t_c1bf)),
                 reads=[b_slot["s0"], b_slot["s3"], b_z[1][c], b_c1bfc[c]],
                 writes=[b_pmm[bank]])
            P.op("scalar",
                 lambda e, m=m, bank=bank: e.activation(
                     t_sg[0][:], pmm[bank][:, 0:CH], AF.Sigmoid,
                     bias=t_b["bf"][:, m:m + 1]),
                 reads=[b_pmm[bank], B_("b_bf")], writes=[B_("sg0")])
            P.op("vector",
                 lambda e, m=m, c=c: e.tensor_tensor(
                     c1_ap(m, c), c1_ap(m, c), t_sg[0][:], ALU.mult),
                 reads=[B_("sg0")], writes=[c1_buf(m)])

    def step1_ig(c, ms):
        for m in ms:
            bank_i = (c * G + m) % 2
            bank_g = 2 + (c * G + m) % 2
            P.op("tensor", gate_mms(t_s1, m, t_z1, c, bank_i,
                                    extra=(t_ui2, t_c1bf)),
                 reads=[b_slot["s1"], b_ui2, b_z[1][c], b_c1bfc[c]],
                 writes=[b_pmm[bank_i]])
            P.op("tensor", gate_mms(t_s2, m, t_z1, c, bank_g,
                                    extra=(t_uc2, t_c1bf)),
                 reads=[b_slot["s2"], b_uc2, b_z[1][c], b_c1bfc[c]],
                 writes=[b_pmm[bank_g]])
            P.op("scalar",
                 lambda e, m=m, bank_i=bank_i: e.activation(
                     t_sg[1][:], pmm[bank_i][:, 0:CH], AF.Sigmoid,
                     bias=t_b["bi"][:, m:m + 1]),
                 reads=[b_pmm[bank_i], B_("b_bi")], writes=[B_("sg1")])
            P.op("scalar",
                 lambda e, m=m, bank_g=bank_g: e.activation(
                     t_sg[2][:], pmm[bank_g][:, 0:CH], AF.Tanh,
                     bias=t_b["bc"][:, m:m + 1]),
                 reads=[b_pmm[bank_g], B_("b_bc")], writes=[B_("sg2")])
            P.op("vector",
                 lambda e: e.tensor_tensor(t_sg[1][:], t_sg[1][:], t_sg[2][:],
                                           ALU.mult),
                 reads=[B_("sg2")], writes=[B_("sg1")])
            P.op("vector",
                 lambda e, m=m, c=c: e.tensor_tensor(
                     c1_ap(m, c), c1_ap(m, c), t_sg[1][:], ALU.add),
                 reads=[B_("sg1")], writes=[c1_buf(m)])


    B_chunk_p1(0, 0, (t_uchE, t_vchE2, B_("uchE"), B_("vchE2")))
    rank1_mm("uc", 2 * G)
    rank1_fin()
    dbg_dump("b0f", t_b0["f"][:], [B_("b0f")])
    B_chunk_p2(0, 0)
    x_combine(1)
    rs01 = uv_restream(0, 1)
    step0_f(0, range(0, 5))
    B_chunk_p1(0, 1, rs01)
    step0_f(0, range(5, 8))
    step0_ig(0, range(0, 3))
    B_chunk_p2(0, 1)
    rs10 = uv_restream(1, 0)
    step0_ig(0, range(3, 5))
    B_chunk_p1(1, 0, rs10)
    step0_ig(0, range(5, 8))
    step0_f(1, range(0, 2))
    B_chunk_p2(1, 0)
    rs11 = uv_restream(1, 1)
    step0_f(1, range(2, 8))
    step0_ig(1, range(0, 1))
    B_chunk_p1(1, 1, rs11)
    step0_ig(1, range(1, 5))
    B_chunk_p2(1, 1)
    step0_ig(1, range(5, 8))
    c1bf_cast(0)
    step1_f(0, range(0, 4))
    dbg_dump("z0", t_z0[:], b_z[0])
    dbg_dump("z1", t_z1[:], b_z[1])

    dbg_dump("c1", t_c1bf[:], [b_c1bf])
    dma("scalar", "SQ",
        t_ui2.ap().rearrange("p (g d) -> p g d", g=G),
        wT["ui"].rearrange("(g p) d -> p g d", p=128),
        writes=[b_ui2, b_uch])
    dma("scalar", "SQ",
        t_uc2.ap().rearrange("p (g d) -> p g d", g=G),
        wT["uc"].rearrange("(g p) d -> p g d", p=128),
        writes=[b_uc2, b_vch] + list(b_q))

    step1_f(0, range(4, G))
    c1bf_cast(1)
    step1_f(1, range(G))
    load_w("wo")
    load_w("uo")
    step1_ig(0, range(G))
    step1_ig(1, range(G))

    P.op("vector", lambda e: e.tensor_copy(t_c1bf[:, 0:4 * POS], t_c1a[:]),
         reads=[b_c1a], writes=[b_c1bf])
    P.op("vector", lambda e: e.tensor_copy(t_c1bf[:, 4 * POS:], t_c1b[:]),
         reads=[b_c1b], writes=[b_c1bf])
    dbg_dump("c2", t_c1bf[:], [b_c1bf])

    b_hD = [B_("hD0"), B_("hD1")]
    b_x2r = [B_("x2r0"), B_("x2r1")]
    b_hbfD = B_("hbfD")
    b_sqD = B_("sqD")
    hD_t = [t_hD, t_hD2]
    x2r_t = [t_x2r, t_x2r2]
    hD_alias = [[b_c1a], [b_c1b]]
    x2r_alias = [[b_ui2], [b_uc2]]
    st_toks = []
    chunks = [(b, c) for b in range(B) for c in range(NCH)]

    def D_compute(idx, ms):
        b, c = chunks[idx]
        par = idx % 2
        thD, tx2 = hD_t[par], x2r_t[par]
        bhD, bx2 = b_hD[par], b_x2r[par]
        if ms[0] == 0:
            dma("sync", "SY", tx2[:], x2_sp[b, c],
                writes=[bx2] + x2r_alias[par], deps=[x2_tok[(b, c)]])
        for m in ms:
            bank = (c * G + m) % 4
            P.op("tensor", gate_mms(t_s0, m, z_t[b], c, bank,
                                    extra=(t_s3, t_c1bf)),
                 reads=[b_slot["s0"], b_slot["s3"], b_z[b][c], b_c1bf],
                 writes=[b_pmm[bank]])
            P.op("scalar",
                 lambda e, m=m, bank=bank, thD=thD: e.activation(
                     thD[:, m * CH:(m + 1) * CH], pmm[bank][:, 0:CH],
                     AF.Tanh, bias=t_b["bo"][:, m:m + 1]),
                 reads=[b_pmm[bank], B_("b_bo")],
                 writes=[bhD] + hD_alias[par])
        if b == 0 and c == 0:
            dbg_dump("hD00", thD[:], [bhD])

    def D_ln_stats(idx):
        b, c = chunks[idx]
        par = idx % 2
        thD = hD_t[par]
        bhD = b_hD[par]
        ln_stats(thD, bhD, t_hbfD, alias=[b_hbfD, b_abf], eng="vector")
        rows_block(t_ceps[0:1, :], False, True)

    def D_ln_tail(idx):
        b, c = chunks[idx]
        par = idx % 2
        thD, tx2 = hD_t[par], x2r_t[par]
        bhD, bx2 = b_hD[par], b_x2r[par]
        P.op("tensor", bcast(r_r, 0), reads=[b_rows, b_consts],
             writes=[b_pbc[0]])
        P.op("tensor", bcast(r_mr, 1), reads=[b_rows, b_consts],
             writes=[b_pbc[1]])
        P.op("vector",
             lambda e, thD=thD: e.tensor_tensor(g3(thD), g3(thD), bc3(pbc[0]),
                                                ALU.mult),
             reads=[b_pbc[0]], writes=[bhD])
        P.op("vector",
             lambda e, thD=thD: e.tensor_tensor(g3(thD), g3(thD), bc3(pbc[1]),
                                                ALU.subtract),
             reads=[b_pbc[1]], writes=[bhD])
        P.op("vector",
             lambda e, thD=thD, tx2=tx2: e.tensor_tensor(
                 tx2[:], thD[:], tx2[:], ALU.add),
             reads=[bhD], writes=[bx2])
        st_toks.append(
            dma("scalar", "SQ",
                outT[b].rearrange("(g p) c -> p g c", p=128)[:, :, c * CH:(c + 1) * CH],
                tx2.ap().rearrange("p (g c) -> p g c", g=G),
                reads=[bx2]))

    D_compute(0, range(G))
    for idx in range(1, len(chunks)):
        D_compute(idx, range(0, 6))
        D_ln_stats(idx - 1)
        D_compute(idx, range(6, 8))
        D_ln_tail(idx - 1)
    D_ln_stats(len(chunks) - 1)
    D_ln_tail(len(chunks) - 1)

    P.seq["scalar"].append((
        [(s, v) for (_e, s, v) in st_toks], lambda e: None, None, 0))

    nc._rec = []
    nc._plan = P
    with nc.Block() as block:

        @block.tensor
        def _(eng):
            P.emit("tensor", eng, sems, nc, nc._rec)

        @block.vector
        def _(eng):
            P.emit("vector", eng, sems, nc, nc._rec)

        @block.scalar
        def _(eng):
            P.emit("scalar", eng, sems, nc, nc._rec)

        @block.gpsimd
        def _(eng):
            P.emit("gpsimd", eng, sems, nc, nc._rec)

        @block.sync
        def _(eng):
            P.emit("sync", eng, sems, nc, nc._rec)

    return nc


_NC_CACHE = {}


def kernel(L0_attn_x, L0_attn_w_qkva, L0_attn_gamma,
           L0_ffn_wf, L0_ffn_uf, L0_ffn_bf, L0_ffn_wi, L0_ffn_ui, L0_ffn_bi,
           L0_ffn_wc, L0_ffn_uc, L0_ffn_bc, L0_ffn_wo, L0_ffn_uo, L0_ffn_bo,
           L0_ffn_in_c):
    x = np.asarray(L0_attn_x, dtype=np.float32)
    SEQ = x.shape[1]
    PC = SEQ // NCORES

    def feat(v):
        return np.ascontiguousarray(np.asarray(v, np.float32).reshape(G, 128).T)

    def tbf(v):
        b = np.ascontiguousarray(v, np.float32)
        return b.view(np.uint16).reshape(*b.shape, 2)[..., 1].copy().view(
            ml_dtypes.bfloat16)

    shared = {
        "wqkT": tbf(np.asarray(L0_attn_w_qkva, np.float32).T),
        "gamma": feat(L0_attn_gamma),
        "wfT": tbf(np.asarray(L0_ffn_wf, np.float32).T),
        "wiT": tbf(np.asarray(L0_ffn_wi, np.float32).T),
        "wcT": tbf(np.asarray(L0_ffn_wc, np.float32).T),
        "woT": tbf(np.asarray(L0_ffn_wo, np.float32).T),
        "ufT": tbf(np.asarray(L0_ffn_uf, np.float32).T),
        "uiT": tbf(np.asarray(L0_ffn_ui, np.float32).T),
        "ucT": tbf(np.asarray(L0_ffn_uc, np.float32).T),
        "uoT": tbf(np.asarray(L0_ffn_uo, np.float32).T),
        "bf_b": feat(L0_ffn_bf), "bi_b": feat(L0_ffn_bi),
        "bc_b": feat(L0_ffn_bc), "bo_b": feat(L0_ffn_bo),
        "inc_b": feat(L0_ffn_in_c),
    }
    in_maps = []
    for k in range(NCORES):
        m = dict(shared)
        m["x0T"] = np.ascontiguousarray(x[0, k * PC:(k + 1) * PC, :].T)
        m["x1T"] = np.ascontiguousarray(x[1, k * PC:(k + 1) * PC, :].T)
        m["x0Tb"] = tbf(m["x0T"])
        m["x1Tb"] = tbf(m["x1T"])
        mlt = np.zeros((128, G), np.float32)
        mlt[:, :k] = 1.0
        m["mask_lt"] = mlt
        m["mask_c"] = np.ascontiguousarray(1.0 - mlt)
        in_maps.append(m)

    if "nc" not in _NC_CACHE:
        _NC_CACHE["nc"] = build_nc()
    res = run_bass_kernel_spmd(_NC_CACHE["nc"], in_maps,
                               core_ids=list(range(NCORES)))
    _NC_CACHE["res"] = res

    out = np.empty((B, SEQ, D), np.float32)
    for k in range(NCORES):
        ot = res.results[k]["outT"]
        out[0, k * PC:(k + 1) * PC, :] = ot[0].T
        out[1, k * PC:(k + 1) * PC, :] = ot[1].T
    return out


if __name__ == "__main__":
    build_nc()
    print("built ok")


# version 40
# speedup vs baseline: 1.0200x; 1.0049x over previous
import numpy as np
import ml_dtypes
import concourse.bass as bass
import concourse.mybir as mybir
from concourse.bass_utils import run_bass_kernel_spmd

F32 = mybir.dt.float32
BF16 = mybir.dt.bfloat16
AF = mybir.ActivationFunctionType
ALU = mybir.AluOpType

NCORES = 8
D = 1024
G = 8
POS = 1024
B = 2
CH = 512
NCH = POS // CH
KB = 1024

ENG_SEM = {"tensor": "T", "vector": "V", "scalar": "S", "gpsimd": "G", "sync": "Y"}


class Buf:
    __slots__ = ("name", "w", "r")

    def __init__(self, name):
        self.name = name
        self.w = None
        self.r = []


class Plan:
    def __init__(self):
        self.seq = {e: [] for e in ENG_SEM}
        self.cnt = {}
        self.rings = {}

    def ring(self, name, n):
        self.rings[name] = {"sems": [f"{name}_{i}" for i in range(n)], "i": 0,
                            "last": [None] * n}

    def op(self, eng, fn, reads=(), writes=(), deps=(), queue=None, sem=None,
           inc=None):
        dep_list = []
        for b in reads:
            if b.w is not None:
                dep_list.append(b.w)
        for b in writes:
            if b.w is not None:
                dep_list.append(b.w)
            dep_list.extend(b.r)
        dep_list.extend(t for t in deps if t is not None)
        slot = None
        if queue is not None:
            rg = self.rings[queue]
            slot = rg["i"]
            rg["i"] = (slot + 1) % len(rg["sems"])
            sem = rg["sems"][slot]
            inc = 16
            if rg["last"][slot] is not None:
                dep_list.append(rg["last"][slot])
            async_tok = True
        elif sem is not None:
            async_tok = True
        else:
            sem = ENG_SEM[eng]
            inc = 1
            async_tok = False
        waits = []
        for (teng, tsem, tval) in dep_list:
            if teng == eng:
                continue
            waits.append((tsem, tval))
        self.cnt[sem] = self.cnt.get(sem, 0) + inc
        tok = (None if async_tok else eng, sem, self.cnt[sem])
        self.seq[eng].append((waits, fn, sem, inc))
        if queue is not None:
            self.rings[queue]["last"][slot] = tok
        for b in reads:
            b.r.append(tok)
        for b in writes:
            b.w = tok
            b.r = []
        return tok

    def emit(self, eng_name, eng, sems, nc=None, rec=None):
        hw = {}
        for oi, (waits, fn, sem, inc) in enumerate(self.seq[eng_name]):
            emitted = []
            for (sname, val) in waits:
                if hw.get(sname, 0) >= val:
                    continue
                eng.wait_ge(sems[sname], val)
                hw[sname] = val
                emitted.append((sname, val))
            n0 = nc.n_instructions() if nc is not None else 0
            inst = fn(eng)
            if inst is not None and sem is not None:
                inst.then_inc(sems[sem], inc)
            if rec is not None:
                rec.append((eng_name, oi, nc.n_instructions() - n0,
                            emitted, sem, inc))


_DEBUG = False
_NO_CC = False


def build_nc(dbg=False):
    nc = bass.Bass("TRN2", target_bir_lowering=False)

    xT = [nc.dram_tensor(f"x{b}T", [D, POS], F32, kind="ExternalInput").ap()
          for b in range(B)]
    xTb = [nc.dram_tensor(f"x{b}Tb", [D, POS], BF16, kind="ExternalInput").ap()
           for b in range(B)]
    wqkT_d = nc.dram_tensor("wqkT", [D, 3 * D], BF16, kind="ExternalInput").ap()
    gamma_in = nc.dram_tensor("gamma", [128, G], F32, kind="ExternalInput").ap()
    wT = {}
    for nm in ("wf", "wi", "wc", "wo", "uf", "ui", "uc", "uo"):
        wT[nm] = nc.dram_tensor(nm + "T", [D, D], BF16, kind="ExternalInput").ap()
    bias_in = {}
    for nm in ("bf", "bi", "bc", "bo", "inc"):
        bias_in[nm] = nc.dram_tensor(nm + "_b", [128, G], F32,
                                     kind="ExternalInput").ap()
    mlt_in = nc.dram_tensor("mask_lt", [128, G], F32, kind="ExternalInput").ap()
    mlc_in = nc.dram_tensor("mask_c", [128, G], F32, kind="ExternalInput").ap()
    outT = nc.dram_tensor("outT", [B, D, POS], F32, kind="ExternalOutput").ap()

    _spk = "ExternalOutput" if dbg else "Internal"
    uv_sp = nc.dram_tensor("uv_sp", [B, 2, NCH, 128, G * CH], BF16,
                           kind=_spk).ap()
    x2_sp = nc.dram_tensor("x2_sp", [B, NCH, 128, G * CH], F32, kind=_spk).ap()
    cc_in = [nc.dram_tensor(f"cc_in{b}", [128, 16], F32) for b in range(B)]
    cc_out = [nc.dram_tensor(f"cc_out{b}", [NCORES, 128, 16], F32)
              for b in range(B)]
    cc_all_in = nc.dram_tensor("cc_all_in", [128, 32], F32)
    cc_all_out = nc.dram_tensor("cc_all_out", [NCORES, 128, 32], F32)
    cc_wrm = nc.dram_tensor("cc_wrm", [128, 16], F32)
    cc_wrm_o = nc.dram_tensor("cc_wrm_o", [NCORES, 128, 16], F32)

    SB_BASE = nc.sbuf_top - nc.sbuf_bytes_remaining
    SB_BASE = (SB_BASE + 127) // 128 * 128
    assert nc.sbuf_bytes_remaining >= 205 * KB, nc.sbuf_bytes_remaining

    def at(name, shape, dtype, off):
        return nc.alloc_sbuf_tensor_at(name, shape, dtype, offset=SB_BASE + off)

    o = 0

    def bump(name, shape, dtype, align=32):
        nonlocal o
        es = mybir.dt.size(dtype)
        o = (o + align - 1) // align * align
        h = at(name, shape, dtype, o)
        n = 1
        for s in shape[1:]:
            n *= s
        o += n * es
        return h

    t_gamma = bump("gammat", [128, G], F32)
    t_b = {nm: bump(nm + "t", [128, G], F32)
           for nm in ("bf", "bi", "bc", "bo", "inc")}
    t_incbf = bump("incbf", [128, G], BF16)
    t_mlt = bump("mltt", [128, G], F32)
    t_mlc = bump("mlct", [128, G], F32)
    t_cy = bump("cyt", [128, 2 * G], F32)
    t_cp = bump("cpt", [128, 2 * G], F32)
    t_cin = bump("cint", [128, 2 * G], F32)
    t_cst = bump("cstt", [128, 2 * G], F32)
    t_tmpA = bump("tmpAt", [128, 2 * G], F32)
    t_tmpY = bump("tmpYt", [128, 2 * G], F32)
    t_vr = {nm: bump("vr" + nm, [128, G], F32) for nm in ("f", "i", "c")}
    t_b0 = {nm: bump("b0" + nm, [128, G], F32) for nm in ("f", "i", "c")}
    t_summ = bump("summt", [128, 32], F32)
    t_gath = bump("gatht", [128, 256], F32)
    t_onescol = bump("onescol", [128, 1], BF16)
    t_onesrow = bump("onesrow", [1, 128], F32)
    t_ones512 = bump("ones512", [128, CH], F32)
    t_ceps = bump("cepst", [1, 1], F32)
    t_cmax = bump("cmaxt", [1, 1], F32)
    r_m = bump("r_m", [1, CH], F32)
    r_v = bump("r_v", [1, CH], F32)
    r_r = bump("r_r", [1, CH], F32)
    r_mr = bump("r_mr", [1, CH], F32)
    t_sg = [bump(f"sg{i}", [128, CH], F32) for i in range(3)]
    t_cbA = bump("cbA", [128, 72], F32)
    t_cbY = bump("cbY", [128, 72], F32)
    t_cbS = bump("cbS", [128, 72], F32)
    assert o <= 20 * KB, f"small region overflow: {o}"

    t_wqk = at("wqk", [128, G * 3 * D], BF16, 20 * KB)
    t_s0 = at("s0", [128, G * D], BF16, 20 * KB)
    t_s1 = at("s1", [128, G * D], BF16, 36 * KB)
    t_sqB = at("sqB", [128, G * CH], BF16, 36 * KB)
    t_s2 = at("s2", [128, G * D], BF16, 52 * KB)
    t_xbf = at("xbf", [128, G * POS], BF16, 68 * KB)
    t_s3 = at("s3", [128, G * D], BF16, 68 * KB)
    t_hbf0 = at("hbf0", [128, G * POS], BF16, 84 * KB)
    t_s4 = at("s4", [128, G * D], BF16, 84 * KB)
    t_vchE2 = at("vchE2", [128, G * CH], BF16, 92 * KB)
    t_hD = at("hD", [128, G * CH], F32, 84 * KB)
    t_hbf1 = at("hbf1", [128, G * POS], BF16, 100 * KB)
    t_s5 = at("s5", [128, G * D], BF16, 100 * KB)
    t_x2r = at("x2r", [128, G * CH], F32, 100 * KB)
    t_sqA = at("sqA", [128, G * CH], BF16, 116 * KB)
    t_xm = at("xmscr", [128, G * (CH // 2)], F32, 116 * KB)
    t_uchE = at("uchE", [128, G * CH], BF16, 116 * KB)
    t_abf = at("abf", [128, G * CH], BF16, 116 * KB)
    t_hbfD = at("hbfD", [128, G * CH], BF16, 116 * KB)
    t_y = at("ych", [128, G * CH], F32, 124 * KB)
    t_c1a = at("c1a", [128, 4 * POS], F32, 124 * KB)
    t_hD = at("hD", [128, G * CH], F32, 124 * KB)
    t_P = at("Pch", [128, G * CH], F32, 140 * KB)
    t_uch = at("uch", [128, G * CH], F32, 140 * KB)
    t_uchb = at("uchb", [128, G * CH], BF16, 140 * KB)
    t_vchb = at("vchb", [128, G * CH], BF16, 148 * KB)
    t_ui2 = at("ui2", [128, G * D], BF16, 140 * KB)
    t_x2r = at("x2r", [128, G * CH], F32, 140 * KB)
    t_q = at("qch", [128, G * CH], F32, 156 * KB)
    t_vch = at("vch", [128, G * CH], F32, 156 * KB)
    t_uc2 = at("uc2", [128, G * D], BF16, 156 * KB)
    t_x2r2 = at("x2r2", [128, G * CH], F32, 156 * KB)
    t_kv = at("kvch", [128, G * CH], F32, 172 * KB)
    t_z1 = at("z1", [128, G * POS], BF16, 172 * KB)
    t_a = at("ach", [128, G * CH], F32, 188 * KB)
    t_ubf = at("ubf", [128, G * CH], BF16, 188 * KB)
    t_vbf = at("vbf", [128, G * CH], BF16, 196 * KB)
    t_z0 = at("z0", [128, G * POS], BF16, 188 * KB)
    t_c1b = at("c1b", [128, 4 * POS], F32, 84 * KB)
    t_hD2 = at("hD2", [128, G * CH], F32, 84 * KB)
    t_c1bf = at("c1bf", [128, G * POS], BF16, 100 * KB)

    pmm = [nc.alloc_psum_tensor(f"pmm{i}", [128, 512], F32) for i in range(4)]
    pstat = [nc.alloc_psum_tensor(f"pstat{i}", [1, 512], F32) for i in range(2)]
    pbc = [nc.alloc_psum_tensor(f"pbc{i}", [128, 512], F32) for i in range(2)]

    sems = {}
    for s in ("T", "V", "S", "G", "Y", "CC"):
        sems[s] = nc.alloc_semaphore(s)
    P = Plan()
    P.ring("SY", 6)
    P.ring("SQ", 4)
    P.ring("GQ", 2)
    for rg in ("SY", "SQ", "GQ"):
        for s in P.rings[rg]["sems"]:
            sems[s] = nc.alloc_semaphore(s)

    bufs = {}

    def B_(name):
        if name not in bufs:
            bufs[name] = Buf(name)
        return bufs[name]

    def dma(eng, queue, dst_ap, src_ap, reads=(), writes=(), deps=()):
        return P.op(eng, lambda e: e.dma_start(dst_ap, src_ap),
                    reads=reads, writes=writes, deps=deps, queue=queue)

    def vguard(bufs_to_stamp):
        P.op("vector", lambda e: e.memset(t_sg[2][:], 0.0),
             reads=[], writes=[B_("sg2")])
        P.op("vector", lambda e: e.memset(t_sg[2][:], 0.0),
             reads=[], writes=[B_("sg2")] + list(bufs_to_stamp))

    dbg_names = []

    def dbg_dump(name, src_ap, reads):
        if not dbg:
            return
        shp = list(src_ap.shape)
        dt = src_ap.dtype
        d = nc.dram_tensor("dbg_" + name, shp, dt, kind="ExternalOutput").ap()
        dbg_names.append("dbg_" + name)
        dma("sync", "SY", d, src_ap, reads=list(reads))

    def g3(t, c0=0, ncc=CH, width=CH, ng=G):
        ap = t.ap()
        row = ap.ap[0][0]
        return bass.AP(t, c0, [[row, 128], [width, ng], [1, ncc]])

    def bc3(pt, c0=0, ncc=CH, ng=G):
        ap = pt.ap()
        row = ap.ap[0][0]
        return bass.AP(pt, c0, [[row, 128], [0, ng], [1, ncc]])

    b_pstat = [B_("pstat0"), B_("pstat1")]
    b_pbc = [B_("pbc0"), B_("pbc1")]
    b_pmm = [B_(f"pmm{i}") for i in range(4)]
    b_rows = B_("rows")
    b_consts = B_("consts")
    b_params = B_("params")

    b_xbf = B_("xbf")
    b_wqk = B_("wqk")
    b_wqk3 = [B_("wqk3a"), B_("wqk3b"), B_("wqk3c")]
    for hx in range(2):
        dma("scalar", "SQ",
            t_xbf.ap().rearrange("p (g c) -> p g c", g=G)[:, hx * 4:(hx + 1) * 4],
            xTb[0].rearrange("(g p) c -> p g c", p=128)[:, hx * 4:(hx + 1) * 4],
            writes=[b_xbf])
    dma("scalar", "SQ",
        t_wqk.ap().rearrange("p (g o) -> p g o", g=G),
        wqkT_d.rearrange("(g p) o -> p g o", p=128),
        writes=[b_wqk] + b_wqk3)
    dma("sync", "SY", t_gamma[:], gamma_in, writes=[b_params])
    for nm in ("bf", "bi", "bc", "bo", "inc"):
        dma("sync", "SY", t_b[nm][:], bias_in[nm], writes=[B_("b_" + nm)])
    dma("sync", "SY", t_mlt[:], mlt_in, writes=[B_("masks")])
    dma("sync", "SY", t_mlc[:], mlc_in, writes=[B_("masks")])

    def consts(e):
        e.memset(t_onescol[:], 1.0 / 1024.0)
        e.memset(t_onesrow[:], 1.0)
        e.memset(t_ones512[:], 1.0)
        e.memset(t_ceps[:], 1e-5)
        e.memset(t_cmax[:], 9.5e-28)
        e.memset(t_cy[:], 0.0)
        e.memset(t_cp[:], 1.0)
        return e.memset(t_cst[:], 0.0)
    P.op("vector", consts, writes=[b_consts, B_("cy"), B_("cp"), B_("cst")])

    P.op("vector", lambda e: e.tensor_copy(t_incbf[:], t_b["inc"][:]),
         reads=[B_("b_inc")], writes=[B_("incbf")])


    def rows_block(eps_ap, floor, want_mr):
        er = "vector"
        P.op(er, lambda e: e.tensor_copy(r_m[:], pstat[0][0:1, :]),
             reads=[b_pstat[0]], writes=[b_rows])

        def f_var(e):
            e.tensor_tensor(r_v[:], r_m[:], r_m[:], ALU.mult)
            return e.tensor_tensor(r_v[:], pstat[1][0:1, :], r_v[:], ALU.subtract)
        P.op(er, f_var, reads=[b_pstat[1]], writes=[b_rows])
        if floor:
            P.op("vector",
                 lambda e: e.tensor_scalar(r_v[:], r_v[:], t_cmax[0:1, :], None,
                                           ALU.max),
                 reads=[b_consts], writes=[b_rows])

        def f_ln(e):
            if eps_ap is None:
                e.activation(r_r[:], r_v[:], AF.Ln)
            else:
                e.activation(r_r[:], r_v[:], AF.Ln, bias=eps_ap)
            return e.activation(r_r[:], r_r[:], AF.Exp, scale=-0.5)
        P.op("scalar", f_ln, reads=[b_rows, b_consts], writes=[b_rows])
        if want_mr:
            P.op(er, lambda e: e.tensor_tensor(r_mr[:], r_m[:], r_r[:],
                                               ALU.mult),
                 reads=[b_rows], writes=[b_rows])

    def bcast(row_tile, which):
        def f(e):
            return e.matmul(pbc[which][:, :], t_onesrow[:], row_tile[:],
                            start=True, stop=True)
        return f

    def stat_sum(src_tile, src_buf):
        def fsum(e):
            ins = None
            for g in range(G):
                ins = e.matmul(pstat[0][0:1, :], t_onescol[:],
                               src_tile[:, g * CH:(g + 1) * CH],
                               start=(g == 0), stop=(g == G - 1))
            return ins
        P.op("tensor", fsum, reads=[src_buf, b_consts], writes=[b_pstat[0]])

    def stat_sq(sq_tile, sq_buf):
        def fsq(e):
            ins = None
            for g in range(G):
                ins = e.matmul(pstat[1][0:1, :], t_onescol[:],
                               sq_tile[:, g * CH:(g + 1) * CH],
                               start=(g == 0), stop=(g == G - 1))
            return ins
        P.op("tensor", fsq, reads=[sq_buf, b_consts], writes=[b_pstat[1]])

    b_abfh = [B_("abfh0"), B_("abfh1")]

    def stat_half(bank, tile, b_half, h, start):
        lo = h * 4 * CH

        def f(e, lo=lo, bank=bank, start=start):
            ins = None
            for g in range(4):
                ins = e.matmul(pstat[bank][0:1, :], t_onescol[:],
                               tile[:, lo + g * CH: lo + (g + 1) * CH],
                               start=(start and g == 0),
                               stop=(not start and g == 3))
            return ins
        P.op("tensor", f, reads=[b_half, b_consts], writes=[b_pstat[bank]])

    def ln_stats(src_tile, b_src, abf_tile, alias=(), eng="scalar"):
        H = 4 * CH

        def cast_f(e, h):
            if eng == "scalar":
                return e.activation(abf_tile[:, h * H:(h + 1) * H],
                                    src_tile[:, h * H:(h + 1) * H], AF.Copy)
            return e.tensor_copy(abf_tile[:, h * H:(h + 1) * H],
                                 src_tile[:, h * H:(h + 1) * H])

        def sq_f(e, h):
            if eng == "scalar":
                return e.activation(abf_tile[:, h * H:(h + 1) * H],
                                    src_tile[:, h * H:(h + 1) * H], AF.Square)
            return e.tensor_tensor(abf_tile[:, h * H:(h + 1) * H],
                                   src_tile[:, h * H:(h + 1) * H],
                                   src_tile[:, h * H:(h + 1) * H], ALU.mult)
        for h in range(2):
            P.op(eng, lambda e, h=h: cast_f(e, h),
                 reads=[b_src], writes=[b_abfh[h]] + list(alias))
        for h in range(2):
            stat_half(0, abf_tile, b_abfh[h], h, start=(h == 0))
        for h in range(2):
            P.op(eng, lambda e, h=h: sq_f(e, h),
                 reads=[b_src], writes=[b_abfh[h]])
        for h in range(2):
            stat_half(1, abf_tile, b_abfh[h], h, start=(h == 0))

    def stat_pair(src_tile, src_buf, src_off, src_w, sq_tile, sq_buf):
        def fsum(e):
            ins = None
            for g in range(G):
                ins = e.matmul(pstat[0][0:1, :], t_onescol[:],
                               src_tile[:, g * src_w + src_off:
                                        g * src_w + src_off + CH],
                               start=(g == 0), stop=(g == G - 1))
            return ins
        P.op("tensor", fsum, reads=[src_buf, b_consts], writes=[b_pstat[0]])

        def fsq(e):
            ins = None
            for g in range(G):
                ins = e.matmul(pstat[1][0:1, :], t_onescol[:],
                               sq_tile[:, g * CH:(g + 1) * CH],
                               start=(g == 0), stop=(g == G - 1))
            return ins
        P.op("tensor", fsq, reads=[sq_buf, b_consts], writes=[b_pstat[1]])

    def rank1_mm(nm, off):
        def f(e, nm=nm, off=off):
            ins = None
            t = wslot[nm]
            for m in range(G):
                for g in range(G):
                    ins = e.matmul(pmm[3][:, off + m:off + m + 1],
                                   t[:, g * D + m * 128: g * D + (m + 1) * 128],
                                   t_incbf[:, g:g + 1],
                                   start=(g == 0), stop=(g == G - 1))
            return ins
        P.op("tensor", f, reads=[b_slot[slot_of[nm]], B_("incbf")],
             writes=[b_pmm[3]])

    def rank1_fin():
        for off, gate, bname in ((0, "f", "bf"), (G, "i", "bi"),
                                 (2 * G, "c", "bc")):
            P.op("vector",
                 lambda e, off=off, gate=gate, bname=bname: e.tensor_tensor(
                     t_b0[gate][:], pmm[3][:, off:off + G], t_b[bname][:],
                     ALU.add),
                 reads=[b_pmm[3], B_("b_" + bname)], writes=[B_("b0" + gate)])
        vguard([B_("b0f"), B_("b0i"), B_("b0c")])


    b_sqA = B_("sqA")
    b_hbf = [B_("hbf0"), B_("hbf1")]
    b_q = [B_(f"qch{g}") for g in range(G)]
    b_kv = [B_(f"kvch{g}") for g in range(G)]
    b_a = [B_(f"ach{g}") for g in range(G)]
    b_y, b_P = B_("ych"), B_("Pch")
    b_ubf, b_vbf = B_("ubf"), B_("vbf")
    b_cy, b_cp = B_("cy"), B_("cp")
    hbf_t = [t_hbf0, t_hbf1]
    uv_tok = {}

    def ln1_block(b):
        for c in range(NCH):
            P.op("scalar",
                 lambda e, c=c: e.activation(
                     g3(t_sqA), g3(t_xbf, c0=c * CH, width=POS), AF.Square),
                 reads=[b_xbf], writes=[b_sqA])
            stat_pair(t_xbf, b_xbf, c * CH, POS, t_sqA, b_sqA)
            rows_block(None, True, False)
            P.op("tensor", bcast(r_m, 0), reads=[b_rows, b_consts],
                 writes=[b_pbc[0]])
            P.op("tensor", bcast(r_r, 1), reads=[b_rows, b_consts],
                 writes=[b_pbc[1]])
            for h2 in range(2):
                c0 = c * CH + h2 * (CH // 2)
                P.op("vector",
                     lambda e, c0=c0: e.tensor_tensor(
                         g3(t_xm, ncc=CH // 2, width=CH // 2),
                         g3(t_xbf, c0=c0, ncc=CH // 2, width=POS),
                         bc3(pbc[0], c0=(c0 % CH), ncc=CH // 2),
                         ALU.subtract),
                     reads=[b_xbf, b_pbc[0]], writes=[b_sqA])
                for g in range(G):
                    P.op("vector",
                         lambda e, b=b, c0=c0, g=g: e.scalar_tensor_tensor(
                             hbf_t[b][:, g * POS + c0:
                                      g * POS + c0 + CH // 2],
                             t_xm[:, g * (CH // 2):(g + 1) * (CH // 2)],
                             t_gamma[:, g:g + 1],
                             pbc[1][:, (c0 % CH):(c0 % CH) + CH // 2],
                             ALU.mult, ALU.mult),
                         reads=[b_sqA, b_pbc[1], b_params],
                         writes=[b_hbf[b]])
        dbg_dump(f"hbf{b}", hbf_t[b][:], [b_hbf[b]])

    def qkva_chunk(b, c):
        order = []
        for g in range(G):
            order.append((8 + g, "kv"))
            order.append((16 + g, "a"))
        for g in range(G):
            order.append((g, "q"))
        for k, (m, kind) in enumerate(order):
            bank = k % 4

            def fmm2(e, b=b, c=c, m=m, bank=bank):
                ins = None
                for g in range(G):
                    ins = e.matmul(
                        pmm[bank][:, 0:CH],
                        t_wqk[:, g * 3 * D + m * 128:
                              g * 3 * D + (m + 1) * 128],
                        hbf_t[b][:, g * POS + c * CH:
                                 g * POS + (c + 1) * CH],
                        start=(g == 0), stop=(g == G - 1))
                return ins
            P.op("tensor", fmm2, reads=[b_wqk, b_hbf[b]] + b_wqk3,
                 writes=[b_pmm[bank]])
            if kind == "q":
                P.op("scalar",
                     lambda e, m=m, bank=bank: e.copy(
                         t_q[:, m * CH:(m + 1) * CH], pmm[bank][:, 0:CH]),
                     reads=[b_pmm[bank]], writes=[b_q[m]])
            elif kind == "kv":
                P.op("scalar",
                     lambda e, m=m, bank=bank: e.copy(
                         t_kv[:, (m - 8) * CH:(m - 7) * CH],
                         pmm[bank][:, 0:CH]),
                     reads=[b_pmm[bank]], writes=[b_kv[m - 8]])
            else:
                P.op("scalar",
                     lambda e, m=m, bank=bank: e.activation(
                         t_a[:, (m - 16) * CH:(m - 15) * CH],
                         pmm[bank][:, 0:CH], AF.Sigmoid),
                     reads=[b_pmm[bank]],
                     writes=[b_a[m - 16],
                             b_ubf if (m - 16) < 4 else b_vbf])
        if b == 0 and dbg:
            dbg_dump(f"q0{c}", t_q[:], list(b_q))
            dbg_dump(f"kv0{c}", t_kv[:], list(b_kv))
            dbg_dump(f"a0{c}", t_a[:], list(b_a))
        for g in range(G):
            sl = slice(g * CH, (g + 1) * CH)
            cyc = t_cy[:, b * G + g: b * G + g + 1]
            P.op("vector",
                 lambda e, sl=sl, cyc=cyc: e.tensor_tensor_scan(
                     t_y[:, sl], t_a[:, sl], t_kv[:, sl], cyc,
                     ALU.mult, ALU.add),
                 reads=[b_a[g], b_kv[g], b_cy], writes=[b_y])
        for g in range(G):
            sl = slice(g * CH, (g + 1) * CH)
            cpc = t_cp[:, b * G + g: b * G + g + 1]
            P.op("vector",
                 lambda e, sl=sl, cpc=cpc: e.tensor_tensor_scan(
                     t_P[:, sl], t_a[:, sl], t_ones512[:], cpc,
                     ALU.mult, ALU.mult),
                 reads=[b_a[g], b_consts, b_cp], writes=[b_P])
        for g in range(G):
            sl = slice(g * CH, (g + 1) * CH)
            cyc = t_cy[:, b * G + g: b * G + g + 1]
            P.op("vector",
                 lambda e, sl=sl, cyc=cyc: e.tensor_copy(
                     cyc, t_y[:, sl.stop - 1: sl.stop]),
                 reads=[b_y], writes=[b_cy])
        for g in range(G):
            sl = slice(g * CH, (g + 1) * CH)
            cpc = t_cp[:, b * G + g: b * G + g + 1]
            P.op("vector",
                 lambda e, sl=sl, cpc=cpc: e.tensor_copy(
                     cpc, t_P[:, sl.stop - 1: sl.stop]),
                 reads=[b_P], writes=[b_cp])
        peng = "gpsimd" if (b, c) == (1, 1) else "vector"
        for g in range(G):
            sl = slice(g * CH, (g + 1) * CH)
            P.op(peng,
                 lambda e, sl=sl: e.tensor_tensor(
                     t_ubf[:, sl], t_y[:, sl], t_q[:, sl], ALU.mult),
                 reads=[b_y, b_q[g]],
                 writes=[b_ubf] + ([b_a[g2] for g2 in range(4)]
                                   if g == 0 else []))
        for g in range(G):
            sl = slice(g * CH, (g + 1) * CH)
            P.op(peng,
                 lambda e, sl=sl: e.tensor_tensor(
                     t_vbf[:, sl], t_P[:, sl], t_q[:, sl], ALU.mult),
                 reads=[b_P, b_q[g]],
                 writes=[b_vbf] + ([b_a[g2] for g2 in range(4, G)]
                                   if g == 0 else []))
        if b == 0 and dbg:
            dbg_dump(f"y0{c}", t_y[:], [b_y])
            dbg_dump(f"P0{c}", t_P[:], [b_P])
            dbg_dump(f"cyA{c}", t_cy[:], [b_cy])
        uv_tok[(b, 0, c)] = dma("sync", "SY", uv_sp[b, 0, c], t_ubf[:],
                                reads=[b_ubf])
        uv_tok[(b, 1, c)] = dma("sync", "SY", uv_sp[b, 1, c], t_vbf[:],
                                reads=[b_vbf])

    b_summ = [B_("summ0"), B_("summ1")]
    b_gath = [B_("gath0"), B_("gath1")]
    b_cin = [B_("cin0"), B_("cin1")]
    tok_cc = [None, None]

    def x_trigger(b):
        P.op("vector",
             lambda e, b=b: e.tensor_copy(
                 t_summ[:, b * 16: b * 16 + 8], t_cy[:, b * G:(b + 1) * G]),
             reads=[b_cy], writes=[b_summ[b]])
        P.op("vector",
             lambda e, b=b: e.tensor_copy(
                 t_summ[:, b * 16 + 8: b * 16 + 16],
                 t_cp[:, b * G:(b + 1) * G]),
             reads=[b_cp], writes=[b_summ[b]])
        vguard([b_summ[b]])
        tok_sm = dma("gpsimd", "GQ", cc_in[b].ap(),
                     t_summ[:, b * 16:b * 16 + 16], reads=[b_summ[b]])
        tok = P.op("gpsimd",
                   lambda e, b=b: e.collective_compute(
                       "AllGather", ALU.bypass,
                       replica_groups=[list(range(NCORES))],
                       ins=[cc_in[b].ap()], outs=[cc_out[b].ap()]),
                   deps=[tok_sm], sem="CC", inc=1)
        tok_cc[b] = tok
        dma("gpsimd", "GQ",
            t_gath[:, b * 128:(b + 1) * 128].rearrange("p (r c) -> p r c", r=8),
            cc_out[b].ap().rearrange("r p c -> p r c"),
            writes=[b_gath[b]], deps=[tok])

    def st3(t, off):
        ap = t.ap()
        row = ap.ap[0][0]
        return bass.AP(t, off, [[row, 128], [9, 8], [1, 1]])

    def g8(b, off):
        ap = t_gath.ap()
        row = ap.ap[0][0]
        return bass.AP(t_gath, b * 128 + off, [[row, 128], [1, 8], [1, 1]])

    def x_combine(b):
        P.op("vector", lambda e: e.memset(t_cbA[:], 0.0), writes=[B_("cbA")])
        P.op("vector", lambda e: e.memset(t_cbY[:], 0.0), writes=[B_("cbY")])
        for r in range(NCORES):
            P.op("vector",
                 lambda e, r=r, b=b: e.tensor_scalar(
                     st3(t_cbA, 1 + r), g8(b, r * 16 + 8),
                     t_mlt[:, r:r + 1], t_mlc[:, r:r + 1], ALU.mult, ALU.add),
                 reads=[b_gath[b], B_("masks")], writes=[B_("cbA")])
        for r in range(NCORES):
            P.op("vector",
                 lambda e, r=r, b=b: e.tensor_scalar(
                     st3(t_cbY, 1 + r), g8(b, r * 16),
                     t_mlt[:, r:r + 1], None, ALU.mult),
                 reads=[b_gath[b], B_("masks")], writes=[B_("cbY")])
        P.op("vector", lambda e: e.memset(t_sg[2][:, 0:512], 0.0),
             writes=[B_("sg2")])
        P.op("vector",
             lambda e: e.tensor_tensor_scan(t_cbS[:], t_cbA[:], t_cbY[:],
                                            0.0, ALU.mult, ALU.add),
             reads=[B_("cbA"), B_("cbY")], writes=[B_("cbS")])
        P.op("vector", lambda e: e.memset(t_sg[2][:, 0:512], 0.0),
             writes=[B_("sg2")])

        def stfin():
            ap = t_cbS.ap()
            row = ap.ap[0][0]
            return bass.AP(t_cbS, 8, [[row, 128], [9, 8]])
        P.op("vector",
             lambda e, b=b: e.tensor_copy(t_cin[:, b * G:(b + 1) * G], stfin()),
             reads=[B_("cbS")], writes=[b_cin[b]])
        vguard([b_cin[b]])
        if b == 1:
            dbg_dump("cy", t_cy[:], [b_cy])
            dbg_dump("cp", t_cp[:], [b_cp])
            dbg_dump("gath", t_gath[:], [b_gath[0], b_gath[1]])
            dbg_dump("cin", t_cin[:], [b_cin[0], b_cin[1]])

    wslot = {"wf": t_s0, "wi": t_s1, "wc": t_s2, "uf": t_s3, "ui": t_s4,
             "uc": t_s5, "wo": t_s0, "uo": t_s3}
    slot_of = {"wf": "s0", "wi": "s1", "wc": "s2", "uf": "s3", "ui": "s4",
               "uc": "s5", "wo": "s0", "uo": "s3"}
    b_slot = {s: B_("slot_" + s) for s in ("s0", "s1", "s2", "s3", "s4", "s5")}
    slot_alias = {"s0": [b_wqk3[0]],
                  "s1": [b_wqk3[1], B_("sqB")],
                  "s2": [b_wqk3[2]],
                  "s3": [b_xbf], "s4": [b_hbf[0]], "s5": [b_hbf[1]]}

    def load_w(nm):
        t = wslot[nm]
        s = slot_of[nm]
        return dma("scalar", "SQ",
                   t.ap().rearrange("p (g d) -> p g d", g=G),
                   wT[nm].rearrange("(g p) d -> p g d", p=128),
                   writes=[b_slot[s]] + slot_alias[s])

    ln1_block(0)
    dma("scalar", "SQ",
        t_xbf.ap().rearrange("p (g c) -> p g c", g=G),
        xTb[1].rearrange("(g p) c -> p g c", p=128),
        writes=[b_xbf])
    qkva_chunk(0, 0)
    ln1_block(1)
    load_w("uf")
    qkva_chunk(0, 1)
    load_w("ui")
    x_trigger(0)
    dma("sync", "SY", t_uchE[:], uv_sp[0, 0, 0],
        writes=[B_("uchE"), b_sqA], deps=[uv_tok[(0, 0, 0)]])
    qkva_chunk(1, 0)
    x_combine(0)
    qkva_chunk(1, 1)
    rank1_mm("uf", 0)
    rank1_mm("ui", G)
    dma("gpsimd", "GQ", t_vchE2[:], uv_sp[0, 1, 0],
        writes=[B_("vchE2"), b_slot["s4"]], deps=[uv_tok[(0, 1, 0)]])
    x_trigger(1)
    for _nm in ("uc", "wf", "wi", "wc"):
        load_w(_nm)

    b_uch, b_vch = B_("uch"), B_("vch")
    b_abf = B_("abf")
    b_z = [[B_("z00"), B_("z01")], [B_("z10"), B_("z11")]]
    z_t = [t_z0, t_z1]
    b_c1a, b_c1b, b_c1bf = B_("c1a"), B_("c1b"), B_("c1bf")
    b_ui2, b_uc2 = B_("ui2"), B_("uc2")
    x2_tok = {}

    def c1_ap(m, c):
        t = t_c1a if m < 4 else t_c1b
        mm = m if m < 4 else m - 4
        return t[:, mm * POS + c * CH: mm * POS + (c + 1) * CH]

    def c1_buf(m):
        return b_c1a if m < 4 else b_c1b

    def uv_restream(b, c, early=False):
        if early:
            tu, tv = t_uchE, t_vchE
            bu, bv = B_("uchE"), B_("vchE")
            al = [b_slot["s1"]]
        else:
            tu, tv = t_uchb, t_vchb
            bu, bv = B_("uchb"), B_("vchb")
            al = [b_uch] + ([b_P] if (b, c) == (0, 1) else [])
        dma("sync", "SY", tu[:], uv_sp[b, 0, c],
            writes=[bu] + al, deps=[uv_tok[(b, 0, c)]])
        dma("sync", "SY", tv[:], uv_sp[b, 1, c],
            writes=[bv] + al, deps=[uv_tok[(b, 1, c)]])
        return tu, tv, bu, bv

    def B_chunk_p1(b, c, rs):
        tu, tv, bu, bv = rs
        for g in range(G):
            sl = slice(g * CH, (g + 1) * CH)
            P.op("vector",
                 lambda e, sl=sl, b=b, tu=tu, tv=tv, g=g:
                 e.scalar_tensor_tensor(
                     t_vch[:, sl], tv[:, sl],
                     t_cin[:, b * G + g: b * G + g + 1],
                     tu[:, sl], ALU.mult, ALU.add),
                 reads=[bu, bv, b_cin[b]],
                 writes=[b_vch] + (list(b_q) if g == 0 else []))
        ln_stats(t_vch, b_vch, t_abf, alias=[b_abf, b_sqA, B_("uchE")])
        rows_block(t_ceps[0:1, :], False, True)
        P.op("tensor", bcast(r_r, 0), reads=[b_rows, b_consts],
             writes=[b_pbc[0]])
        P.op("tensor", bcast(r_mr, 1), reads=[b_rows, b_consts],
             writes=[b_pbc[1]])
        P.op("vector",
             lambda e: e.tensor_tensor(g3(t_vch), g3(t_vch), bc3(pbc[0]),
                                       ALU.mult),
             reads=[b_pbc[0]], writes=[b_vch])
        P.op("vector",
             lambda e: e.tensor_tensor(g3(t_vch), g3(t_vch), bc3(pbc[1]),
                                       ALU.subtract),
             reads=[b_pbc[1]], writes=[b_vch])
        xal = [b_P] if (b, c) == (0, 0) else []
        dma("sync", "SY",
            t_uch.ap().rearrange("p (g c) -> p g c", g=G),
            xT[b].rearrange("(g p) c -> p g c", p=128)[:, :, c * CH:(c + 1) * CH],
            writes=[b_uch, B_("uchb"), B_("vchb")] + xal)
        P.op("vector",
             lambda e: e.tensor_tensor(t_vch[:], t_vch[:], t_uch[:], ALU.add),
             reads=[b_uch], writes=[b_vch])
        x2_tok[(b, c)] = dma("scalar", "SQ", x2_sp[b, c], t_vch[:],
                             reads=[b_vch])

    def B_chunk_p2(b, c):
        ln_stats(t_vch, b_vch, t_abf, alias=[b_abf])
        rows_block(t_ceps[0:1, :], False, False)
        P.op("tensor", bcast(r_m, 0), reads=[b_rows, b_consts],
             writes=[b_pbc[0]])
        P.op("tensor", bcast(r_r, 1), reads=[b_rows, b_consts],
             writes=[b_pbc[1]])
        P.op("vector",
             lambda e: e.tensor_tensor(g3(t_uch), g3(t_vch), bc3(pbc[0]),
                                       ALU.subtract),
             reads=[b_vch, b_pbc[0]], writes=[b_uch])
        extra_w = (list(b_a) + [b_ubf, b_vbf]) if b == 0 else list(b_kv)
        P.op("vector",
             lambda e, b=b, c=c: e.tensor_tensor(
                 g3(z_t[b], c0=c * CH, width=POS), g3(t_uch), bc3(pbc[1]),
                 ALU.mult),
             reads=[b_uch, b_pbc[1]], writes=[b_z[b][c]] + extra_w)

    def gate_mms(wt, m, acts_tile, c, bank, extra=None):
        def f(e, wt=wt, m=m, c=c, bank=bank, extra=extra):
            ins = None
            n = G if extra is None else 2 * G
            k = 0
            for g in range(G):
                ins = e.matmul(pmm[bank][:, 0:CH],
                               wt[:, g * D + m * 128: g * D + (m + 1) * 128],
                               acts_tile[:, g * POS + c * CH:
                                         g * POS + (c + 1) * CH],
                               start=(k == 0), stop=(k == n - 1))
                k += 1
            if extra is not None:
                wt2, acts2 = extra
                for g in range(G):
                    ins = e.matmul(pmm[bank][:, 0:CH],
                                   wt2[:, g * D + m * 128: g * D + (m + 1) * 128],
                                   acts2[:, g * POS + c * CH:
                                         g * POS + (c + 1) * CH],
                                   start=False, stop=(k == n - 1))
                    k += 1
            return ins
        return f

    def step0_f(c, ms):
        for m in ms:
            bank = (c * G + m) % 4
            P.op("tensor", gate_mms(t_s0, m, t_z0, c, bank),
                 reads=[b_slot["s0"], b_z[0][c], B_("b0f")],
                 writes=[b_pmm[bank]])
            P.op("scalar",
                 lambda e, m=m, bank=bank: e.activation(
                     t_sg[0][:], pmm[bank][:, 0:CH], AF.Sigmoid,
                     bias=t_b0["f"][:, m:m + 1]),
                 reads=[b_pmm[bank], B_("b0f")], writes=[B_("sg0")])
            al = [b_y] if m < 4 else [b_slot["s4"], B_("vchE2")]
            P.op("vector",
                 lambda e, m=m, c=c: e.tensor_scalar_mul(
                     c1_ap(m, c), t_sg[0][:], t_b["inc"][:, m:m + 1]),
                 reads=[B_("sg0"), B_("b_inc")], writes=[c1_buf(m)] + al)

    def step0_ig(c, ms):
        for m in ms:
            bank_i = (c * G + m) % 2
            bank_g = 2 + (c * G + m) % 2
            P.op("tensor", gate_mms(t_s1, m, t_z0, c, bank_i),
                 reads=[b_slot["s1"], b_z[0][c], B_("b0i")],
                 writes=[b_pmm[bank_i]])
            P.op("tensor", gate_mms(t_s2, m, t_z0, c, bank_g),
                 reads=[b_slot["s2"], b_z[0][c], B_("b0c")],
                 writes=[b_pmm[bank_g]])
            P.op("scalar",
                 lambda e, m=m, bank_i=bank_i: e.activation(
                     t_sg[1][:], pmm[bank_i][:, 0:CH], AF.Sigmoid,
                     bias=t_b0["i"][:, m:m + 1]),
                 reads=[b_pmm[bank_i], B_("b0i")], writes=[B_("sg1")])
            P.op("scalar",
                 lambda e, m=m, bank_g=bank_g: e.activation(
                     t_sg[2][:], pmm[bank_g][:, 0:CH], AF.Tanh,
                     bias=t_b0["c"][:, m:m + 1]),
                 reads=[b_pmm[bank_g], B_("b0c")], writes=[B_("sg2")])
            P.op("vector",
                 lambda e: e.tensor_tensor(t_sg[1][:], t_sg[1][:], t_sg[2][:],
                                           ALU.mult),
                 reads=[B_("sg2")], writes=[B_("sg1")])
            P.op("vector",
                 lambda e, m=m, c=c: e.tensor_tensor(
                     c1_ap(m, c), c1_ap(m, c), t_sg[1][:], ALU.add),
                 reads=[B_("sg1")], writes=[c1_buf(m)])

    b_c1bfc = [B_("c1bfc0"), B_("c1bfc1")]

    def c1bf_cast(c):
        al = [b_slot["s5"]] if c == 0 else []
        row_d = G * POS
        for half, (t_src, bsrc) in enumerate(((t_c1a, b_c1a),
                                              (t_c1b, b_c1b))):
            row_s = 4 * POS
            P.op("vector",
                 lambda e, half=half, t_src=t_src, c=c: e.tensor_copy(
                     bass.AP(t_c1bf, half * 4 * POS + c * CH,
                             [[row_d, 128], [POS, 4], [1, CH]]),
                     bass.AP(t_src, c * CH, [[row_s, 128], [POS, 4], [1, CH]])),
                 reads=[bsrc], writes=[b_c1bfc[c], b_c1bf] + al)

    def step1_f(c, ms):
        for m in ms:
            bank = (c * G + m) % 4
            P.op("tensor", gate_mms(t_s0, m, t_z1, c, bank,
                                    extra=(t_s3, t_c1bf)),
                 reads=[b_slot["s0"], b_slot["s3"], b_z[1][c], b_c1bfc[c]],
                 writes=[b_pmm[bank]])
            P.op("scalar",
                 lambda e, m=m, bank=bank: e.activation(
                     t_sg[0][:], pmm[bank][:, 0:CH], AF.Sigmoid,
                     bias=t_b["bf"][:, m:m + 1]),
                 reads=[b_pmm[bank], B_("b_bf")], writes=[B_("sg0")])
            P.op("vector",
                 lambda e, m=m, c=c: e.tensor_tensor(
                     c1_ap(m, c), c1_ap(m, c), t_sg[0][:], ALU.mult),
                 reads=[B_("sg0")], writes=[c1_buf(m)])

    def step1_ig(c, ms):
        for m in ms:
            bank_i = (c * G + m) % 2
            bank_g = 2 + (c * G + m) % 2
            P.op("tensor", gate_mms(t_s1, m, t_z1, c, bank_i,
                                    extra=(t_ui2, t_c1bf)),
                 reads=[b_slot["s1"], b_ui2, b_z[1][c], b_c1bfc[c]],
                 writes=[b_pmm[bank_i]])
            P.op("tensor", gate_mms(t_s2, m, t_z1, c, bank_g,
                                    extra=(t_uc2, t_c1bf)),
                 reads=[b_slot["s2"], b_uc2, b_z[1][c], b_c1bfc[c]],
                 writes=[b_pmm[bank_g]])
            P.op("scalar",
                 lambda e, m=m, bank_i=bank_i: e.activation(
                     t_sg[1][:], pmm[bank_i][:, 0:CH], AF.Sigmoid,
                     bias=t_b["bi"][:, m:m + 1]),
                 reads=[b_pmm[bank_i], B_("b_bi")], writes=[B_("sg1")])
            P.op("scalar",
                 lambda e, m=m, bank_g=bank_g: e.activation(
                     t_sg[2][:], pmm[bank_g][:, 0:CH], AF.Tanh,
                     bias=t_b["bc"][:, m:m + 1]),
                 reads=[b_pmm[bank_g], B_("b_bc")], writes=[B_("sg2")])
            P.op("vector",
                 lambda e: e.tensor_tensor(t_sg[1][:], t_sg[1][:], t_sg[2][:],
                                           ALU.mult),
                 reads=[B_("sg2")], writes=[B_("sg1")])
            P.op("vector",
                 lambda e, m=m, c=c: e.tensor_tensor(
                     c1_ap(m, c), c1_ap(m, c), t_sg[1][:], ALU.add),
                 reads=[B_("sg1")], writes=[c1_buf(m)])


    B_chunk_p1(0, 0, (t_uchE, t_vchE2, B_("uchE"), B_("vchE2")))
    rank1_mm("uc", 2 * G)
    rank1_fin()
    dbg_dump("b0f", t_b0["f"][:], [B_("b0f")])
    B_chunk_p2(0, 0)
    x_combine(1)
    rs01 = uv_restream(0, 1)
    step0_f(0, range(0, 5))
    B_chunk_p1(0, 1, rs01)
    step0_f(0, range(5, 8))
    step0_ig(0, range(0, 3))
    B_chunk_p2(0, 1)
    rs10 = uv_restream(1, 0)
    step0_ig(0, range(3, 5))
    B_chunk_p1(1, 0, rs10)
    step0_ig(0, range(5, 8))
    step0_f(1, range(0, 2))
    B_chunk_p2(1, 0)
    rs11 = uv_restream(1, 1)
    step0_f(1, range(2, 8))
    step0_ig(1, range(0, 1))
    B_chunk_p1(1, 1, rs11)
    step0_ig(1, range(1, 5))
    B_chunk_p2(1, 1)
    step0_ig(1, range(5, 8))
    c1bf_cast(0)
    step1_f(0, range(0, 4))
    dbg_dump("z0", t_z0[:], b_z[0])
    dbg_dump("z1", t_z1[:], b_z[1])

    dbg_dump("c1", t_c1bf[:], [b_c1bf])
    dma("scalar", "SQ",
        t_ui2.ap().rearrange("p (g d) -> p g d", g=G),
        wT["ui"].rearrange("(g p) d -> p g d", p=128),
        writes=[b_ui2, b_uch])
    dma("scalar", "SQ",
        t_uc2.ap().rearrange("p (g d) -> p g d", g=G),
        wT["uc"].rearrange("(g p) d -> p g d", p=128),
        writes=[b_uc2, b_vch] + list(b_q))

    step1_f(0, range(4, G))
    c1bf_cast(1)
    step1_f(1, range(G))
    load_w("wo")
    load_w("uo")
    step1_ig(0, range(G))
    step1_ig(1, range(G))

    P.op("vector", lambda e: e.tensor_copy(t_c1bf[:, 0:4 * POS], t_c1a[:]),
         reads=[b_c1a], writes=[b_c1bf])
    P.op("vector", lambda e: e.tensor_copy(t_c1bf[:, 4 * POS:], t_c1b[:]),
         reads=[b_c1b], writes=[b_c1bf])
    dbg_dump("c2", t_c1bf[:], [b_c1bf])

    b_hD = [B_("hD0"), B_("hD1")]
    b_x2r = [B_("x2r0"), B_("x2r1")]
    b_hbfD = B_("hbfD")
    b_sqD = B_("sqD")
    hD_t = [t_hD, t_hD2]
    x2r_t = [t_x2r, t_x2r2]
    hD_alias = [[b_c1a], [b_c1b]]
    x2r_alias = [[b_ui2], [b_uc2]]
    st_toks = []
    chunks = [(b, c) for b in range(B) for c in range(NCH)]

    def D_compute(idx, ms):
        b, c = chunks[idx]
        par = idx % 2
        thD, tx2 = hD_t[par], x2r_t[par]
        bhD, bx2 = b_hD[par], b_x2r[par]
        if ms[0] == 0:
            dma("sync", "SY", tx2[:], x2_sp[b, c],
                writes=[bx2] + x2r_alias[par], deps=[x2_tok[(b, c)]])
        for m in ms:
            bank = (c * G + m) % 4
            P.op("tensor", gate_mms(t_s0, m, z_t[b], c, bank,
                                    extra=(t_s3, t_c1bf)),
                 reads=[b_slot["s0"], b_slot["s3"], b_z[b][c], b_c1bf],
                 writes=[b_pmm[bank]])
            P.op("scalar",
                 lambda e, m=m, bank=bank, thD=thD: e.activation(
                     thD[:, m * CH:(m + 1) * CH], pmm[bank][:, 0:CH],
                     AF.Tanh, bias=t_b["bo"][:, m:m + 1]),
                 reads=[b_pmm[bank], B_("b_bo")],
                 writes=[bhD] + hD_alias[par])
        if b == 0 and c == 0:
            dbg_dump("hD00", thD[:], [bhD])

    def D_ln_stats(idx):
        b, c = chunks[idx]
        par = idx % 2
        thD = hD_t[par]
        bhD = b_hD[par]
        ln_stats(thD, bhD, t_hbfD, alias=[b_hbfD, b_abf], eng="vector")
        rows_block(t_ceps[0:1, :], False, True)

    def D_ln_tail(idx):
        b, c = chunks[idx]
        par = idx % 2
        thD, tx2 = hD_t[par], x2r_t[par]
        bhD, bx2 = b_hD[par], b_x2r[par]
        P.op("tensor", bcast(r_r, 0), reads=[b_rows, b_consts],
             writes=[b_pbc[0]])
        P.op("tensor", bcast(r_mr, 1), reads=[b_rows, b_consts],
             writes=[b_pbc[1]])
        P.op("vector",
             lambda e, thD=thD: e.tensor_tensor(g3(thD), g3(thD), bc3(pbc[0]),
                                                ALU.mult),
             reads=[b_pbc[0]], writes=[bhD])
        P.op("vector",
             lambda e, thD=thD: e.tensor_tensor(g3(thD), g3(thD), bc3(pbc[1]),
                                                ALU.subtract),
             reads=[b_pbc[1]], writes=[bhD])
        P.op("vector",
             lambda e, thD=thD, tx2=tx2: e.tensor_tensor(
                 tx2[:], thD[:], tx2[:], ALU.add),
             reads=[bhD], writes=[bx2])
        st_toks.append(
            dma("scalar", "SQ",
                outT[b].rearrange("(g p) c -> p g c", p=128)[:, :, c * CH:(c + 1) * CH],
                tx2.ap().rearrange("p (g c) -> p g c", g=G),
                reads=[bx2]))

    D_compute(0, range(G))
    for idx in range(1, len(chunks)):
        D_compute(idx, range(0, 6))
        D_ln_stats(idx - 1)
        D_compute(idx, range(6, 8))
        D_ln_tail(idx - 1)
    D_ln_stats(len(chunks) - 1)
    D_ln_tail(len(chunks) - 1)

    P.seq["scalar"].append((
        [(s, v) for (_e, s, v) in st_toks], lambda e: None, None, 0))

    nc._rec = []
    nc._plan = P
    with nc.Block() as block:

        @block.tensor
        def _(eng):
            P.emit("tensor", eng, sems, nc, nc._rec)

        @block.vector
        def _(eng):
            P.emit("vector", eng, sems, nc, nc._rec)

        @block.scalar
        def _(eng):
            P.emit("scalar", eng, sems, nc, nc._rec)

        @block.gpsimd
        def _(eng):
            P.emit("gpsimd", eng, sems, nc, nc._rec)

        @block.sync
        def _(eng):
            P.emit("sync", eng, sems, nc, nc._rec)

    return nc


_NC_CACHE = {}


def kernel(L0_attn_x, L0_attn_w_qkva, L0_attn_gamma,
           L0_ffn_wf, L0_ffn_uf, L0_ffn_bf, L0_ffn_wi, L0_ffn_ui, L0_ffn_bi,
           L0_ffn_wc, L0_ffn_uc, L0_ffn_bc, L0_ffn_wo, L0_ffn_uo, L0_ffn_bo,
           L0_ffn_in_c):
    x = np.asarray(L0_attn_x, dtype=np.float32)
    SEQ = x.shape[1]
    PC = SEQ // NCORES

    def feat(v):
        return np.ascontiguousarray(np.asarray(v, np.float32).reshape(G, 128).T)

    def tbf(v):
        b = np.ascontiguousarray(v, np.float32)
        return b.view(np.uint16).reshape(*b.shape, 2)[..., 1].copy().view(
            ml_dtypes.bfloat16)

    shared = {
        "wqkT": tbf(np.asarray(L0_attn_w_qkva, np.float32).T),
        "gamma": feat(L0_attn_gamma),
        "wfT": tbf(np.asarray(L0_ffn_wf, np.float32).T),
        "wiT": tbf(np.asarray(L0_ffn_wi, np.float32).T),
        "wcT": tbf(np.asarray(L0_ffn_wc, np.float32).T),
        "woT": tbf(np.asarray(L0_ffn_wo, np.float32).T),
        "ufT": tbf(np.asarray(L0_ffn_uf, np.float32).T),
        "uiT": tbf(np.asarray(L0_ffn_ui, np.float32).T),
        "ucT": tbf(np.asarray(L0_ffn_uc, np.float32).T),
        "uoT": tbf(np.asarray(L0_ffn_uo, np.float32).T),
        "bf_b": feat(L0_ffn_bf), "bi_b": feat(L0_ffn_bi),
        "bc_b": feat(L0_ffn_bc), "bo_b": feat(L0_ffn_bo),
        "inc_b": feat(L0_ffn_in_c),
    }
    in_maps = []
    for k in range(NCORES):
        m = dict(shared)
        m["x0T"] = np.ascontiguousarray(x[0, k * PC:(k + 1) * PC, :].T)
        m["x1T"] = np.ascontiguousarray(x[1, k * PC:(k + 1) * PC, :].T)
        m["x0Tb"] = tbf(m["x0T"])
        m["x1Tb"] = tbf(m["x1T"])
        mlt = np.zeros((128, G), np.float32)
        mlt[:, :k] = 1.0
        m["mask_lt"] = mlt
        m["mask_c"] = np.ascontiguousarray(1.0 - mlt)
        in_maps.append(m)

    if "nc" not in _NC_CACHE:
        _NC_CACHE["nc"] = build_nc()
    res = run_bass_kernel_spmd(_NC_CACHE["nc"], in_maps,
                               core_ids=list(range(NCORES)))
    _NC_CACHE["res"] = res

    out = np.empty((B, SEQ, D), np.float32)
    for k in range(NCORES):
        ot = res.results[k]["outT"]
        out[0, k * PC:(k + 1) * PC, :] = ot[0].T
        out[1, k * PC:(k + 1) * PC, :] = ot[1].T
    return out


if __name__ == "__main__":
    build_nc()
    print("built ok")


# version 41
# speedup vs baseline: 1.0279x; 1.0078x over previous
import numpy as np
import ml_dtypes
import concourse.bass as bass
import concourse.mybir as mybir
from concourse.bass_utils import run_bass_kernel_spmd

F32 = mybir.dt.float32
BF16 = mybir.dt.bfloat16
AF = mybir.ActivationFunctionType
ALU = mybir.AluOpType

NCORES = 8
D = 1024
G = 8
POS = 1024
B = 2
CH = 512
NCH = POS // CH
KB = 1024

ENG_SEM = {"tensor": "T", "vector": "V", "scalar": "S", "gpsimd": "G", "sync": "Y"}


class Buf:
    __slots__ = ("name", "w", "r")

    def __init__(self, name):
        self.name = name
        self.w = None
        self.r = []


class Plan:
    def __init__(self):
        self.seq = {e: [] for e in ENG_SEM}
        self.cnt = {}
        self.rings = {}

    def ring(self, name, n):
        self.rings[name] = {"sems": [f"{name}_{i}" for i in range(n)], "i": 0,
                            "last": [None] * n}

    def op(self, eng, fn, reads=(), writes=(), deps=(), queue=None, sem=None,
           inc=None):
        dep_list = []
        for b in reads:
            if b.w is not None:
                dep_list.append(b.w)
        for b in writes:
            if b.w is not None:
                dep_list.append(b.w)
            dep_list.extend(b.r)
        dep_list.extend(t for t in deps if t is not None)
        slot = None
        if queue is not None:
            rg = self.rings[queue]
            slot = rg["i"]
            rg["i"] = (slot + 1) % len(rg["sems"])
            sem = rg["sems"][slot]
            inc = 16
            if rg["last"][slot] is not None:
                dep_list.append(rg["last"][slot])
            async_tok = True
        elif sem is not None:
            async_tok = True
        else:
            sem = ENG_SEM[eng]
            inc = 1
            async_tok = False
        waits = []
        for (teng, tsem, tval) in dep_list:
            if teng == eng:
                continue
            waits.append((tsem, tval))
        self.cnt[sem] = self.cnt.get(sem, 0) + inc
        tok = (None if async_tok else eng, sem, self.cnt[sem])
        self.seq[eng].append((waits, fn, sem, inc))
        if queue is not None:
            self.rings[queue]["last"][slot] = tok
        for b in reads:
            b.r.append(tok)
        for b in writes:
            b.w = tok
            b.r = []
        return tok

    def emit(self, eng_name, eng, sems, nc=None, rec=None):
        hw = {}
        for oi, (waits, fn, sem, inc) in enumerate(self.seq[eng_name]):
            emitted = []
            for (sname, val) in waits:
                if hw.get(sname, 0) >= val:
                    continue
                eng.wait_ge(sems[sname], val)
                hw[sname] = val
                emitted.append((sname, val))
            n0 = nc.n_instructions() if nc is not None else 0
            inst = fn(eng)
            if inst is not None and sem is not None:
                inst.then_inc(sems[sem], inc)
            if rec is not None:
                rec.append((eng_name, oi, nc.n_instructions() - n0,
                            emitted, sem, inc))


_DEBUG = False
_NO_CC = False


def build_nc(dbg=False):
    nc = bass.Bass("TRN2", target_bir_lowering=False)

    xT = [nc.dram_tensor(f"x{b}T", [D, POS], F32, kind="ExternalInput").ap()
          for b in range(B)]
    xTb = [nc.dram_tensor(f"x{b}Tb", [D, POS], BF16, kind="ExternalInput").ap()
           for b in range(B)]
    wqkT_d = nc.dram_tensor("wqkT", [D, 3 * D], BF16, kind="ExternalInput").ap()
    gamma_in = nc.dram_tensor("gamma", [128, G], F32, kind="ExternalInput").ap()
    wT = {}
    for nm in ("wf", "wi", "wc", "wo", "uf", "ui", "uc", "uo"):
        wT[nm] = nc.dram_tensor(nm + "T", [D, D], BF16, kind="ExternalInput").ap()
    bias_in = {}
    for nm in ("bf", "bi", "bc", "bo", "inc"):
        bias_in[nm] = nc.dram_tensor(nm + "_b", [128, G], F32,
                                     kind="ExternalInput").ap()
    mlt_in = nc.dram_tensor("mask_lt", [128, G], F32, kind="ExternalInput").ap()
    mlc_in = nc.dram_tensor("mask_c", [128, G], F32, kind="ExternalInput").ap()
    outT = nc.dram_tensor("outT", [B, D, POS], F32, kind="ExternalOutput").ap()

    _spk = "ExternalOutput" if dbg else "Internal"
    uv_sp = nc.dram_tensor("uv_sp", [B, 2, NCH, 128, G * CH], BF16,
                           kind=_spk).ap()
    x2_sp = nc.dram_tensor("x2_sp", [B, NCH, 128, G * CH], F32, kind=_spk).ap()
    cc_in = [nc.dram_tensor(f"cc_in{b}", [128, 16], F32) for b in range(B)]
    cc_out = [nc.dram_tensor(f"cc_out{b}", [NCORES, 128, 16], F32)
              for b in range(B)]
    cc_all_in = nc.dram_tensor("cc_all_in", [128, 32], F32)
    cc_all_out = nc.dram_tensor("cc_all_out", [NCORES, 128, 32], F32)
    cc_wrm = nc.dram_tensor("cc_wrm", [128, 16], F32)
    cc_wrm_o = nc.dram_tensor("cc_wrm_o", [NCORES, 128, 16], F32)

    SB_BASE = nc.sbuf_top - nc.sbuf_bytes_remaining
    SB_BASE = (SB_BASE + 127) // 128 * 128
    assert nc.sbuf_bytes_remaining >= 205 * KB, nc.sbuf_bytes_remaining

    def at(name, shape, dtype, off):
        return nc.alloc_sbuf_tensor_at(name, shape, dtype, offset=SB_BASE + off)

    o = 0

    def bump(name, shape, dtype, align=32):
        nonlocal o
        es = mybir.dt.size(dtype)
        o = (o + align - 1) // align * align
        h = at(name, shape, dtype, o)
        n = 1
        for s in shape[1:]:
            n *= s
        o += n * es
        return h

    t_gamma = bump("gammat", [128, G], F32)
    t_b = {nm: bump(nm + "t", [128, G], F32)
           for nm in ("bf", "bi", "bc", "bo", "inc")}
    t_incbf = bump("incbf", [128, G], BF16)
    t_mlt = bump("mltt", [128, G], F32)
    t_mlc = bump("mlct", [128, G], F32)
    t_cy = bump("cyt", [128, 2 * G], F32)
    t_cp = bump("cpt", [128, 2 * G], F32)
    t_cin = bump("cint", [128, 2 * G], F32)
    t_cst = bump("cstt", [128, 2 * G], F32)
    t_tmpA = bump("tmpAt", [128, 2 * G], F32)
    t_tmpY = bump("tmpYt", [128, 2 * G], F32)
    t_vr = {nm: bump("vr" + nm, [128, G], F32) for nm in ("f", "i", "c")}
    t_b0 = {nm: bump("b0" + nm, [128, G], F32) for nm in ("f", "i", "c")}
    t_summ = bump("summt", [128, 32], F32)
    t_gath = bump("gatht", [128, 256], F32)
    t_onescol = bump("onescol", [128, 1], BF16)
    t_onesrow = bump("onesrow", [1, 128], F32)
    t_ones512 = bump("ones512", [128, CH], F32)
    t_ceps = bump("cepst", [1, 1], F32)
    t_cmax = bump("cmaxt", [1, 1], F32)
    r_m = bump("r_m", [1, CH], F32)
    r_v = bump("r_v", [1, CH], F32)
    r_r = bump("r_r", [1, CH], F32)
    r_mr = bump("r_mr", [1, CH], F32)
    t_sg = [bump(f"sg{i}", [128, CH], F32) for i in range(3)]
    t_cbA = bump("cbA", [128, 72], F32)
    t_cbY = bump("cbY", [128, 72], F32)
    t_cbS = bump("cbS", [128, 72], F32)
    assert o <= 20 * KB, f"small region overflow: {o}"

    t_wqk = at("wqk", [128, G * 3 * D], BF16, 20 * KB)
    t_s0 = at("s0", [128, G * D], BF16, 20 * KB)
    t_s1 = at("s1", [128, G * D], BF16, 36 * KB)
    t_sqB = at("sqB", [128, G * CH], BF16, 36 * KB)
    t_s2 = at("s2", [128, G * D], BF16, 52 * KB)
    t_xbf = at("xbf", [128, G * POS], BF16, 68 * KB)
    t_s3 = at("s3", [128, G * D], BF16, 68 * KB)
    t_hbf0 = at("hbf0", [128, G * POS], BF16, 84 * KB)
    t_s4 = at("s4", [128, G * D], BF16, 84 * KB)
    t_vchE2 = at("vchE2", [128, G * CH], BF16, 92 * KB)
    t_hD = at("hD", [128, G * CH], F32, 84 * KB)
    t_hbf1 = at("hbf1", [128, G * POS], BF16, 100 * KB)
    t_s5 = at("s5", [128, G * D], BF16, 100 * KB)
    t_x2r = at("x2r", [128, G * CH], F32, 100 * KB)
    t_sqA = at("sqA", [128, G * CH], BF16, 116 * KB)
    t_xm = at("xmscr", [128, G * (CH // 2)], F32, 116 * KB)
    t_uchE = at("uchE", [128, G * CH], BF16, 116 * KB)
    t_abf = at("abf", [128, G * CH], BF16, 116 * KB)
    t_hbfD = at("hbfD", [128, G * CH], BF16, 116 * KB)
    t_y = at("ych", [128, G * CH], F32, 124 * KB)
    t_c1a = at("c1a", [128, 4 * POS], F32, 124 * KB)
    t_hD = at("hD", [128, G * CH], F32, 124 * KB)
    t_P = at("Pch", [128, G * CH], F32, 140 * KB)
    t_uch = at("uch", [128, G * CH], F32, 140 * KB)
    t_uchb = at("uchb", [128, G * CH], BF16, 140 * KB)
    t_vchb = at("vchb", [128, G * CH], BF16, 148 * KB)
    t_ui2 = at("ui2", [128, G * D], BF16, 140 * KB)
    t_x2r = at("x2r", [128, G * CH], F32, 140 * KB)
    t_q = at("qch", [128, G * CH], F32, 156 * KB)
    t_vch = at("vch", [128, G * CH], F32, 156 * KB)
    t_uc2 = at("uc2", [128, G * D], BF16, 156 * KB)
    t_x2r2 = at("x2r2", [128, G * CH], F32, 156 * KB)
    t_kv = at("kvch", [128, G * CH], F32, 172 * KB)
    t_z1 = at("z1", [128, G * POS], BF16, 172 * KB)
    t_a = at("ach", [128, G * CH], F32, 188 * KB)
    t_ubf = at("ubf", [128, G * CH], BF16, 188 * KB)
    t_vbf = at("vbf", [128, G * CH], BF16, 196 * KB)
    t_z0 = at("z0", [128, G * POS], BF16, 188 * KB)
    t_c1b = at("c1b", [128, 4 * POS], F32, 84 * KB)
    t_hD2 = at("hD2", [128, G * CH], F32, 84 * KB)
    t_c1bf = at("c1bf", [128, G * POS], BF16, 100 * KB)

    pmm = [nc.alloc_psum_tensor(f"pmm{i}", [128, 512], F32) for i in range(4)]
    pstat = [nc.alloc_psum_tensor(f"pstat{i}", [1, 512], F32) for i in range(2)]
    pbc = [nc.alloc_psum_tensor(f"pbc{i}", [128, 512], F32) for i in range(2)]

    sems = {}
    for s in ("T", "V", "S", "G", "Y", "CC"):
        sems[s] = nc.alloc_semaphore(s)
    P = Plan()
    P.ring("SY", 6)
    P.ring("SQ", 4)
    P.ring("GQ", 2)
    for rg in ("SY", "SQ", "GQ"):
        for s in P.rings[rg]["sems"]:
            sems[s] = nc.alloc_semaphore(s)

    bufs = {}

    def B_(name):
        if name not in bufs:
            bufs[name] = Buf(name)
        return bufs[name]

    def dma(eng, queue, dst_ap, src_ap, reads=(), writes=(), deps=()):
        return P.op(eng, lambda e: e.dma_start(dst_ap, src_ap),
                    reads=reads, writes=writes, deps=deps, queue=queue)

    def vguard(bufs_to_stamp):
        P.op("vector", lambda e: e.memset(t_sg[2][:], 0.0),
             reads=[], writes=[B_("sg2")])
        P.op("vector", lambda e: e.memset(t_sg[2][:], 0.0),
             reads=[], writes=[B_("sg2")] + list(bufs_to_stamp))

    dbg_names = []

    def dbg_dump(name, src_ap, reads):
        if not dbg:
            return
        shp = list(src_ap.shape)
        dt = src_ap.dtype
        d = nc.dram_tensor("dbg_" + name, shp, dt, kind="ExternalOutput").ap()
        dbg_names.append("dbg_" + name)
        dma("sync", "SY", d, src_ap, reads=list(reads))

    def g3(t, c0=0, ncc=CH, width=CH, ng=G):
        ap = t.ap()
        row = ap.ap[0][0]
        return bass.AP(t, c0, [[row, 128], [width, ng], [1, ncc]])

    def bc3(pt, c0=0, ncc=CH, ng=G):
        ap = pt.ap()
        row = ap.ap[0][0]
        return bass.AP(pt, c0, [[row, 128], [0, ng], [1, ncc]])

    b_pstat = [B_("pstat0"), B_("pstat1")]
    b_pbc = [B_("pbc0"), B_("pbc1")]
    b_pmm = [B_(f"pmm{i}") for i in range(4)]
    b_rows = B_("rows")
    b_consts = B_("consts")
    b_params = B_("params")

    b_xbf = B_("xbf")
    b_wqk = B_("wqk")
    b_wqk3 = [B_("wqk3a"), B_("wqk3b"), B_("wqk3c")]
    dma("scalar", "SQ",
        t_xbf.ap().rearrange("p (g c) -> p g c", g=G),
        xTb[0].rearrange("(g p) c -> p g c", p=128),
        writes=[b_xbf])
    dma("scalar", "SQ",
        t_wqk.ap().rearrange("p (g o) -> p g o", g=G),
        wqkT_d.rearrange("(g p) o -> p g o", p=128),
        writes=[b_wqk] + b_wqk3)
    dma("sync", "SY", t_gamma[:], gamma_in, writes=[b_params])
    for nm in ("bf", "bi", "bc", "bo", "inc"):
        dma("sync", "SY", t_b[nm][:], bias_in[nm], writes=[B_("b_" + nm)])
    dma("sync", "SY", t_mlt[:], mlt_in, writes=[B_("masks")])
    dma("sync", "SY", t_mlc[:], mlc_in, writes=[B_("masks")])

    def consts(e):
        e.memset(t_onescol[:], 1.0 / 1024.0)
        e.memset(t_onesrow[:], 1.0)
        e.memset(t_ones512[:], 1.0)
        e.memset(t_ceps[:], 1e-5)
        e.memset(t_cmax[:], 9.5e-28)
        e.memset(t_cy[:], 0.0)
        e.memset(t_cp[:], 1.0)
        return e.memset(t_cst[:], 0.0)
    P.op("vector", consts, writes=[b_consts, B_("cy"), B_("cp"), B_("cst")])

    P.op("vector", lambda e: e.tensor_copy(t_incbf[:], t_b["inc"][:]),
         reads=[B_("b_inc")], writes=[B_("incbf")])


    def rows_block(eps_ap, floor, want_mr):
        er = "vector"
        P.op(er, lambda e: e.tensor_copy(r_m[:], pstat[0][0:1, :]),
             reads=[b_pstat[0]], writes=[b_rows])

        def f_var(e):
            e.tensor_tensor(r_v[:], r_m[:], r_m[:], ALU.mult)
            return e.tensor_tensor(r_v[:], pstat[1][0:1, :], r_v[:], ALU.subtract)
        P.op(er, f_var, reads=[b_pstat[1]], writes=[b_rows])
        if floor:
            P.op("vector",
                 lambda e: e.tensor_scalar(r_v[:], r_v[:], t_cmax[0:1, :], None,
                                           ALU.max),
                 reads=[b_consts], writes=[b_rows])

        def f_ln(e):
            if eps_ap is None:
                e.activation(r_r[:], r_v[:], AF.Ln)
            else:
                e.activation(r_r[:], r_v[:], AF.Ln, bias=eps_ap)
            return e.activation(r_r[:], r_r[:], AF.Exp, scale=-0.5)
        P.op("scalar", f_ln, reads=[b_rows, b_consts], writes=[b_rows])
        if want_mr:
            P.op(er, lambda e: e.tensor_tensor(r_mr[:], r_m[:], r_r[:],
                                               ALU.mult),
                 reads=[b_rows], writes=[b_rows])

    def bcast(row_tile, which):
        def f(e):
            return e.matmul(pbc[which][:, :], t_onesrow[:], row_tile[:],
                            start=True, stop=True)
        return f

    def stat_sum(src_tile, src_buf):
        def fsum(e):
            ins = None
            for g in range(G):
                ins = e.matmul(pstat[0][0:1, :], t_onescol[:],
                               src_tile[:, g * CH:(g + 1) * CH],
                               start=(g == 0), stop=(g == G - 1))
            return ins
        P.op("tensor", fsum, reads=[src_buf, b_consts], writes=[b_pstat[0]])

    def stat_sq(sq_tile, sq_buf):
        def fsq(e):
            ins = None
            for g in range(G):
                ins = e.matmul(pstat[1][0:1, :], t_onescol[:],
                               sq_tile[:, g * CH:(g + 1) * CH],
                               start=(g == 0), stop=(g == G - 1))
            return ins
        P.op("tensor", fsq, reads=[sq_buf, b_consts], writes=[b_pstat[1]])

    b_abfh = [B_("abfh0"), B_("abfh1")]

    def stat_half(bank, tile, b_half, h, start):
        lo = h * 4 * CH

        def f(e, lo=lo, bank=bank, start=start):
            ins = None
            for g in range(4):
                ins = e.matmul(pstat[bank][0:1, :], t_onescol[:],
                               tile[:, lo + g * CH: lo + (g + 1) * CH],
                               start=(start and g == 0),
                               stop=(not start and g == 3))
            return ins
        P.op("tensor", f, reads=[b_half, b_consts], writes=[b_pstat[bank]])

    def ln_stats(src_tile, b_src, abf_tile, alias=(), eng="scalar"):
        H = 4 * CH

        def eng_of(h):
            if eng == "split":
                return "vector" if h == 0 else "scalar"
            return eng

        def cast_f(e, h):
            if eng_of(h) == "scalar":
                return e.activation(abf_tile[:, h * H:(h + 1) * H],
                                    src_tile[:, h * H:(h + 1) * H], AF.Copy)
            return e.tensor_copy(abf_tile[:, h * H:(h + 1) * H],
                                 src_tile[:, h * H:(h + 1) * H])

        def sq_f(e, h):
            if eng_of(h) == "scalar":
                return e.activation(abf_tile[:, h * H:(h + 1) * H],
                                    src_tile[:, h * H:(h + 1) * H], AF.Square)
            return e.tensor_tensor(abf_tile[:, h * H:(h + 1) * H],
                                   src_tile[:, h * H:(h + 1) * H],
                                   src_tile[:, h * H:(h + 1) * H], ALU.mult)
        for h in range(2):
            P.op(eng_of(h), lambda e, h=h: cast_f(e, h),
                 reads=[b_src], writes=[b_abfh[h]] + list(alias))
        for h in range(2):
            stat_half(0, abf_tile, b_abfh[h], h, start=(h == 0))
        for h in range(2):
            P.op(eng_of(h), lambda e, h=h: sq_f(e, h),
                 reads=[b_src], writes=[b_abfh[h]])
        for h in range(2):
            stat_half(1, abf_tile, b_abfh[h], h, start=(h == 0))

    def stat_pair(src_tile, src_buf, src_off, src_w, sq_tile, sq_buf):
        def fsum(e):
            ins = None
            for g in range(G):
                ins = e.matmul(pstat[0][0:1, :], t_onescol[:],
                               src_tile[:, g * src_w + src_off:
                                        g * src_w + src_off + CH],
                               start=(g == 0), stop=(g == G - 1))
            return ins
        P.op("tensor", fsum, reads=[src_buf, b_consts], writes=[b_pstat[0]])

        def fsq(e):
            ins = None
            for g in range(G):
                ins = e.matmul(pstat[1][0:1, :], t_onescol[:],
                               sq_tile[:, g * CH:(g + 1) * CH],
                               start=(g == 0), stop=(g == G - 1))
            return ins
        P.op("tensor", fsq, reads=[sq_buf, b_consts], writes=[b_pstat[1]])

    def rank1_mm(nm, off):
        def f(e, nm=nm, off=off):
            ins = None
            t = wslot[nm]
            for m in range(G):
                for g in range(G):
                    ins = e.matmul(pmm[3][:, off + m:off + m + 1],
                                   t[:, g * D + m * 128: g * D + (m + 1) * 128],
                                   t_incbf[:, g:g + 1],
                                   start=(g == 0), stop=(g == G - 1))
            return ins
        P.op("tensor", f, reads=[b_slot[slot_of[nm]], B_("incbf")],
             writes=[b_pmm[3]])

    def rank1_fin():
        for off, gate, bname in ((0, "f", "bf"), (G, "i", "bi"),
                                 (2 * G, "c", "bc")):
            P.op("vector",
                 lambda e, off=off, gate=gate, bname=bname: e.tensor_tensor(
                     t_b0[gate][:], pmm[3][:, off:off + G], t_b[bname][:],
                     ALU.add),
                 reads=[b_pmm[3], B_("b_" + bname)], writes=[B_("b0" + gate)])
        vguard([B_("b0f"), B_("b0i"), B_("b0c")])


    b_sqA = B_("sqA")
    b_hbf = [B_("hbf0"), B_("hbf1")]
    b_q = [B_(f"qch{g}") for g in range(G)]
    b_kv = [B_(f"kvch{g}") for g in range(G)]
    b_a = [B_(f"ach{g}") for g in range(G)]
    b_y, b_P = B_("ych"), B_("Pch")
    b_ubf, b_vbf = B_("ubf"), B_("vbf")
    b_cy, b_cp = B_("cy"), B_("cp")
    hbf_t = [t_hbf0, t_hbf1]
    uv_tok = {}

    def ln1_block(b):
        for c in range(NCH):
            P.op("scalar",
                 lambda e, c=c: e.activation(
                     g3(t_sqA), g3(t_xbf, c0=c * CH, width=POS), AF.Square),
                 reads=[b_xbf], writes=[b_sqA])
            stat_pair(t_xbf, b_xbf, c * CH, POS, t_sqA, b_sqA)
            rows_block(None, True, False)
            P.op("tensor", bcast(r_m, 0), reads=[b_rows, b_consts],
                 writes=[b_pbc[0]])
            P.op("tensor", bcast(r_r, 1), reads=[b_rows, b_consts],
                 writes=[b_pbc[1]])
            for h2 in range(2):
                c0 = c * CH + h2 * (CH // 2)
                P.op("vector",
                     lambda e, c0=c0: e.tensor_tensor(
                         g3(t_xm, ncc=CH // 2, width=CH // 2),
                         g3(t_xbf, c0=c0, ncc=CH // 2, width=POS),
                         bc3(pbc[0], c0=(c0 % CH), ncc=CH // 2),
                         ALU.subtract),
                     reads=[b_xbf, b_pbc[0]], writes=[b_sqA])
                for g in range(G):
                    P.op("vector",
                         lambda e, b=b, c0=c0, g=g: e.scalar_tensor_tensor(
                             hbf_t[b][:, g * POS + c0:
                                      g * POS + c0 + CH // 2],
                             t_xm[:, g * (CH // 2):(g + 1) * (CH // 2)],
                             t_gamma[:, g:g + 1],
                             pbc[1][:, (c0 % CH):(c0 % CH) + CH // 2],
                             ALU.mult, ALU.mult),
                         reads=[b_sqA, b_pbc[1], b_params],
                         writes=[b_hbf[b]])
        dbg_dump(f"hbf{b}", hbf_t[b][:], [b_hbf[b]])

    def qkva_chunk(b, c):
        order = []
        for g in range(G):
            order.append((8 + g, "kv"))
            order.append((16 + g, "a"))
        for g in range(G):
            order.append((g, "q"))
        for k, (m, kind) in enumerate(order):
            bank = k % 4

            def fmm2(e, b=b, c=c, m=m, bank=bank):
                ins = None
                for g in range(G):
                    ins = e.matmul(
                        pmm[bank][:, 0:CH],
                        t_wqk[:, g * 3 * D + m * 128:
                              g * 3 * D + (m + 1) * 128],
                        hbf_t[b][:, g * POS + c * CH:
                                 g * POS + (c + 1) * CH],
                        start=(g == 0), stop=(g == G - 1))
                return ins
            P.op("tensor", fmm2, reads=[b_wqk, b_hbf[b]] + b_wqk3,
                 writes=[b_pmm[bank]])
            if kind == "q":
                P.op("scalar",
                     lambda e, m=m, bank=bank: e.copy(
                         t_q[:, m * CH:(m + 1) * CH], pmm[bank][:, 0:CH]),
                     reads=[b_pmm[bank]], writes=[b_q[m]])
            elif kind == "kv":
                P.op("scalar",
                     lambda e, m=m, bank=bank: e.copy(
                         t_kv[:, (m - 8) * CH:(m - 7) * CH],
                         pmm[bank][:, 0:CH]),
                     reads=[b_pmm[bank]], writes=[b_kv[m - 8]])
            else:
                P.op("scalar",
                     lambda e, m=m, bank=bank: e.activation(
                         t_a[:, (m - 16) * CH:(m - 15) * CH],
                         pmm[bank][:, 0:CH], AF.Sigmoid),
                     reads=[b_pmm[bank]],
                     writes=[b_a[m - 16],
                             b_ubf if (m - 16) < 4 else b_vbf])
        if b == 0 and dbg:
            dbg_dump(f"q0{c}", t_q[:], list(b_q))
            dbg_dump(f"kv0{c}", t_kv[:], list(b_kv))
            dbg_dump(f"a0{c}", t_a[:], list(b_a))
        for g in range(G):
            sl = slice(g * CH, (g + 1) * CH)
            cyc = t_cy[:, b * G + g: b * G + g + 1]
            P.op("vector",
                 lambda e, sl=sl, cyc=cyc: e.tensor_tensor_scan(
                     t_y[:, sl], t_a[:, sl], t_kv[:, sl], cyc,
                     ALU.mult, ALU.add),
                 reads=[b_a[g], b_kv[g], b_cy], writes=[b_y])
        for g in range(G):
            sl = slice(g * CH, (g + 1) * CH)
            cpc = t_cp[:, b * G + g: b * G + g + 1]
            P.op("vector",
                 lambda e, sl=sl, cpc=cpc: e.tensor_tensor_scan(
                     t_P[:, sl], t_a[:, sl], t_ones512[:], cpc,
                     ALU.mult, ALU.mult),
                 reads=[b_a[g], b_consts, b_cp], writes=[b_P])
        for g in range(G):
            sl = slice(g * CH, (g + 1) * CH)
            cyc = t_cy[:, b * G + g: b * G + g + 1]
            P.op("vector",
                 lambda e, sl=sl, cyc=cyc: e.tensor_copy(
                     cyc, t_y[:, sl.stop - 1: sl.stop]),
                 reads=[b_y], writes=[b_cy])
        for g in range(G):
            sl = slice(g * CH, (g + 1) * CH)
            cpc = t_cp[:, b * G + g: b * G + g + 1]
            P.op("vector",
                 lambda e, sl=sl, cpc=cpc: e.tensor_copy(
                     cpc, t_P[:, sl.stop - 1: sl.stop]),
                 reads=[b_P], writes=[b_cp])
        peng = "gpsimd" if (b, c) == (1, 1) else "vector"
        for g in range(G):
            sl = slice(g * CH, (g + 1) * CH)
            P.op(peng,
                 lambda e, sl=sl: e.tensor_tensor(
                     t_ubf[:, sl], t_y[:, sl], t_q[:, sl], ALU.mult),
                 reads=[b_y, b_q[g]],
                 writes=[b_ubf] + ([b_a[g2] for g2 in range(4)]
                                   if g == 0 else []))
        for g in range(G):
            sl = slice(g * CH, (g + 1) * CH)
            P.op(peng,
                 lambda e, sl=sl: e.tensor_tensor(
                     t_vbf[:, sl], t_P[:, sl], t_q[:, sl], ALU.mult),
                 reads=[b_P, b_q[g]],
                 writes=[b_vbf] + ([b_a[g2] for g2 in range(4, G)]
                                   if g == 0 else []))
        if b == 0 and dbg:
            dbg_dump(f"y0{c}", t_y[:], [b_y])
            dbg_dump(f"P0{c}", t_P[:], [b_P])
            dbg_dump(f"cyA{c}", t_cy[:], [b_cy])
        uv_tok[(b, 0, c)] = dma("sync", "SY", uv_sp[b, 0, c], t_ubf[:],
                                reads=[b_ubf])
        uv_tok[(b, 1, c)] = dma("sync", "SY", uv_sp[b, 1, c], t_vbf[:],
                                reads=[b_vbf])

    b_summ = [B_("summ0"), B_("summ1")]
    b_gath = [B_("gath0"), B_("gath1")]
    b_cin = [B_("cin0"), B_("cin1")]
    tok_cc = [None, None]

    def x_trigger(b):
        P.op("vector",
             lambda e, b=b: e.tensor_copy(
                 t_summ[:, b * 16: b * 16 + 8], t_cy[:, b * G:(b + 1) * G]),
             reads=[b_cy], writes=[b_summ[b]])
        P.op("vector",
             lambda e, b=b: e.tensor_copy(
                 t_summ[:, b * 16 + 8: b * 16 + 16],
                 t_cp[:, b * G:(b + 1) * G]),
             reads=[b_cp], writes=[b_summ[b]])
        vguard([b_summ[b]])
        tok_sm = dma("gpsimd", "GQ", cc_in[b].ap(),
                     t_summ[:, b * 16:b * 16 + 16], reads=[b_summ[b]])
        tok = P.op("gpsimd",
                   lambda e, b=b: e.collective_compute(
                       "AllGather", ALU.bypass,
                       replica_groups=[list(range(NCORES))],
                       ins=[cc_in[b].ap()], outs=[cc_out[b].ap()]),
                   deps=[tok_sm], sem="CC", inc=1)
        tok_cc[b] = tok
        dma("gpsimd", "GQ",
            t_gath[:, b * 128:(b + 1) * 128].rearrange("p (r c) -> p r c", r=8),
            cc_out[b].ap().rearrange("r p c -> p r c"),
            writes=[b_gath[b]], deps=[tok])

    def st3(t, off):
        ap = t.ap()
        row = ap.ap[0][0]
        return bass.AP(t, off, [[row, 128], [9, 8], [1, 1]])

    def g8(b, off):
        ap = t_gath.ap()
        row = ap.ap[0][0]
        return bass.AP(t_gath, b * 128 + off, [[row, 128], [1, 8], [1, 1]])

    def x_combine(b):
        P.op("vector", lambda e: e.memset(t_cbA[:], 0.0), writes=[B_("cbA")])
        P.op("vector", lambda e: e.memset(t_cbY[:], 0.0), writes=[B_("cbY")])
        for r in range(NCORES):
            P.op("vector",
                 lambda e, r=r, b=b: e.tensor_scalar(
                     st3(t_cbA, 1 + r), g8(b, r * 16 + 8),
                     t_mlt[:, r:r + 1], t_mlc[:, r:r + 1], ALU.mult, ALU.add),
                 reads=[b_gath[b], B_("masks")], writes=[B_("cbA")])
        for r in range(NCORES):
            P.op("vector",
                 lambda e, r=r, b=b: e.tensor_scalar(
                     st3(t_cbY, 1 + r), g8(b, r * 16),
                     t_mlt[:, r:r + 1], None, ALU.mult),
                 reads=[b_gath[b], B_("masks")], writes=[B_("cbY")])
        P.op("vector", lambda e: e.memset(t_sg[2][:, 0:512], 0.0),
             writes=[B_("sg2")])
        P.op("vector",
             lambda e: e.tensor_tensor_scan(t_cbS[:], t_cbA[:], t_cbY[:],
                                            0.0, ALU.mult, ALU.add),
             reads=[B_("cbA"), B_("cbY")], writes=[B_("cbS")])
        P.op("vector", lambda e: e.memset(t_sg[2][:, 0:512], 0.0),
             writes=[B_("sg2")])

        def stfin():
            ap = t_cbS.ap()
            row = ap.ap[0][0]
            return bass.AP(t_cbS, 8, [[row, 128], [9, 8]])
        P.op("vector",
             lambda e, b=b: e.tensor_copy(t_cin[:, b * G:(b + 1) * G], stfin()),
             reads=[B_("cbS")], writes=[b_cin[b]])
        vguard([b_cin[b]])
        if b == 1:
            dbg_dump("cy", t_cy[:], [b_cy])
            dbg_dump("cp", t_cp[:], [b_cp])
            dbg_dump("gath", t_gath[:], [b_gath[0], b_gath[1]])
            dbg_dump("cin", t_cin[:], [b_cin[0], b_cin[1]])

    wslot = {"wf": t_s0, "wi": t_s1, "wc": t_s2, "uf": t_s3, "ui": t_s4,
             "uc": t_s5, "wo": t_s0, "uo": t_s3}
    slot_of = {"wf": "s0", "wi": "s1", "wc": "s2", "uf": "s3", "ui": "s4",
               "uc": "s5", "wo": "s0", "uo": "s3"}
    b_slot = {s: B_("slot_" + s) for s in ("s0", "s1", "s2", "s3", "s4", "s5")}
    slot_alias = {"s0": [b_wqk3[0]],
                  "s1": [b_wqk3[1], B_("sqB")],
                  "s2": [b_wqk3[2]],
                  "s3": [b_xbf], "s4": [b_hbf[0]], "s5": [b_hbf[1]]}

    def load_w(nm):
        t = wslot[nm]
        s = slot_of[nm]
        return dma("scalar", "SQ",
                   t.ap().rearrange("p (g d) -> p g d", g=G),
                   wT[nm].rearrange("(g p) d -> p g d", p=128),
                   writes=[b_slot[s]] + slot_alias[s])

    ln1_block(0)
    dma("scalar", "SQ",
        t_xbf.ap().rearrange("p (g c) -> p g c", g=G),
        xTb[1].rearrange("(g p) c -> p g c", p=128),
        writes=[b_xbf])
    qkva_chunk(0, 0)
    ln1_block(1)
    load_w("uf")
    qkva_chunk(0, 1)
    load_w("ui")
    x_trigger(0)
    dma("sync", "SY", t_uchE[:], uv_sp[0, 0, 0],
        writes=[B_("uchE"), b_sqA], deps=[uv_tok[(0, 0, 0)]])
    qkva_chunk(1, 0)
    x_combine(0)
    qkva_chunk(1, 1)
    rank1_mm("uf", 0)
    rank1_mm("ui", G)
    dma("gpsimd", "GQ", t_vchE2[:], uv_sp[0, 1, 0],
        writes=[B_("vchE2"), b_slot["s4"]], deps=[uv_tok[(0, 1, 0)]])
    x_trigger(1)
    for _nm in ("uc", "wf", "wi", "wc"):
        load_w(_nm)

    b_uch, b_vch = B_("uch"), B_("vch")
    b_abf = B_("abf")
    b_z = [[B_("z00"), B_("z01")], [B_("z10"), B_("z11")]]
    z_t = [t_z0, t_z1]
    b_c1a, b_c1b, b_c1bf = B_("c1a"), B_("c1b"), B_("c1bf")
    b_ui2, b_uc2 = B_("ui2"), B_("uc2")
    x2_tok = {}

    def c1_ap(m, c):
        t = t_c1a if m < 4 else t_c1b
        mm = m if m < 4 else m - 4
        return t[:, mm * POS + c * CH: mm * POS + (c + 1) * CH]

    def c1_buf(m):
        return b_c1a if m < 4 else b_c1b

    def uv_restream(b, c, early=False):
        if early:
            tu, tv = t_uchE, t_vchE
            bu, bv = B_("uchE"), B_("vchE")
            al = [b_slot["s1"]]
        else:
            tu, tv = t_uchb, t_vchb
            bu, bv = B_("uchb"), B_("vchb")
            al = [b_uch] + ([b_P] if (b, c) == (0, 1) else [])
        dma("sync", "SY", tu[:], uv_sp[b, 0, c],
            writes=[bu] + al, deps=[uv_tok[(b, 0, c)]])
        dma("sync", "SY", tv[:], uv_sp[b, 1, c],
            writes=[bv] + al, deps=[uv_tok[(b, 1, c)]])
        return tu, tv, bu, bv

    def B_chunk_p1(b, c, rs):
        tu, tv, bu, bv = rs
        for g in range(G):
            sl = slice(g * CH, (g + 1) * CH)
            P.op("vector",
                 lambda e, sl=sl, b=b, tu=tu, tv=tv, g=g:
                 e.scalar_tensor_tensor(
                     t_vch[:, sl], tv[:, sl],
                     t_cin[:, b * G + g: b * G + g + 1],
                     tu[:, sl], ALU.mult, ALU.add),
                 reads=[bu, bv, b_cin[b]],
                 writes=[b_vch] + (list(b_q) if g == 0 else []))
        ln_stats(t_vch, b_vch, t_abf, alias=[b_abf, b_sqA, B_("uchE")])
        rows_block(t_ceps[0:1, :], False, True)
        P.op("tensor", bcast(r_r, 0), reads=[b_rows, b_consts],
             writes=[b_pbc[0]])
        P.op("tensor", bcast(r_mr, 1), reads=[b_rows, b_consts],
             writes=[b_pbc[1]])
        P.op("vector",
             lambda e: e.tensor_tensor(g3(t_vch), g3(t_vch), bc3(pbc[0]),
                                       ALU.mult),
             reads=[b_pbc[0]], writes=[b_vch])
        P.op("vector",
             lambda e: e.tensor_tensor(g3(t_vch), g3(t_vch), bc3(pbc[1]),
                                       ALU.subtract),
             reads=[b_pbc[1]], writes=[b_vch])
        xal = [b_P] if (b, c) == (0, 0) else []
        dma("sync", "SY",
            t_uch.ap().rearrange("p (g c) -> p g c", g=G),
            xT[b].rearrange("(g p) c -> p g c", p=128)[:, :, c * CH:(c + 1) * CH],
            writes=[b_uch, B_("uchb"), B_("vchb")] + xal)
        P.op("vector",
             lambda e: e.tensor_tensor(t_vch[:], t_vch[:], t_uch[:], ALU.add),
             reads=[b_uch], writes=[b_vch])
        x2_tok[(b, c)] = dma("scalar", "SQ", x2_sp[b, c], t_vch[:],
                             reads=[b_vch])

    def B_chunk_p2(b, c):
        ln_stats(t_vch, b_vch, t_abf, alias=[b_abf])
        rows_block(t_ceps[0:1, :], False, False)
        P.op("tensor", bcast(r_m, 0), reads=[b_rows, b_consts],
             writes=[b_pbc[0]])
        P.op("tensor", bcast(r_r, 1), reads=[b_rows, b_consts],
             writes=[b_pbc[1]])
        P.op("vector",
             lambda e: e.tensor_tensor(g3(t_uch), g3(t_vch), bc3(pbc[0]),
                                       ALU.subtract),
             reads=[b_vch, b_pbc[0]], writes=[b_uch])
        extra_w = (list(b_a) + [b_ubf, b_vbf]) if b == 0 else list(b_kv)
        P.op("vector",
             lambda e, b=b, c=c: e.tensor_tensor(
                 g3(z_t[b], c0=c * CH, width=POS), g3(t_uch), bc3(pbc[1]),
                 ALU.mult),
             reads=[b_uch, b_pbc[1]], writes=[b_z[b][c]] + extra_w)

    def gate_mms(wt, m, acts_tile, c, bank, extra=None):
        def f(e, wt=wt, m=m, c=c, bank=bank, extra=extra):
            ins = None
            n = G if extra is None else 2 * G
            k = 0
            for g in range(G):
                ins = e.matmul(pmm[bank][:, 0:CH],
                               wt[:, g * D + m * 128: g * D + (m + 1) * 128],
                               acts_tile[:, g * POS + c * CH:
                                         g * POS + (c + 1) * CH],
                               start=(k == 0), stop=(k == n - 1))
                k += 1
            if extra is not None:
                wt2, acts2 = extra
                for g in range(G):
                    ins = e.matmul(pmm[bank][:, 0:CH],
                                   wt2[:, g * D + m * 128: g * D + (m + 1) * 128],
                                   acts2[:, g * POS + c * CH:
                                         g * POS + (c + 1) * CH],
                                   start=False, stop=(k == n - 1))
                    k += 1
            return ins
        return f

    def step0_f(c, ms):
        for m in ms:
            bank = (c * G + m) % 4
            P.op("tensor", gate_mms(t_s0, m, t_z0, c, bank),
                 reads=[b_slot["s0"], b_z[0][c], B_("b0f")],
                 writes=[b_pmm[bank]])
            P.op("scalar",
                 lambda e, m=m, bank=bank: e.activation(
                     t_sg[0][:], pmm[bank][:, 0:CH], AF.Sigmoid,
                     bias=t_b0["f"][:, m:m + 1]),
                 reads=[b_pmm[bank], B_("b0f")], writes=[B_("sg0")])
            al = [b_y] if m < 4 else [b_slot["s4"], B_("vchE2")]
            P.op("vector",
                 lambda e, m=m, c=c: e.tensor_scalar_mul(
                     c1_ap(m, c), t_sg[0][:], t_b["inc"][:, m:m + 1]),
                 reads=[B_("sg0"), B_("b_inc")], writes=[c1_buf(m)] + al)

    def step0_ig(c, ms):
        for m in ms:
            bank_i = (c * G + m) % 2
            bank_g = 2 + (c * G + m) % 2
            P.op("tensor", gate_mms(t_s1, m, t_z0, c, bank_i),
                 reads=[b_slot["s1"], b_z[0][c], B_("b0i")],
                 writes=[b_pmm[bank_i]])
            P.op("tensor", gate_mms(t_s2, m, t_z0, c, bank_g),
                 reads=[b_slot["s2"], b_z[0][c], B_("b0c")],
                 writes=[b_pmm[bank_g]])
            P.op("scalar",
                 lambda e, m=m, bank_i=bank_i: e.activation(
                     t_sg[1][:], pmm[bank_i][:, 0:CH], AF.Sigmoid,
                     bias=t_b0["i"][:, m:m + 1]),
                 reads=[b_pmm[bank_i], B_("b0i")], writes=[B_("sg1")])
            P.op("scalar",
                 lambda e, m=m, bank_g=bank_g: e.activation(
                     t_sg[2][:], pmm[bank_g][:, 0:CH], AF.Tanh,
                     bias=t_b0["c"][:, m:m + 1]),
                 reads=[b_pmm[bank_g], B_("b0c")], writes=[B_("sg2")])
            P.op("vector",
                 lambda e: e.tensor_tensor(t_sg[1][:], t_sg[1][:], t_sg[2][:],
                                           ALU.mult),
                 reads=[B_("sg2")], writes=[B_("sg1")])
            P.op("vector",
                 lambda e, m=m, c=c: e.tensor_tensor(
                     c1_ap(m, c), c1_ap(m, c), t_sg[1][:], ALU.add),
                 reads=[B_("sg1")], writes=[c1_buf(m)])

    b_c1bfc = [B_("c1bfc0"), B_("c1bfc1")]

    def c1bf_cast(c):
        al = [b_slot["s5"]] if c == 0 else []
        row_d = G * POS
        for half, (t_src, bsrc) in enumerate(((t_c1a, b_c1a),
                                              (t_c1b, b_c1b))):
            row_s = 4 * POS
            P.op("vector",
                 lambda e, half=half, t_src=t_src, c=c: e.tensor_copy(
                     bass.AP(t_c1bf, half * 4 * POS + c * CH,
                             [[row_d, 128], [POS, 4], [1, CH]]),
                     bass.AP(t_src, c * CH, [[row_s, 128], [POS, 4], [1, CH]])),
                 reads=[bsrc], writes=[b_c1bfc[c], b_c1bf] + al)

    def step1_f(c, ms):
        for m in ms:
            bank = (c * G + m) % 4
            P.op("tensor", gate_mms(t_s0, m, t_z1, c, bank,
                                    extra=(t_s3, t_c1bf)),
                 reads=[b_slot["s0"], b_slot["s3"], b_z[1][c], b_c1bfc[c]],
                 writes=[b_pmm[bank]])
            P.op("scalar",
                 lambda e, m=m, bank=bank: e.activation(
                     t_sg[0][:], pmm[bank][:, 0:CH], AF.Sigmoid,
                     bias=t_b["bf"][:, m:m + 1]),
                 reads=[b_pmm[bank], B_("b_bf")], writes=[B_("sg0")])
            P.op("vector",
                 lambda e, m=m, c=c: e.tensor_tensor(
                     c1_ap(m, c), c1_ap(m, c), t_sg[0][:], ALU.mult),
                 reads=[B_("sg0")], writes=[c1_buf(m)])

    def step1_ig(c, ms):
        for m in ms:
            bank_i = (c * G + m) % 2
            bank_g = 2 + (c * G + m) % 2
            P.op("tensor", gate_mms(t_s1, m, t_z1, c, bank_i,
                                    extra=(t_ui2, t_c1bf)),
                 reads=[b_slot["s1"], b_ui2, b_z[1][c], b_c1bfc[c]],
                 writes=[b_pmm[bank_i]])
            P.op("tensor", gate_mms(t_s2, m, t_z1, c, bank_g,
                                    extra=(t_uc2, t_c1bf)),
                 reads=[b_slot["s2"], b_uc2, b_z[1][c], b_c1bfc[c]],
                 writes=[b_pmm[bank_g]])
            P.op("scalar",
                 lambda e, m=m, bank_i=bank_i: e.activation(
                     t_sg[1][:], pmm[bank_i][:, 0:CH], AF.Sigmoid,
                     bias=t_b["bi"][:, m:m + 1]),
                 reads=[b_pmm[bank_i], B_("b_bi")], writes=[B_("sg1")])
            P.op("scalar",
                 lambda e, m=m, bank_g=bank_g: e.activation(
                     t_sg[2][:], pmm[bank_g][:, 0:CH], AF.Tanh,
                     bias=t_b["bc"][:, m:m + 1]),
                 reads=[b_pmm[bank_g], B_("b_bc")], writes=[B_("sg2")])
            P.op("vector",
                 lambda e: e.tensor_tensor(t_sg[1][:], t_sg[1][:], t_sg[2][:],
                                           ALU.mult),
                 reads=[B_("sg2")], writes=[B_("sg1")])
            P.op("vector",
                 lambda e, m=m, c=c: e.tensor_tensor(
                     c1_ap(m, c), c1_ap(m, c), t_sg[1][:], ALU.add),
                 reads=[B_("sg1")], writes=[c1_buf(m)])


    B_chunk_p1(0, 0, (t_uchE, t_vchE2, B_("uchE"), B_("vchE2")))
    rank1_mm("uc", 2 * G)
    rank1_fin()
    dbg_dump("b0f", t_b0["f"][:], [B_("b0f")])
    B_chunk_p2(0, 0)
    x_combine(1)
    rs01 = uv_restream(0, 1)
    step0_f(0, range(0, 5))
    B_chunk_p1(0, 1, rs01)
    step0_f(0, range(5, 8))
    step0_ig(0, range(0, 3))
    B_chunk_p2(0, 1)
    rs10 = uv_restream(1, 0)
    step0_ig(0, range(3, 5))
    B_chunk_p1(1, 0, rs10)
    step0_ig(0, range(5, 8))
    step0_f(1, range(0, 2))
    B_chunk_p2(1, 0)
    rs11 = uv_restream(1, 1)
    step0_f(1, range(2, 8))
    step0_ig(1, range(0, 1))
    B_chunk_p1(1, 1, rs11)
    step0_ig(1, range(1, 5))
    B_chunk_p2(1, 1)
    step0_ig(1, range(5, 8))
    c1bf_cast(0)
    step1_f(0, range(0, 4))
    dbg_dump("z0", t_z0[:], b_z[0])
    dbg_dump("z1", t_z1[:], b_z[1])

    dbg_dump("c1", t_c1bf[:], [b_c1bf])
    dma("scalar", "SQ",
        t_ui2.ap().rearrange("p (g d) -> p g d", g=G),
        wT["ui"].rearrange("(g p) d -> p g d", p=128),
        writes=[b_ui2, b_uch])
    dma("scalar", "SQ",
        t_uc2.ap().rearrange("p (g d) -> p g d", g=G),
        wT["uc"].rearrange("(g p) d -> p g d", p=128),
        writes=[b_uc2, b_vch] + list(b_q))

    step1_f(0, range(4, G))
    c1bf_cast(1)
    step1_f(1, range(G))
    load_w("wo")
    load_w("uo")
    step1_ig(0, range(G))
    step1_ig(1, range(G))

    P.op("vector", lambda e: e.tensor_copy(t_c1bf[:, 0:4 * POS], t_c1a[:]),
         reads=[b_c1a], writes=[b_c1bf])
    P.op("vector", lambda e: e.tensor_copy(t_c1bf[:, 4 * POS:], t_c1b[:]),
         reads=[b_c1b], writes=[b_c1bf])
    dbg_dump("c2", t_c1bf[:], [b_c1bf])

    b_hD = [B_("hD0"), B_("hD1")]
    b_x2r = [B_("x2r0"), B_("x2r1")]
    b_hbfD = B_("hbfD")
    b_sqD = B_("sqD")
    hD_t = [t_hD, t_hD2]
    x2r_t = [t_x2r, t_x2r2]
    hD_alias = [[b_c1a], [b_c1b]]
    x2r_alias = [[b_ui2], [b_uc2]]
    st_toks = []
    chunks = [(b, c) for b in range(B) for c in range(NCH)]

    def D_compute(idx, ms):
        b, c = chunks[idx]
        par = idx % 2
        thD, tx2 = hD_t[par], x2r_t[par]
        bhD, bx2 = b_hD[par], b_x2r[par]
        if ms[0] == 0:
            dma("sync", "SY", tx2[:], x2_sp[b, c],
                writes=[bx2] + x2r_alias[par], deps=[x2_tok[(b, c)]])
        for m in ms:
            bank = (c * G + m) % 4
            P.op("tensor", gate_mms(t_s0, m, z_t[b], c, bank,
                                    extra=(t_s3, t_c1bf)),
                 reads=[b_slot["s0"], b_slot["s3"], b_z[b][c], b_c1bf],
                 writes=[b_pmm[bank]])
            P.op("scalar",
                 lambda e, m=m, bank=bank, thD=thD: e.activation(
                     thD[:, m * CH:(m + 1) * CH], pmm[bank][:, 0:CH],
                     AF.Tanh, bias=t_b["bo"][:, m:m + 1]),
                 reads=[b_pmm[bank], B_("b_bo")],
                 writes=[bhD] + hD_alias[par])
        if b == 0 and c == 0:
            dbg_dump("hD00", thD[:], [bhD])

    def D_ln_stats(idx):
        b, c = chunks[idx]
        par = idx % 2
        thD = hD_t[par]
        bhD = b_hD[par]
        ln_stats(thD, bhD, t_hbfD, alias=[b_hbfD, b_abf],
                 eng="split" if idx == len(chunks) - 1 else "vector")
        rows_block(t_ceps[0:1, :], False, True)

    def D_ln_tail(idx):
        b, c = chunks[idx]
        par = idx % 2
        thD, tx2 = hD_t[par], x2r_t[par]
        bhD, bx2 = b_hD[par], b_x2r[par]
        P.op("tensor", bcast(r_r, 0), reads=[b_rows, b_consts],
             writes=[b_pbc[0]])
        P.op("tensor", bcast(r_mr, 1), reads=[b_rows, b_consts],
             writes=[b_pbc[1]])
        P.op("vector",
             lambda e, thD=thD: e.tensor_tensor(g3(thD), g3(thD), bc3(pbc[0]),
                                                ALU.mult),
             reads=[b_pbc[0]], writes=[bhD])
        P.op("vector",
             lambda e, thD=thD: e.tensor_tensor(g3(thD), g3(thD), bc3(pbc[1]),
                                                ALU.subtract),
             reads=[b_pbc[1]], writes=[bhD])
        P.op("vector",
             lambda e, thD=thD, tx2=tx2: e.tensor_tensor(
                 tx2[:], thD[:], tx2[:], ALU.add),
             reads=[bhD], writes=[bx2])
        st_toks.append(
            dma("scalar", "SQ",
                outT[b].rearrange("(g p) c -> p g c", p=128)[:, :, c * CH:(c + 1) * CH],
                tx2.ap().rearrange("p (g c) -> p g c", g=G),
                reads=[bx2]))

    D_compute(0, range(G))
    for idx in range(1, len(chunks)):
        D_compute(idx, range(0, 6))
        D_ln_stats(idx - 1)
        D_compute(idx, range(6, 8))
        D_ln_tail(idx - 1)
    D_ln_stats(len(chunks) - 1)
    D_ln_tail(len(chunks) - 1)

    P.seq["scalar"].append((
        [(s, v) for (_e, s, v) in st_toks], lambda e: None, None, 0))

    nc._rec = []
    nc._plan = P
    with nc.Block() as block:

        @block.tensor
        def _(eng):
            P.emit("tensor", eng, sems, nc, nc._rec)

        @block.vector
        def _(eng):
            P.emit("vector", eng, sems, nc, nc._rec)

        @block.scalar
        def _(eng):
            P.emit("scalar", eng, sems, nc, nc._rec)

        @block.gpsimd
        def _(eng):
            P.emit("gpsimd", eng, sems, nc, nc._rec)

        @block.sync
        def _(eng):
            P.emit("sync", eng, sems, nc, nc._rec)

    return nc


_NC_CACHE = {}


def kernel(L0_attn_x, L0_attn_w_qkva, L0_attn_gamma,
           L0_ffn_wf, L0_ffn_uf, L0_ffn_bf, L0_ffn_wi, L0_ffn_ui, L0_ffn_bi,
           L0_ffn_wc, L0_ffn_uc, L0_ffn_bc, L0_ffn_wo, L0_ffn_uo, L0_ffn_bo,
           L0_ffn_in_c):
    x = np.asarray(L0_attn_x, dtype=np.float32)
    SEQ = x.shape[1]
    PC = SEQ // NCORES

    def feat(v):
        return np.ascontiguousarray(np.asarray(v, np.float32).reshape(G, 128).T)

    def tbf(v):
        b = np.ascontiguousarray(v, np.float32)
        return b.view(np.uint16).reshape(*b.shape, 2)[..., 1].copy().view(
            ml_dtypes.bfloat16)

    shared = {
        "wqkT": tbf(np.asarray(L0_attn_w_qkva, np.float32).T),
        "gamma": feat(L0_attn_gamma),
        "wfT": tbf(np.asarray(L0_ffn_wf, np.float32).T),
        "wiT": tbf(np.asarray(L0_ffn_wi, np.float32).T),
        "wcT": tbf(np.asarray(L0_ffn_wc, np.float32).T),
        "woT": tbf(np.asarray(L0_ffn_wo, np.float32).T),
        "ufT": tbf(np.asarray(L0_ffn_uf, np.float32).T),
        "uiT": tbf(np.asarray(L0_ffn_ui, np.float32).T),
        "ucT": tbf(np.asarray(L0_ffn_uc, np.float32).T),
        "uoT": tbf(np.asarray(L0_ffn_uo, np.float32).T),
        "bf_b": feat(L0_ffn_bf), "bi_b": feat(L0_ffn_bi),
        "bc_b": feat(L0_ffn_bc), "bo_b": feat(L0_ffn_bo),
        "inc_b": feat(L0_ffn_in_c),
    }
    in_maps = []
    for k in range(NCORES):
        m = dict(shared)
        m["x0T"] = np.ascontiguousarray(x[0, k * PC:(k + 1) * PC, :].T)
        m["x1T"] = np.ascontiguousarray(x[1, k * PC:(k + 1) * PC, :].T)
        m["x0Tb"] = tbf(m["x0T"])
        m["x1Tb"] = tbf(m["x1T"])
        mlt = np.zeros((128, G), np.float32)
        mlt[:, :k] = 1.0
        m["mask_lt"] = mlt
        m["mask_c"] = np.ascontiguousarray(1.0 - mlt)
        in_maps.append(m)

    if "nc" not in _NC_CACHE:
        _NC_CACHE["nc"] = build_nc()
    res = run_bass_kernel_spmd(_NC_CACHE["nc"], in_maps,
                               core_ids=list(range(NCORES)))
    _NC_CACHE["res"] = res

    out = np.empty((B, SEQ, D), np.float32)
    for k in range(NCORES):
        ot = res.results[k]["outT"]
        out[0, k * PC:(k + 1) * PC, :] = ot[0].T
        out[1, k * PC:(k + 1) * PC, :] = ot[1].T
    return out


if __name__ == "__main__":
    build_nc()
    print("built ok")
